# version 24
# speedup vs baseline: 1.0413x; 1.0240x over previous
import contextlib
import math
import os
import numpy as np

import concourse.bass as bass
import concourse.mybir as mybir
from concourse.bass_utils import run_bass_kernel_spmd
from concourse.bass_core_rust import IndirectOffsetOnAxis

F32 = mybir.dt.float32
BF16 = mybir.dt.bfloat16
I32 = mybir.dt.int32
AF = mybir.ActivationFunctionType
ALU = mybir.AluOpType
AX = mybir.AxisListType

N_CORES = 8
D = 1024
B = 2
T = 8192
TPC = T * B // N_CORES
NCH = D // 128

ENGINES = ("tensor", "vector", "scalar", "gpsimd", "sync")


class TV:
    __slots__ = ("ap", "key")

    def __init__(self, ap, key):
        self.ap = ap
        self.key = key


class Tile:

    def __init__(self, handle, name):
        self.h = handle
        self.name = name

    def __getitem__(self, idx):
        return TV(self.h[idx], self.name)

    @property
    def ap(self):
        return TV(self.h[:], self.name)

    def tag(self, tagname, idx=None):
        ap = self.h[:] if idx is None else self.h[idx]
        return TV(ap, f"{self.name}.{tagname}")


class Em:

    def __init__(self):
        self.counts = {}
        self.clocks = {e: {} for e in ENGINES}
        self.last_write = {}
        self.readers = {}
        self.stream_keys = {}
        self.tape = []
        self.n_waits = 0

    def _bump(self, stream, inc):
        self.counts[stream] = self.counts.get(stream, 0) + inc
        return self.counts[stream]

    def _record_wait(self, engine, waits, stream, count, snapshot):
        if stream.startswith("q|"):
            count = self.counts.get(stream, 0)
        ck = self.clocks[engine]
        if ck.get(stream, 0) >= count:
            return
        waits.append((stream, count))
        self.n_waits += 1
        ck[stream] = count
        if snapshot:
            for s, c in snapshot.items():
                if ck.get(s, 0) < c:
                    ck[s] = c

    def op(self, engine, emit, reads=(), writes=(), dma_queue=None, n_inc=1,
           inc_override=None):
        rkeys = [r.key if isinstance(r, TV) else r for r in reads]
        wkeys = [w.key if isinstance(w, TV) else w for w in writes]
        stream = dma_queue if dma_queue is not None else engine
        inc = 16 * n_inc if dma_queue is not None else n_inc
        if inc_override is not None:
            inc = inc_override

        waits = []
        for k in rkeys:
            lw = self.last_write.get(k)
            if lw is not None:
                s, c, ck = lw
                self._record_wait(engine, waits, s, c, ck)
        for k in wkeys:
            lw = self.last_write.get(k)
            if lw is not None:
                s, c, ck = lw
                self._record_wait(engine, waits, s, c, ck)
            for (s, c, ck) in self.readers.get(k, ()):
                self._record_wait(engine, waits, s, c, ck)
        if dma_queue is not None:
            prior = self.stream_keys.setdefault(stream, set())
            for k2 in prior:
                if k2 in wkeys:
                    continue
                for (s, c, ck) in self.readers.get(k2, ()):
                    self._record_wait(engine, waits, s, c, ck)
            prior.update(wkeys)

        cnt = self._bump(stream, inc)
        if dma_queue is None:
            snapshot = dict(self.clocks[engine])
            snapshot[stream] = cnt
            if engine == "tensor":
                self.clocks[engine][stream] = cnt
        else:
            snapshot = dict(self.clocks[engine])
            snapshot[stream] = cnt
        for k in rkeys:
            self.readers.setdefault(k, []).append((stream, cnt, snapshot))
        for k in wkeys:
            self.last_write[k] = (stream, cnt, snapshot)
            self.readers[k] = []
        self.tape.append((engine, waits, emit, (stream, inc)))


    @staticmethod
    def _qname(key):
        import re as _re
        return "q|" + _re.sub(r"_u\d+", "", key)

    CST_MERGE = {
        "sb_ident_bf", "sb_ones_col", "sb_ones_row", "trils", "trilms",
        "triums", "triuIs", "idf_sb", "ci_sb", "idxp_sb", "hon_sb", "e2",
        "blk2", "epsc", "tclip", "onws", "gb1s", "gb2s", "gw2s", "rb2s",
        "rb1col", "sp_e8", "sp_esel", "sp_o512", "sp_tokid", "sp_b2row",
        "sp_eb1col", "sp_dumpc", "firls", "firss", "wbs", "lgt",
    }

    def dma(self, engine, out, in_, q=None, **kw):
        if q is None:
            q = self._qname(out.key if isinstance(out, TV) else str(out))
            if q[2:] in Em.CST_MERGE:
                q = "q|cstg" if engine == "gpsimd" else "q|cst"
        oap, iap = out.ap, in_.ap
        self.op(engine, lambda e: e.dma_start(out=oap, in_=iap, **kw),
                reads=[in_], writes=[out], dma_queue=q)

    def mm(self, out, lhsT, rhs, start=True, stop=True, **kw):
        oap, lap, rap = out.ap, lhsT.ap, rhs.ap
        self.op("tensor",
                lambda e: e.matmul(oap, lap, rap, start=start, stop=stop, **kw),
                reads=[lhsT, rhs], writes=[out])

    def transpose(self, out, in_, ident, **kw):
        oap, iap, dap = out.ap, in_.ap, ident.ap
        self.op("tensor", lambda e: e.transpose(oap, iap, dap, **kw),
                reads=[in_, ident], writes=[out])

    def act(self, out, in_, func, bias=0.0, scale=1.0, accum_out=None,
            engine="scalar"):
        reads = [in_]
        writes = [out]
        kw = {}
        if isinstance(bias, TV):
            reads.append(bias)
            kw["bias"] = bias.ap
        else:
            kw["bias"] = bias
        if isinstance(scale, TV):
            reads.append(scale)
            kw["scale"] = scale.ap
        else:
            kw["scale"] = scale
        if accum_out is not None:
            writes.append(accum_out)
            kw["accum_out"] = accum_out.ap
        oap, iap = out.ap, in_.ap
        self.op(engine, lambda e: e.activation(oap, iap, func, **kw),
                reads=reads, writes=writes)

    def tt(self, engine, out, a, b, alu):
        oap, aap, bap = out.ap, a.ap, b.ap
        self.op(engine, lambda e: e.tensor_tensor(oap, aap, bap, alu),
                reads=[a, b], writes=[out])

    def ts(self, engine, out, in0, s1, s2, op0, op1=None, accum_out=None):
        reads = [in0]
        writes = [out]
        a1 = s1
        if isinstance(s1, TV):
            reads.append(s1)
            a1 = s1.ap
        a2 = s2
        if isinstance(s2, TV):
            reads.append(s2)
            a2 = s2.ap
        kw = {}
        if accum_out is not None:
            writes.append(accum_out)
            kw["accum_out"] = accum_out.ap
        if op1 is None:
            op1 = ALU.bypass
        oap, iap = out.ap, in0.ap
        self.op(engine,
                lambda e: e.tensor_scalar(oap, iap, a1, a2, op0, op1, **kw),
                reads=reads, writes=writes)

    def stt(self, engine, out, in0, scalar, in1, op0, op1):
        reads = [in0, in1]
        sc = scalar
        if isinstance(scalar, TV):
            reads.append(scalar)
            sc = scalar.ap
        oap, i0, i1 = out.ap, in0.ap, in1.ap
        self.op(engine,
                lambda e: e.scalar_tensor_tensor(oap, i0, sc, i1, op0, op1),
                reads=reads, writes=[out])

    def reduce(self, engine, out, in_, alu, negate=False):
        oap, iap = out.ap, in_.ap
        self.op(engine,
                lambda e: e.tensor_reduce(oap, iap, AX.X, alu, negate=negate),
                reads=[in_], writes=[out])

    def copy(self, engine, out, in_):
        oap, iap = out.ap, in_.ap
        if engine == "scalar":
            self.op(engine, lambda e: e.copy(oap, iap),
                    reads=[in_], writes=[out])
        else:
            self.op(engine, lambda e: e.tensor_copy(oap, iap),
                    reads=[in_], writes=[out])

    def scopy(self, out, in_):
        self.act(out, in_, AF.Copy)

    def memset(self, engine, out, val):
        oap = out.ap
        self.op(engine, lambda e: e.memset(oap, val), reads=[], writes=[out])

    def recip(self, out, in_, engine="vector"):
        oap, iap = out.ap, in_.ap
        self.op(engine, lambda e: e.reciprocal(oap, iap),
                reads=[in_], writes=[out])

    def collective(self, kind, alu, rg, ins, outs):
        iaps = [i.ap for i in ins]
        oaps = [o.ap for o in outs]
        q = self._qname(outs[0].key)
        self.op("gpsimd",
                lambda e: e.collective_compute(kind, alu, rg, ins=iaps, outs=oaps),
                reads=list(ins), writes=list(outs), dma_queue=q,
                inc_override=1)

    def indirect(self, out, out_off, in_, in_off, q=None, extra_reads=None,
                 **kw):
        reads = [in_]
        writes = [out]
        ooap = out_off.ap if out_off is not None else None
        ioap = in_off.ap if in_off is not None else None
        if out_off is not None:
            reads.append(out_off)
        if in_off is not None:
            reads.append(in_off)
        if extra_reads:
            reads.extend(extra_reads)
        oap, iap = out.ap, in_.ap
        if q is None:
            q = self._qname(out.key)

        def emit(e):
            oo = IndirectOffsetOnAxis(ap=ooap, axis=0) if ooap is not None else None
            io = IndirectOffsetOnAxis(ap=ioap, axis=0) if ioap is not None else None
            try:
                return e.indirect_dma_start(oap, oo, iap, io, **kw)
            except Exception:
                print(f"indirect failed: out={oap} in={iap} "
                      f"oo={ooap} io={ioap} kw={kw}")
                raise
        self.op("gpsimd", emit, reads=reads, writes=writes, dma_queue=q)

    def fence(self, engine, streams=None):
        waits = []
        ck = self.clocks[engine]
        items = self.counts.items() if streams is None else \
            [(s, self.counts.get(s, 0)) for s in streams]
        for s, c in list(items):
            if s != engine and c and ck.get(s, 0) < c:
                waits.append((s, c))
                ck[s] = c
        if waits:
            self.tape.append((engine, waits, None, None))

    def barrier(self):
        for e in ENGINES:
            self.fence(e)


def bcast(tv, n):
    from concourse.ap import AP as _AP
    ap = tv.ap
    nap = _AP(ap.tensor, ap.offset,
              [list(d) for d in ap.ap] + [[0, n]])
    return TV(nap, tv.key)


class Prog:

    def __init__(self, name):
        self.name = name
        self.nc = bass.Bass("TRN2", target_bir_lowering=False)
        self.stack = contextlib.ExitStack()
        self.em = Em()
        self.inputs = {}
        self.outputs = {}
        self._cms = {}

    def din(self, name, shape, dtype=F32):
        t = self.nc.dram_tensor(name, list(shape), dtype, kind="ExternalInput")
        tv = Tile(t.ap(), "D." + name)
        self.inputs[name] = tv
        return tv

    def dout(self, name, shape, dtype=F32):
        t = self.nc.dram_tensor(name, list(shape), dtype, kind="ExternalOutput")
        tv = Tile(t.ap(), "D." + name)
        self.outputs[name] = tv
        return tv

    def dram(self, name, shape, dtype=F32, addr_space="Local"):
        self._serial = getattr(self, "_serial", 0) + 1
        name = f"{name}_u{self._serial}"
        t = self.nc.dram_tensor(name, list(shape), dtype, kind="Internal",
                                addr_space=addr_space)
        return Tile(t.ap(), "D." + name)

    def mark(self):
        self._order = getattr(self, "_order", [])
        return len(self._order)

    def free_since(self, mark):
        names = set(self._order[mark:])
        if not names:
            return
        self.em.barrier()
        for nm in reversed(self._order[mark:]):
            cm = self._cms.pop(nm)
            cm.__exit__(None, None, None)
        self._order = self._order[:mark]

    def sbuf(self, name, shape, dtype=F32):
        self._serial = getattr(self, "_serial", 0) + 1
        name = f"{name}_u{self._serial}"
        cm = self.nc.sbuf_tensor(name, list(shape), dtype)
        h = cm.__enter__()
        t = Tile(h, name)
        self._cms[name] = cm
        self._order = getattr(self, "_order", [])
        self._order.append(name)
        return t

    def free(self, *tiles):
        self.em.barrier()
        names = set(t.name for t in tiles)
        for nm in reversed(self._order):
            if nm in names:
                cm = self._cms.pop(nm)
                cm.__exit__(None, None, None)
                names.discard(nm)
        self._order = [n for n in self._order if n in self._cms]

    def psum(self, name, shape, dtype=F32):
        h = self.stack.enter_context(self.nc.psum_tensor(name, list(shape), dtype))
        return Tile(h, name)

    def breg(self, val):
        self._bregs = getattr(self, "_bregs", {})
        if val not in self._bregs:
            r = self.nc.gpsimd.alloc_register(f"bchk{len(self._bregs)}")
            self._bregs[val] = r
            self.em.op("gpsimd",
                       (lambda rr, vv: lambda e: e.reg_mov(rr, vv))(r, val),
                       reads=[], writes=[])
        return self._bregs[val]

    def finalize(self):
        em = self.em
        for name in list(self._cms):
            self.stack.push(self._cms.pop(name))
        with self.stack:
            sems = {}
            with contextlib.ExitStack() as semstack:
                for s in em.counts:
                    sems[s] = semstack.enter_context(
                        self.nc.semaphore(f"sem_{s}"))
                with self.nc.Block() as blk:
                    def runner(engine_name):
                        def run(eng):
                            for (e, waits, emit, inc) in em.tape:
                                if e != engine_name:
                                    continue
                                for (s, c) in waits:
                                    eng.wait_ge(sems[s], c)
                                if emit is not None:
                                    inst = emit(eng)
                                    st, iv = inc
                                    inst.then_inc(sems[st], iv)
                        return run
                    blk.tensor(runner("tensor"))
                    blk.vector(runner("vector"))
                    blk.scalar(runner("scalar"))
                    blk.gpsimd(runner("gpsimd"))
                    blk.sync(runner("sync"))
        return self.nc



def alloc_psum_pool(p):
    return {
        "P0": p.psum("ps_p0", [128, 512], F32),
        "P1": p.psum("ps_p1", [128, 512], F32),
        "P2": p.psum("ps_p2", [128, 512], F32),
        "P3": p.psum("ps_p3", [128, 512], F32),
        "PT": p.psum("ps_pt", [128, 1024], BF16),
        "PT2": p.psum("ps_pt2", [128, 1024], BF16),
        "R1": p.psum("ps_r1", [2, 512], F32),
        "R2": p.psum("ps_r2", [2, 512], F32),
    }


def alloc_common(p, nch=NCH):
    c = {}
    c["ones_col"] = p.din("cst_ones_col", [128, 1], F32)
    c["ones_row"] = p.din("cst_ones_row", [1, 128], F32)
    c["ident_bf"] = p.din("cst_ident_bf", [128, 128], BF16)
    sb = {}
    sb["ones_col"] = p.sbuf("sb_ones_col", [128, 1], F32)
    sb["ones_row"] = p.sbuf("sb_ones_row", [1, 128], F32)
    sb["ident_bf"] = p.sbuf("sb_ident_bf", [128, 128], BF16)
    em = p.em
    for k in ("ones_col", "ones_row", "ident_bf"):
        em.dma("sync", sb[k].ap, c[k].ap)
    sb["xf"] = p.sbuf("ln_xf", [128, 2, 512], F32)
    sb["xsq"] = p.sbuf("ln_xsq", [128, 512], F32)
    for r in ("r_mu", "r_e2", "r_var", "r_sd", "r_rs", "r_mb"):
        sb[r] = p.sbuf("ln_" + r, [1, 512], F32)
    sb["ap_t"] = p.sbuf("ln_apt", [128, 512], F32)
    return sb


def sb_ident(sc):
    return sc["ident_bf"].ap


def emit_ln_fm(p, em, ps, sc, x_dram, xn_sb, ntok, xcol0=0, ncol0=0,
               gb=None, nch=NCH, dmodel=None, resid_sb=None, eps=1e-5,
               shift_half=False):
    if dmodel is None:
        dmodel = nch * 128
    nsub = (ntok + 511) // 512
    if resid_sb is None:
        xr = x_dram.h.rearrange("(c p) t -> p c t", p=128)

    def src_chunk(c, t0, w, slot):
        sh = -1 if (shift_half and c >= nch // 2) else 0
        if resid_sb is not None:
            return TV(resid_sb.h[:, c, ncol0 + t0 + sh:ncol0 + t0 + sh + w],
                      resid_sb.name)
        dst = TV(sc["xf"].h[:, slot, 0:w], f"ln_xf.{slot}")
        em.dma("sync", dst,
               TV(xr[:, c, xcol0 + t0 + sh:xcol0 + t0 + sh + w], x_dram.name))
        return dst

    for s in range(nsub):
        w = min(512, ntok - s * 512)
        t0 = s * 512
        for c in range(nch):
            xf = src_chunk(c, t0, w, c % 2)
            em.act(TV(sc["xsq"].h[:, 0:w], sc["xsq"].name), xf, AF.Square)
            em.mm(TV(ps["R1"].h[0:1, 0:w], ps["R1"].name),
                  sc["ones_col"].ap, xf,
                  start=(c == 0), stop=(c == nch - 1))
            em.mm(TV(ps["R2"].h[0:1, 0:w], ps["R2"].name),
                  sc["ones_col"].ap,
                  TV(sc["xsq"].h[:, 0:w], sc["xsq"].name),
                  start=(c == 0), stop=(c == nch - 1))
        rw = lambda t: TV(t.h[:, 0:w], t.name)
        em.ts("vector", rw(sc["r_mu"]), TV(ps["R1"].h[0:1, 0:w], ps["R1"].name),
              1.0 / dmodel, None, ALU.mult)
        em.ts("vector", rw(sc["r_e2"]), TV(ps["R2"].h[0:1, 0:w], ps["R2"].name),
              1.0 / dmodel, None, ALU.mult)
        em.tt("vector", rw(sc["r_var"]), rw(sc["r_mu"]), rw(sc["r_mu"]), ALU.mult)
        em.tt("vector", rw(sc["r_var"]), rw(sc["r_e2"]), rw(sc["r_var"]), ALU.subtract)
        em.ts("vector", rw(sc["r_var"]), rw(sc["r_var"]), eps, None, ALU.add)
        em.act(rw(sc["r_sd"]), rw(sc["r_var"]), AF.Sqrt)
        em.recip(rw(sc["r_rs"]), rw(sc["r_sd"]))
        em.tt("vector", rw(sc["r_mb"]), rw(sc["r_mu"]), rw(sc["r_rs"]), ALU.mult)
        em.mm(TV(ps["P2"].h[:, 0:w], ps["P2"].name), sc["ones_row"].ap,
              rw(sc["r_rs"]))
        em.mm(TV(ps["P3"].h[:, 0:w], ps["P3"].name), sc["ones_row"].ap,
              rw(sc["r_mb"]))
        for c in range(nch):
            xf = src_chunk(c, t0, w, c % 2)
            em.tt("vector", TV(sc["ap_t"].h[:, 0:w], sc["ap_t"].name),
                  xf, TV(ps["P2"].h[:, 0:w], ps["P2"].name), ALU.mult)
            dst = TV(xn_sb.h[:, c, ncol0 + t0: ncol0 + t0 + w], xn_sb.name)
            if gb is None:
                em.tt("vector", dst, TV(sc["ap_t"].h[:, 0:w], sc["ap_t"].name),
                      TV(ps["P3"].h[:, 0:w], ps["P3"].name), ALU.subtract)
            else:
                g_sb, b_sb = gb
                em.tt("vector", TV(sc["ap_t"].h[:, 0:w], sc["ap_t"].name),
                      TV(sc["ap_t"].h[:, 0:w], sc["ap_t"].name),
                      TV(ps["P3"].h[:, 0:w], ps["P3"].name), ALU.subtract)
                em.ts("vector", dst, TV(sc["ap_t"].h[:, 0:w], sc["ap_t"].name),
                      TV(g_sb.h[:, c:c + 1], g_sb.name),
                      TV(b_sb.h[:, c:c + 1], b_sb.name), ALU.mult, ALU.add)


def emit_rope(p, em, cfg, sc, q_sb, ncols, colA0, tabA, tabB, scratch):
    nch = cfg["nch"]
    t1, t2, qsw = scratch["t1"], scratch["t2"], scratch["qsw"]
    npar = t1.h.shape[1] if len(t1.h.shape) == 3 else 1
    ver = cfg.get("rope_ver", 0)
    if ver is None:
        tA = TV(tabA.h[:, colA0:colA0 + ncols], tabA.name)
        tB = TV(tabB.h[:, colA0:colA0 + ncols], tabB.name)
    else:
        tA = TV(tabA.h[:, ver, colA0:colA0 + ncols], tabA.name)
        tB = TV(tabB.h[:, ver, colA0:colA0 + ncols], tabB.name)
    for c in range(nch):
        if npar == 1:
            t1v = TV(t1.h[:, 0:ncols], t1.name)
            t2v = TV(t2.h[:, 0:ncols], t2.name)
            qsv = lambda a, b_: TV(qsw.h[a:b_, 0:ncols], qsw.name)
            qsf = TV(qsw.h[:, 0:ncols], qsw.name)
        else:
            pr = c % npar
            t1v = TV(t1.h[:, pr, 0:ncols], f"{t1.name}.{pr}")
            t2v = TV(t2.h[:, pr, 0:ncols], f"{t2.name}.{pr}")
            qsv = lambda a, b_, pr=pr: TV(qsw.h[a:b_, pr, 0:ncols],
                                          f"{qsw.name}.{pr}")
            qsf = TV(qsw.h[:, pr, 0:ncols], f"{qsw.name}.{pr}")
        qc = TV(q_sb.h[:, c, 0:ncols], q_sb.name)
        for b in range(4):
            sb_ = b ^ 1
            em.dma("scalar", qsv(b * 32, (b + 1) * 32),
                   TV(q_sb.h[sb_ * 32:(sb_ + 1) * 32, c, 0:ncols],
                      q_sb.name))
        em.tt("vector", t1v, qc, tA, ALU.mult)
        em.tt("vector", t2v, qsf, tB, ALU.mult)
        em.tt("vector", qc, t1v, t2v, ALU.add)



def build_l0_attn(T_half=1024, HALO=512, WIN=512, nheads=16, nch=NCH,
                  debug=False, pp=None, io=None):
    dm = nch * 128
    KSPAN = WIN + 128
    NSLOT = KSPAN // 128
    QT = T_half // 128
    QGRP = min(4, QT)
    WGRP = QGRP * 128
    NTOK = HALO + T_half
    hpc = nheads // nch
    assert hpc == 2

    if pp is None:
        p = Prog("l0attn")
        em = p.em
        x = p.din("x_fm", [dm, HALO + 2 * T_half])
        wqkv = p.din("wqkv", [2 * nch, 128, nch * 128])
        vw_in = p.din("vw_in", [128, nch, dm])
        wout = p.din("wout", [nch, 128, nch * 128])
        ropeA = p.din("ropeA", [2, 128, HALO + 2 * T_half])
        ropeB = p.din("ropeB", [2, 128, HALO + 2 * T_half])
        maskT = p.din("maskT", [2, QT, 128, KSPAN], BF16)
        xo = p.dout("xo_fm", [dm, 2 * T_half])
        ps = alloc_psum_pool(p)
        sc = alloc_common(p, nch)
    else:
        p = pp
        em = p.em
        x, wqkv, vw_in, wout, ropeA, ropeB, maskT, xo = (io[k] for k in
            ("x", "wqkv", "vw_in", "wout", "ropeA", "ropeB", "maskT", "xo"))
        ps, sc = io["ps"], io["sc"]
    mark = p.mark()

    xn = p.sbuf("xn", [128, nch, NTOK], BF16)
    q_sb = p.sbuf("q_sb", [128, nch, T_half], BF16)
    k_sb = p.sbuf("k_sb", [128, nch, NTOK], BF16)
    v_tm = p.sbuf("v_tm", [128, NTOK // 128, nheads * 65], BF16)
    tabA = p.sbuf("tabA", [128, 2, NTOK], BF16)
    tabB = p.sbuf("tabB", [128, 2, NTOK], BF16)
    wbuf = p.sbuf("wbuf", [128, 4, nch, 128], BF16)
    vw = p.sbuf("vw", [128, nch, dm], BF16)
    msk = p.sbuf("msk", [128, QT, KSPAN], BF16)
    rsc = {"t1": p.sbuf("rp_t1", [128, NTOK], BF16),
           "t2": p.sbuf("rp_t2", [128, NTOK], BF16),
           "qsw": p.sbuf("rp_qsw", [128, NTOK], BF16)}
    p_sb = p.sbuf("p_sb", [128, KSPAN], BF16)
    pt_sb = p.sbuf("pt_sb", [128, KSPAN], BF16)
    o_tm = p.sbuf("o_tm", [128, 64], BF16)
    o_fm = p.sbuf("o_fm", [128, nch, 512], BF16)
    rec = p.sbuf("rec_s", [128, 1], F32)
    xres = p.sbuf("xres", [128, 512], F32)
    xout = p.sbuf("xout", [128, 512], F32)
    mx0 = p.sbuf("mx0", [128, 1], F32)
    mx1 = p.sbuf("mx1", [128, 1], F32)
    bias = p.sbuf("bias_s", [128, 1], F32)

    em.dma("gpsimd", vw.ap, vw_in.ap)


    cfg = {"nch": nch}
    for half in range(2):
        xc0 = half * T_half
        em.dma("gpsimd", tabA.ap,
               TV(ropeA.h.rearrange("v p t -> p v t")[:, :, xc0:xc0 + NTOK],
                  ropeA.name))
        em.dma("gpsimd", tabB.ap,
               TV(ropeB.h.rearrange("v p t -> p v t")[:, :, xc0:xc0 + NTOK],
                  ropeB.name))
        em.dma("gpsimd", msk.ap,
               TV(maskT.h[half].rearrange("q p k -> p q k"), maskT.name))
        emit_ln_fm(p, em, ps, sc, x, xn, NTOK, xcol0=xc0)
        for m in range(2 * nch):
            wb = TV(wbuf.h[:, m % 4].rearrange("p c m -> p (c m)"),
                    f"wbuf.{m % 4}")
            em.dma("gpsimd", wb, TV(wqkv.h[m], wqkv.name))
            is_q = m < nch
            ntok_m = T_half if is_q else NTOK
            tok0 = HALO if is_q else 0
            dst = q_sb if is_q else k_sb
            ch = m % nch
            for s0 in range(0, ntok_m, 512):
                w = min(512, ntok_m - s0)
                pst = ps["P0" if (s0 // 512) % 2 == 0 else "P1"]
                for c in range(nch):
                    em.mm(TV(pst.h[:, 0:w], pst.name),
                          TV(wbuf.h[:, m % 4, c, :], f"wbuf.{m % 4}"),
                          TV(xn.h[:, c, tok0 + s0:tok0 + s0 + w], xn.name),
                          start=(c == 0), stop=(c == nch - 1))
                em.copy("vector", TV(dst.h[:, ch, s0:s0 + w], dst.name),
                        TV(pst.h[:, 0:w], pst.name))
        for tt_ in range(NTOK // 128):
            em.memset("vector", TV(v_tm.h[:, tt_, :], v_tm.name), 1.0)
            for nh2 in range(2):
                pst = ps["P2"]
                for c in range(nch):
                    em.mm(TV(pst.h[:, :], pst.name),
                          TV(xn.h[:, c, tt_ * 128:(tt_ + 1) * 128], xn.name),
                          TV(vw.h[:, c, nh2 * 512:(nh2 + 1) * 512], vw.name),
                          start=(c == 0), stop=(c == nch - 1))
                em.copy("scalar",
                        TV(v_tm.h[:, tt_, :].rearrange(
                            "p (h e) -> p h e", h=nheads)[:, nh2 * 8:(nh2 + 1) * 8, 0:64],
                           v_tm.name),
                        TV(pst.h[:, :].rearrange("p (h e) -> p h e", e=64),
                           pst.name))
        if debug and half == 0:
            dbg_xn = p.dout("dbg_xn", [128, nch, NTOK], BF16)
            em.dma("gpsimd", dbg_xn.ap, xn.ap)
            dbg_q0 = p.dout("dbg_q0", [128, nch, T_half], BF16)
            em.dma("gpsimd", dbg_q0.ap, q_sb.ap)
            dbg_v = p.dout("dbg_v", [128, NTOK // 128, nheads * 65], BF16)
            em.dma("gpsimd", dbg_v.ap, v_tm.ap)
        emit_rope(p, em, dict(cfg, rope_ver=1), sc, q_sb, T_half, HALO,
                  tabA, tabB, rsc)
        emit_rope(p, em, dict(cfg, rope_ver=0), sc, k_sb, NTOK, 0,
                  tabA, tabB, rsc)
        if debug and half == 0:
            dbg_q = p.dout("dbg_q", [128, nch, T_half], BF16)
            em.dma("gpsimd", dbg_q.ap, q_sb.ap)
            dbg_k = p.dout("dbg_k", [128, nch, NTOK], BF16)
            em.dma("gpsimd", dbg_k.ap, k_sb.ap)
        for qt in range(QT):
            for h in range(nheads):
                ch, sub = h // hpc, h % hpc
                qv = TV(q_sb.h[sub * 64:(sub + 1) * 64, ch,
                               qt * 128:(qt + 1) * 128], q_sb.name)
                em.mm(TV(ps["P0"].h[:, :], ps["P0"].name), qv,
                      TV(k_sb.h[sub * 64:(sub + 1) * 64, ch,
                                qt * 128:qt * 128 + 512], k_sb.name))
                em.mm(TV(ps["P1"].h[:, 0:KSPAN - 512], ps["P1"].name), qv,
                      TV(k_sb.h[sub * 64:(sub + 1) * 64, ch,
                                qt * 128 + 512:qt * 128 + KSPAN], k_sb.name))
                em.reduce("vector", mx0.ap, ps["P0"].ap, ALU.max)
                em.reduce("vector", mx1.ap,
                          TV(ps["P1"].h[:, 0:KSPAN - 512], ps["P1"].name),
                          ALU.max)
                em.tt("vector", mx0.ap, mx0.ap, mx1.ap, ALU.max)
                em.ts("vector", bias.ap, mx0.ap, -1.0, None, ALU.mult)
                em.act(TV(p_sb.h[:, 0:512], p_sb.name), ps["P0"].ap, AF.Exp,
                       bias=bias.ap)
                em.act(TV(p_sb.h[:, 512:KSPAN], p_sb.name),
                       TV(ps["P1"].h[:, 0:KSPAN - 512], ps["P1"].name), AF.Exp,
                       bias=bias.ap)
                if debug and half == 0 and qt == 0 and h == 3:
                    dbg_p = p.dout("dbg_p", [128, KSPAN], BF16)
                    em.dma("gpsimd", dbg_p.ap, p_sb.ap)
                for s in range(NSLOT):
                    em.transpose(TV(ps["PT"].h[:, s * 128:(s + 1) * 128],
                                    ps["PT"].name),
                                 TV(p_sb.h[:, s * 128:(s + 1) * 128], p_sb.name),
                                 sb_ident(sc))
                em.tt("vector", pt_sb.ap,
                      TV(ps["PT"].h[:, 0:KSPAN], ps["PT"].name),
                      TV(msk.h[:, qt, :], msk.name), ALU.mult)
                for s in range(NSLOT):
                    em.mm(TV(ps["P2"].h[:, 0:65], ps["P2"].name),
                          TV(pt_sb.h[:, s * 128:(s + 1) * 128], pt_sb.name),
                          TV(v_tm.h[:, qt + s, h * 65:(h + 1) * 65], v_tm.name),
                          start=(s == 0), stop=(s == NSLOT - 1))
                em.recip(rec.ap, TV(ps["P2"].h[:, 64:65], ps["P2"].name))
                em.ts("vector", o_tm.ap, TV(ps["P2"].h[:, 0:64], ps["P2"].name),
                      rec.ap, None, ALU.mult)
                if debug and half == 0 and qt == 0 and h == 3:
                    dbg_pt = p.dout("dbg_pt", [128, KSPAN], BF16)
                    em.dma("gpsimd", dbg_pt.ap, pt_sb.ap)
                    dbg_ot = p.dout("dbg_ot", [128, 64], BF16)
                    em.dma("gpsimd", dbg_ot.ap, o_tm.ap)
                    dbg_sum = p.dout("dbg_sum", [128, 1], F32)
                    em.copy("vector", TV(sc["r_sd"].h[0:... if False else slice(None), :].__class__ and sc["r_sd"].h[:, 0:1], "dbgtmp"),
                            TV(ps["P2"].h[:, 64:65], ps["P2"].name)) if False else None
                    em.dma("gpsimd", dbg_sum.ap, rec.ap)
                em.transpose(TV(ps["PT"].h[0:64, 768:896], ps["PT"].name),
                             o_tm.ap, sb_ident(sc))
                em.copy("scalar",
                        TV(o_fm.h[sub * 64:(sub + 1) * 64, ch,
                                  (qt % QGRP) * 128:((qt % QGRP) + 1) * 128],
                           o_fm.name),
                        TV(ps["PT"].h[0:64, 768:896], ps["PT"].name))
            if debug and half == 0 and qt == QGRP - 1:
                dbg_of = p.dout("dbg_of", [128, nch, WGRP], BF16)
                em.dma("gpsimd", dbg_of.ap,
                       TV(o_fm.h[:, :, 0:WGRP], o_fm.name))
            if qt % QGRP == QGRP - 1:
                g0 = (qt - QGRP + 1) * 128
                xr = x.h.rearrange("(c p) t -> p c t", p=128)
                xov = xo.h.rearrange("(c p) t -> p c t", p=128)
                for m in range(nch):
                    wb = TV(wbuf.h[:, m % 2].rearrange("p c m -> p (c m)"),
                            f"wbuf.{m % 2}")
                    em.dma("gpsimd", wb, TV(wout.h[m], wout.name))
                    em.dma("sync", TV(xres.h[:, 0:WGRP], xres.name),
                           TV(xr[:, m, HALO + xc0 + g0:HALO + xc0 + g0 + WGRP],
                              x.name))
                    for c in range(nch):
                        em.mm(TV(ps["P3"].h[:, 0:WGRP], ps["P3"].name),
                              TV(wbuf.h[:, m % 2, c, :], f"wbuf.{m % 2}"),
                              TV(o_fm.h[:, c, 0:WGRP], o_fm.name),
                              start=(c == 0), stop=(c == nch - 1))
                    em.tt("vector", TV(xout.h[:, 0:WGRP], xout.name),
                          TV(ps["P3"].h[:, 0:WGRP], ps["P3"].name),
                          TV(xres.h[:, 0:WGRP], xres.name), ALU.add)
                    em.dma("gpsimd",
                           TV(xov[:, m, xc0 + g0:xc0 + g0 + WGRP], xo.name),
                           TV(xout.h[:, 0:WGRP], xout.name))
    em.fence("sync")
    if pp is not None:
        p.free_since(mark)
    return p



_PERM64 = np.concatenate([np.arange(0, 64, 2), np.arange(1, 64, 2)])


def tile_w(W):
    K, M = W.shape
    t = W.reshape(K // 128, 128, M // 128, 128).transpose(2, 1, 0, 3)
    return np.ascontiguousarray(t.reshape(M // 128, 128, (K // 128) * 128))


def host_common_inputs():
    return {
        "cst_ones_col": np.ones((128, 1), np.float32),
        "cst_ones_row": np.ones((1, 128), np.float32),
        "cst_ident_bf": np.eye(128, dtype=np.float32).astype(_bf16()),
    }


def _bf16():
    import ml_dtypes
    return ml_dtypes.bfloat16


def host_l0_rope_tables(pos, rope_base=10000.0):
    invf = (1.0 / rope_base ** (np.arange(0, 64, 2) / 64.0)).astype(np.float64)
    ang = pos[None, :].astype(np.float64) * invf[:, None]
    cos = np.cos(ang).astype(np.float32)
    sin = np.sin(ang).astype(np.float32)
    A64 = np.concatenate([cos, cos], 0)
    B64 = np.concatenate([-sin, sin], 0)
    return np.concatenate([A64, A64], 0), np.concatenate([B64, B64], 0)


def host_window_maskT(qbase, QT, KSPAN, win_len, lookback=None, seq_start=0):
    if lookback is None:
        lookback = KSPAN - 128
    out = np.zeros((QT, 128, KSPAN), np.float32)
    r = np.arange(128)
    for qt in range(QT):
        qg = qbase + qt * 128 + r
        for s in range(KSPAN // 128):
            c = np.arange(128)
            kg = qbase + qt * 128 - lookback + s * 128 + c
            valid = (qg[None, :] >= kg[:, None]) & \
                    (qg[None, :] - kg[:, None] < win_len) & \
                    (kg[:, None] >= seq_start) & (qg[None, :] >= seq_start)
            out[qt, :, s * 128:(s + 1) * 128] = valid.astype(np.float32)
    return out


def host_l0_attn_core_inputs(x_batch, w_qkv, w_out, t0, T_half=1024, HALO=512,
                             WIN=512):
    dm = x_batch.shape[1]
    Town = 2 * T_half
    lo = t0 - HALO
    if lo < 0:
        pad = np.zeros((-lo, dm), np.float32)
        xc = np.concatenate([pad, x_batch[0:t0 + Town]], 0)
    else:
        xc = x_batch[lo:t0 + Town]
    perm = np.concatenate([h * 64 + _PERM64 for h in range(dm // 64)])
    wq = w_qkv[:, 0:dm][:, perm]
    wk = w_qkv[:, dm:2 * dm][:, perm]
    wv = w_qkv[:, 2 * dm:3 * dm]
    wqkv_p = tile_w(np.concatenate([wq, wk], 1))
    vw_in = np.ascontiguousarray(
        wv.reshape(dm // 128, 128, dm).transpose(1, 0, 2))
    pos = np.maximum(np.arange(lo, t0 + Town), 0)
    A1, B1 = host_l0_rope_tables(pos)
    A = np.stack([A1, A1 * 0.125], 0)
    Bt = np.stack([B1, B1 * 0.125], 0)
    QT = T_half // 128
    KSPAN = WIN + 128
    m0 = host_window_maskT(t0, QT, KSPAN, WIN)
    m1 = host_window_maskT(t0 + T_half, QT, KSPAN, WIN)
    maskT = np.stack([m0, m1], 0).astype(_bf16())
    ins = {
        "x_fm": np.ascontiguousarray(xc.T),
        "wqkv": wqkv_p, "vw_in": vw_in,
        "wout": tile_w(w_out),
        "ropeA": A, "ropeB": Bt,
        "maskT": maskT,
    }
    ins.update(host_common_inputs())
    return ins



_PROG_CACHE = {}


def _get_prog(name, builder):
    if name not in _PROG_CACHE:
        p = builder()
        nc = p.finalize()
        _PROG_CACHE[name] = (p, nc)
    return _PROG_CACHE[name]


def run_phase_l0_attn(x_np, wqkv, wout):
    p, nc = _get_prog("l0attn", lambda: build_l0_attn())
    in_maps = []
    for core in range(N_CORES):
        b = core // 4
        t0 = (core % 4) * TPC
        in_maps.append(host_l0_attn_core_inputs(
            x_np[b], wqkv, wout, t0))
    res = run_bass_kernel_spmd(nc, in_maps, core_ids=list(range(N_CORES)))
    out = np.empty_like(x_np)
    for core in range(N_CORES):
        b = core // 4
        t0 = (core % 4) * TPC
        out[b, t0:t0 + TPC] = res.results[core]["xo_fm"].T
    return out



def build_l1_layer(nch=NCH, nheads=16, WIN=256, HALO_I=512, TOK=TPC,
                   DIN2=2816, debug=False, pp=None, io=None):
    dm = nch * 128
    KSPAN = WIN + 128
    NSLOT = KSPAN // 128
    QT_ALL = TOK // 128 + 1

    if pp is None:
        p = Prog("l1layer")
        em = p.em
        h_in = p.din("h_in", [dm, HALO_I + TOK])
        wqkv = p.din("wqkv", [2 * nch, 128, nch * 128])
        vw_in = p.din("vw_in", [128, nch, dm])
        wo = p.din("wo", [nch, 128, nch * 128])
        qscol = p.din("qscol", [128, 1])
        qspcol = p.din("qspcol", [128, 1])
        kscol = p.din("kscol", [128, 1])
        kspcol = p.din("kspcol", [128, 1])
        ln_g = p.din("ln_g", [128, nch])
        ln_b = p.din("ln_b", [128, nch])
        w1 = p.din("w1", [2 * DIN2 // 128, 128, nch * 128])
        b1c = p.din("b1c", [128, 2 * DIN2 // 128])
        w2 = p.din("w2", [nch, 128, (DIN2 // 128) * 128])
        b2c = p.din("b2c", [128, nch])
        ropeAq = p.din("ropeAq", [128, WIN + 128 + TOK])
        ropeBq = p.din("ropeBq", [128, WIN + 128 + TOK])
        ropeAk = p.din("ropeAk", [128, WIN + 128 + TOK])
        ropeBk = p.din("ropeBk", [128, WIN + 128 + TOK])
        maskT = p.din("maskT", [QT_ALL, 128, KSPAN], BF16)
        h_out = p.dout("h_out", [dm, TOK])
        ps = alloc_psum_pool(p)
        sc = alloc_common(p, nch)
        cst_blk2 = p.din("cst_blk2", [128, 2])
        cst_e2 = p.din("cst_e2", [2, 128])
    else:
        p = pp
        em = p.em
        (h_in, wqkv, vw_in, wo, qscol, qspcol, kscol, kspcol, ln_g, ln_b,
         w1, b1c, w2, b2c, ropeAq, ropeBq, ropeAk, ropeBk, maskT,
         h_out) = (io[k] for k in
            ("h_in", "wqkv", "vw_in", "wo", "qscol", "qspcol", "kscol",
             "kspcol", "ln_g", "ln_b", "w1", "b1c", "w2", "b2c", "ropeAq",
             "ropeBq", "ropeAk", "ropeBk", "maskT", "h_out"))
        ps, sc = io["ps"], io["sc"]
        cst_blk2, cst_e2 = io["cst_blk2"], io["cst_e2"]
    mark = p.mark()
    blk2 = p.sbuf("blk2", [128, 2])
    e2 = p.sbuf("e2", [2, 128])
    em.dma("sync", blk2.ap, cst_blk2.ap)
    em.dma("sync", e2.ap, cst_e2.ap)
    gcol = p.sbuf("gcol", [128, nch])
    bcol = p.sbuf("bcol", [128, nch])
    b1col = p.sbuf("b1col", [128, 2 * DIN2 // 128])
    b2col = p.sbuf("b2col", [128, nch])
    qsc = p.sbuf("qsc", [128, 4])
    em.dma("sync", gcol.ap, ln_g.ap)
    em.dma("sync", bcol.ap, ln_b.ap)
    em.dma("sync", b1col.ap, b1c.ap)
    em.dma("sync", b2col.ap, b2c.ap)
    em.dma("sync", TV(qsc.h[:, 0:1], qsc.name), qscol.ap)
    em.dma("sync", TV(qsc.h[:, 1:2], qsc.name), qspcol.ap)
    em.dma("sync", TV(qsc.h[:, 2:3], qsc.name), kscol.ap)
    em.dma("sync", TV(qsc.h[:, 3:4], qsc.name), kspcol.ap)

    hm = p.dram("hm", [dm, HALO_I + TOK])

    halves = [(0, (QT_ALL + 1) // 2), ((QT_ALL + 1) // 2, QT_ALL)]
    KT_MAX = max(q1 - q0 for q0, q1 in halves) * 128 + WIN
    QT_MAX = max(q1 - q0 for q0, q1 in halves) * 128

    wbuf = p.sbuf("wbuf", [128, 4, nch, 128], BF16)
    p_sb = p.sbuf("p_sb", [128, 2, KSPAN], BF16)
    pt_sb = p.sbuf("pt_sb", [128, 2, KSPAN], BF16)
    o_tm = p.sbuf("o_tm", [128, 2, 64], BF16)
    o_fm = p.sbuf("o_fm", [128, nch, 512], BF16)
    rec = p.sbuf("rec_s", [128, 2], F32)
    xres = p.sbuf("xres", [128, 512], F32)
    xout = p.sbuf("xout", [128, 512], F32)
    mx0 = p.sbuf("mx0", [128, 2], F32)
    bias = p.sbuf("bias_s", [128, 2], F32)
    o16 = p.sbuf("o16", [128, 16, 64], BF16)
    s16 = p.sbuf("s16", [128, 16], F32)
    r2a = p.sbuf("r2a", [2, 512], F32)
    r2b = p.sbuf("r2b", [2, 512], F32)
    xsq2 = p.sbuf("xsq2", [128, 512], F32)
    xn = p.sbuf("xn", [128, nch, KT_MAX], BF16)
    q_sb = p.sbuf("q_sb", [128, nch, QT_MAX], BF16)
    k_sb = p.sbuf("k_sb", [128, nch, KT_MAX], BF16)
    v_tm = p.sbuf("v_tm", [128, KT_MAX // 128, nheads * 65], BF16)
    tstage = p.sbuf("tstage", [128, KT_MAX], F32)
    tabs = {}
    for nm in ("Aq", "Bq", "Ak", "Bk"):
        tabs[nm] = p.sbuf("tab" + nm, [128, KT_MAX], BF16)
    msk = p.sbuf("msk", [128, QT_ALL, KSPAN], BF16)
    vw = p.sbuf("vw", [128, nch, dm], BF16)
    rsc = {"t1": p.sbuf("rp_t1", [128, KT_MAX], BF16),
           "t2": p.sbuf("rp_t2", [128, KT_MAX], BF16),
           "qsw": p.sbuf("rp_qsw", [128, KT_MAX], BF16)}

    em.dma("gpsimd", msk.ap,
           TV(maskT.h.rearrange("q p k -> p q k"), maskT.name))
    em.dma("gpsimd", vw.ap, vw_in.ap)
    hr = h_in.h.rearrange("(c p) t -> p c t", p=128)
    hmr = hm.h.rearrange("(c p) t -> p c t", p=128)
    hor = h_out.h.rearrange("(c p) t -> p c t", p=128)
    cfg = {"nch": nch}

    def l2norm_inplace(dst, ncols, which):
        it = 0
        for c in range(nch):
            for s0 in range(0, ncols, 512):
                pr = it % 2
                it += 1
                PS_R = ps["R1" if pr == 0 else "R2"]
                PS_B = ps["P2" if pr == 0 else "P3"]
                w = min(512, ncols - s0)
                xq = TV(sc["xsq"].h[:, 0:w] if pr == 0 else xsq2.h[:, 0:w],
                        "xsq0" if pr == 0 else "xsq1")
                rr_ = r2a if pr == 0 else r2b
                ra = TV(rr_.h[:, 0:w], rr_.name)
                dv = TV(dst.h[:, c, s0:s0 + w], dst.name)
                em.act(xq, dv, AF.Square)
                em.mm(TV(PS_R.h[0:2, 0:w], PS_R.name), blk2.ap, xq)
                em.act(ra, TV(PS_R.h[0:2, 0:w], PS_R.name), AF.Sqrt)
                em.ts("vector", ra, ra, 1e-12, None, ALU.max)
                em.recip(ra, ra)
                em.mm(TV(PS_B.h[:, 0:w], PS_B.name), e2.ap, ra)
                em.tt("vector", dv, dv, TV(PS_B.h[:, 0:w], PS_B.name),
                      ALU.mult)

    for hf, (hq0, hq1) in enumerate(halves):
        QTh = hq1 - hq0
        NQ = QTh * 128
        NK = NQ + WIN
        qtok0 = -128 + hq0 * 128
        ktok0 = qtok0 - WIN
        kcol0 = HALO_I + ktok0

        for c in range(nch):
            off = kcol0 if c < nch // 2 else kcol0 - 1
            em.dma("gpsimd", TV(xn.h[:, c, 0:NK], xn.name),
                   TV(hr[:, c, off:off + NK], h_in.name))
        tcol0 = ktok0 + WIN + 128
        for nm, src, scol_i in (("Aq", ropeAq, 0), ("Bq", ropeBq, 1),
                                ("Ak", ropeAk, 2), ("Bk", ropeBk, 3)):
            em.dma("sync", TV(tstage.h[:, 0:NK], tstage.name),
                   TV(src.h[:, tcol0:tcol0 + NK], src.name))
            em.ts("vector", TV(tabs[nm].h[:, 0:NK], tabs[nm].name),
                  TV(tstage.h[:, 0:NK], tstage.name),
                  TV(qsc.h[:, scol_i:scol_i + 1], qsc.name), None, ALU.mult)
        for m in range(2 * nch):
            wb = TV(wbuf.h[:, m % 4].rearrange("p c m -> p (c m)"),
                    f"wbuf.{m % 4}")
            em.dma("gpsimd", wb, TV(wqkv.h[m], wqkv.name))
            is_q = m < nch
            ntok_m = NQ if is_q else NK
            tok0 = WIN if is_q else 0
            dst = q_sb if is_q else k_sb
            ch = m % nch
            for s0 in range(0, ntok_m, 512):
                w = min(512, ntok_m - s0)
                pst = ps["P0" if (s0 // 512) % 2 == 0 else "P1"]
                for c in range(nch):
                    em.mm(TV(pst.h[:, 0:w], pst.name),
                          TV(wbuf.h[:, m % 4, c, :], f"wbuf.{m % 4}"),
                          TV(xn.h[:, c, tok0 + s0:tok0 + s0 + w], xn.name),
                          start=(c == 0), stop=(c == nch - 1))
                em.copy("vector", TV(dst.h[:, ch, s0:s0 + w], dst.name),
                        TV(pst.h[:, 0:w], pst.name))
        l2norm_inplace(q_sb, NQ, "q")
        l2norm_inplace(k_sb, NK, "k")
        emit_rope(p, em, dict(cfg, rope_ver=None), sc, q_sb, NQ, WIN,
                  tabs["Aq"], tabs["Bq"], rsc)
        emit_rope(p, em, dict(cfg, rope_ver=None), sc, k_sb, NK, 0,
                  tabs["Ak"], tabs["Bk"], rsc)
        for tt_ in range(NK // 128):
            em.memset("vector", TV(v_tm.h[:, tt_, :], v_tm.name), 1.0)
            for nh2 in range(2):
                pst = ps["P2"]
                for c in range(nch):
                    em.mm(TV(pst.h[:, :], pst.name),
                          TV(xn.h[:, c, tt_ * 128:(tt_ + 1) * 128], xn.name),
                          TV(vw.h[:, c, nh2 * 512:(nh2 + 1) * 512], vw.name),
                          start=(c == 0), stop=(c == nch - 1))
                em.copy("scalar",
                        TV(v_tm.h[:, tt_, :].rearrange(
                            "p (h e) -> p h e", h=nheads)
                           [:, nh2 * 8:(nh2 + 1) * 8, 0:64], v_tm.name),
                        TV(pst.h[:, :].rearrange("p (h e) -> p h e", e=64),
                           pst.name))
        hpc = nheads // nch
        for qtl in range(QTh):
            qt = hq0 + qtl

            def mk_S(h, pr):
                PS_S = ps["P0" if pr == 0 else "P1"]
                ch, sub = h // hpc, h % hpc
                qv = TV(q_sb.h[sub * 64:(sub + 1) * 64, ch,
                               qtl * 128:(qtl + 1) * 128], q_sb.name)
                em.mm(TV(PS_S.h[:, 0:KSPAN], PS_S.name), qv,
                      TV(k_sb.h[sub * 64:(sub + 1) * 64, ch,
                                qtl * 128:qtl * 128 + KSPAN], k_sb.name))

            def mk_rest(h, pr, qtl=qtl, qt=qt):
                def go():
                    PS_S = ps["P0" if pr == 0 else "P1"]
                    PS_V = ps["P2" if pr == 0 else "P3"]
                    PS_T = ps["PT" if pr == 0 else "PT2"]
                    psb = TV(p_sb.h[:, pr, :], f"p_sb.{pr}")
                    ptb = TV(pt_sb.h[:, pr, :], f"pt_sb.{pr}")
                    mxb = TV(mx0.h[:, pr:pr + 1], f"mx0.{pr}")
                    bib = TV(bias.h[:, pr:pr + 1], f"bias.{pr}")
                    em.reduce("vector", mxb,
                              TV(PS_S.h[:, 0:KSPAN], PS_S.name), ALU.max)
                    em.ts("vector", bib, mxb, -1.0, None, ALU.mult)
                    em.act(psb, TV(PS_S.h[:, 0:KSPAN], PS_S.name), AF.Exp,
                           bias=bib)
                    for s in range(NSLOT):
                        em.transpose(TV(PS_T.h[:, s * 128:(s + 1) * 128],
                                        PS_T.name),
                                     TV(p_sb.h[:, pr, s * 128:(s + 1) * 128],
                                        f"p_sb.{pr}"), sb_ident(sc))
                    em.tt("vector", ptb,
                          TV(PS_T.h[:, 0:KSPAN], PS_T.name),
                          TV(msk.h[:, qt, :], msk.name), ALU.mult)
                    for s in range(NSLOT):
                        em.mm(TV(PS_V.h[:, 0:65], PS_V.name),
                              TV(pt_sb.h[:, pr, s * 128:(s + 1) * 128],
                                 f"pt_sb.{pr}"),
                              TV(v_tm.h[:, qtl + s, h * 65:(h + 1) * 65],
                                 v_tm.name),
                              start=(s == 0), stop=(s == NSLOT - 1))
                    em.copy("vector", TV(o16.h[:, h, :], o16.name),
                            TV(PS_V.h[:, 0:64], PS_V.name))
                    em.copy("vector", TV(s16.h[:, h:h + 1], s16.name),
                            TV(PS_V.h[:, 64:65], PS_V.name))
                return go

            rest = None
            for h in range(nheads):
                pr = h % 2
                mk_S(h, pr)
                if rest is not None:
                    rest()
                rest = mk_rest(h, pr)
            rest()
            em.ts("vector", s16.ap, s16.ap, 1e-30, None, ALU.add)
            em.recip(s16.ap, s16.ap)
            for h in range(nheads):
                pr = h % 2
                ch, sub = h // hpc, h % hpc
                PS_T = ps["PT" if pr == 0 else "PT2"]
                otb = TV(o_tm.h[:, pr, :], f"o_tm.{pr}")
                em.ts("vector", otb, TV(o16.h[:, h, :], o16.name),
                      TV(s16.h[:, h:h + 1], s16.name), None, ALU.mult)
                em.transpose(TV(PS_T.h[0:64, 768:896], PS_T.name),
                             otb, sb_ident(sc))
                em.copy("scalar",
                        TV(o_fm.h[sub * 64:(sub + 1) * 64, ch,
                                  (qtl % 4) * 128:((qtl % 4) + 1) * 128],
                           o_fm.name),
                        TV(PS_T.h[0:64, 768:896], PS_T.name))
            if qtl % 4 == 3 or qtl == QTh - 1:
                ngt = (qtl % 4) + 1
                wg = ngt * 128
                g0 = (qtl - ngt + 1) * 128
                hcol = HALO_I + qtok0 + g0
                for m in range(nch):
                    slot = 2 + (m % 2)
                    wb = TV(wbuf.h[:, slot].rearrange("p c m -> p (c m)"),
                            f"wbuf.{slot}")
                    em.dma("gpsimd", wb, TV(wo.h[m], wo.name))
                    em.dma("sync", TV(xres.h[:, 0:wg], xres.name),
                           TV(hr[:, m, hcol:hcol + wg], h_in.name))
                    for c in range(nch):
                        em.mm(TV(ps["P3"].h[:, 0:wg], ps["P3"].name),
                              TV(wbuf.h[:, slot, c, :], f"wbuf.{slot}"),
                              TV(o_fm.h[:, c, 0:wg], o_fm.name),
                              start=(c == 0), stop=(c == nch - 1))
                    em.tt("vector", TV(xout.h[:, 0:wg], xout.name),
                          TV(ps["P3"].h[:, 0:wg], ps["P3"].name),
                          TV(xres.h[:, 0:wg], xres.name), ALU.add)
                    em.dma("gpsimd", TV(hmr[:, m, hcol:hcol + wg], hm.name),
                           TV(xout.h[:, 0:wg], xout.name))

    p.free(xn, q_sb, k_sb, v_tm, tstage, rsc["t1"], rsc["t2"], rsc["qsw"],
           tabs["Aq"], tabs["Bq"], tabs["Ak"], tabs["Bk"], msk, vw)
    FH = TOK // 2
    xn2 = p.sbuf("xn2", [128, nch, FH], BF16)
    xn2b = p.sbuf("xn2b", [128, nch, FH], BF16)
    glu = p.sbuf("glu", [128, DIN2 // 128, FH], BF16)
    gg = p.sbuf("gg", [128, FH], F32)
    NMT = 2 * DIN2 // 128
    NMT2 = DIN2 // 128
    emit_ln_fm(p, em, ps, sc, hm, xn2, FH,
               xcol0=HALO_I, gb=(gcol, bcol), shift_half=True)
    for hf in range(2):
        t0 = hf * FH
        xcur = xn2 if hf == 0 else xn2b
        for mi in range(NMT2):
            if hf == 0 and mi == 8:
                emit_ln_fm(p, em, ps, sc, hm, xn2b, FH,
                           xcol0=HALO_I + FH, gb=(gcol, bcol),
                           shift_half=True)
            for role in (1, 0):
                m = mi + role * NMT2
                wb = TV(wbuf.h[:, m % 4].rearrange("p c m -> p (c m)"),
                        f"wbuf.{m % 4}")
                em.dma("gpsimd", wb, TV(w1.h[m], w1.name))
                for s0 in range(0, FH, 512):
                    w = min(512, FH - s0)
                    pst = ps["P0" if (s0 // 512) % 2 == 0 else "P1"]
                    for c in range(nch):
                        em.mm(TV(pst.h[:, 0:w], pst.name),
                              TV(wbuf.h[:, m % 4, c, :], f"wbuf.{m % 4}"),
                              TV(xcur.h[:, c, s0:s0 + w], xcur.name),
                              start=(c == 0), stop=(c == nch - 1))
                    bcolv = TV(b1col.h[:, m:m + 1], b1col.name)
                    if role == 1:
                        em.act(TV(gg.h[:, s0:s0 + w], gg.name),
                               TV(pst.h[:, 0:w], pst.name), AF.Gelu,
                               bias=bcolv)
                    else:
                        em.stt("vector", TV(glu.h[:, mi, s0:s0 + w], glu.name),
                               TV(pst.h[:, 0:w], pst.name), bcolv,
                               TV(gg.h[:, s0:s0 + w], gg.name),
                               ALU.add, ALU.mult)
        for m in range(nch):
            for cc0 in range(0, NMT2, nch):
                ncc = min(nch, NMT2 - cc0)
                slot = (cc0 // nch) % 4
                wb2 = TV(wbuf.h[:, slot, 0:ncc, :]
                         .rearrange("p c m -> p (c m)"), f"wbuf.{slot}")
                em.dma("gpsimd", wb2,
                       TV(w2.h[m][:, cc0 * 128:(cc0 + ncc) * 128],
                          w2.name))
                for s0 in range(0, FH, 512):
                    w = min(512, FH - s0)
                    pst = ps["P2" if (s0 // 512) % 2 == 0 else "P3"]
                    for c in range(ncc):
                        em.mm(TV(pst.h[:, 0:w], pst.name),
                              TV(wbuf.h[:, slot, c, :], f"wbuf.{slot}"),
                              TV(glu.h[:, cc0 + c, s0:s0 + w], glu.name),
                              start=(cc0 + c == 0),
                              stop=(cc0 + c == NMT2 - 1))
            for s0 in range(0, FH, 512):
                w = min(512, FH - s0)
                pst = ps["P2" if (s0 // 512) % 2 == 0 else "P3"]
                em.dma("sync", TV(xres.h[:, 0:w], xres.name),
                       TV(hmr[:, m, HALO_I + t0 + s0:HALO_I + t0 + s0 + w],
                          hm.name))
                em.stt("vector", TV(xout.h[:, 0:w], xout.name),
                       TV(pst.h[:, 0:w], pst.name),
                       TV(b2col.h[:, m:m + 1], b2col.name),
                       TV(xres.h[:, 0:w], xres.name), ALU.add, ALU.add)
                em.dma("gpsimd", TV(hor[:, m, t0 + s0:t0 + s0 + w], h_out.name),
                       TV(xout.h[:, 0:w], xout.name))
    em.fence("sync")
    if pp is not None:
        p.free_since(mark)
    return p


def host_l1_rope_tables(pos, n=8192, dim_head=64, window=256, qk_scale=8.0):
    half = np.arange(0, dim_head, 2, dtype=np.float64)
    inv_freq = 1.0 / 10000.0 ** (half / dim_head)
    t = np.clip(pos, 0, None).astype(np.float64)
    fr = t[None, :] * inv_freq[:, None]
    base_scale = (half + 0.4 * dim_head) / (1.4 * dim_head)
    power = (t - n // 2) / (window // 2)
    scale = base_scale[:, None] ** power[None, :]
    cos, sin = np.cos(fr), np.sin(fr)
    Aq64 = np.concatenate([cos * scale, cos * scale], 0) * qk_scale
    Bq64 = np.concatenate([-sin * scale, sin * scale], 0) * qk_scale
    inv = 1.0 / scale
    Ak64 = np.concatenate([cos * inv, cos * inv], 0)
    Bk64 = np.concatenate([-sin * inv, sin * inv], 0)
    f32 = lambda a: np.concatenate([a, a], 0).astype(np.float32)
    return f32(Aq64), f32(Bq64), f32(Ak64), f32(Bk64)


def host_l1_layer_core_inputs(h_batch, wqkv, wo, qs, ks, g, b, w1, b1, w2, b2,
                              t0, TOK=TPC, HALO_I=512, WIN=256, DIN2=2816,
                              skip_h=False):
    dm = wqkv.shape[0]
    if skip_h:
        xc = np.zeros((1, dm), np.float32)
    else:
        lo = t0 - HALO_I
        if lo < 0:
            xc = np.concatenate([np.zeros((-lo, dm), np.float32),
                                 h_batch[0:t0 + TOK]], 0)
        else:
            xc = h_batch[lo:t0 + TOK]
    colmap = lambda v: np.ascontiguousarray(
        v.reshape(-1, 128).T.astype(np.float32))
    pcol = lambda v: np.ascontiguousarray(
        np.concatenate([v, v])[:, None].astype(np.float32))
    d2 = DIN2 - w2.shape[0]
    w1a = w1[:, 0:w1.shape[1] // 2]
    w1g = w1[:, w1.shape[1] // 2:]
    w1p = np.concatenate([w1a, np.zeros((dm, d2), np.float32),
                          w1g, np.zeros((dm, d2), np.float32)], 1)
    b1a = b1[0:b1.shape[0] // 2]
    b1g = b1[b1.shape[0] // 2:]
    b1p = np.concatenate([b1a, np.zeros(d2, np.float32),
                          b1g, np.zeros(d2, np.float32)])
    w2p = np.concatenate([w2, np.zeros((d2, dm), np.float32)], 0)
    pos = np.arange(t0 - WIN - 128, t0 + TOK)
    Aq, Bq, Ak, Bk = host_l1_rope_tables(pos)
    QT_ALL = TOK // 128 + 1
    maskT = host_window_maskT(t0 - 128, QT_ALL, WIN + 128, WIN + 1, lookback=WIN)
    perm_pair = (np.arange(128) % 64) ^ 32
    wv_ = wqkv[:, 2 * dm:3 * dm]
    ins = {
        "h_in": np.ascontiguousarray(xc.T),
        "wqkv": tile_w(np.ascontiguousarray(wqkv[:, 0:2 * dm])),
        "vw_in": np.ascontiguousarray(
            wv_.reshape(dm // 128, 128, dm).transpose(1, 0, 2)),
        "wo": tile_w(np.ascontiguousarray(wo)),
        "qscol": pcol(qs), "qspcol": pcol(qs[(np.arange(64)) ^ 32]),
        "kscol": pcol(ks), "kspcol": pcol(ks[(np.arange(64)) ^ 32]),
        "ln_g": colmap(g), "ln_b": colmap(b),
        "w1": tile_w(w1p), "b1c": colmap(b1p),
        "w2": np.ascontiguousarray(
            w2p.reshape(DIN2 // 128, 128, dm // 128, 128)
            .transpose(2, 1, 0, 3).reshape(dm // 128, 128, DIN2)),
        "b2c": colmap(b2),
        "ropeAq": Aq, "ropeBq": Bq, "ropeAk": Ak, "ropeBk": Bk,
        "maskT": maskT.astype(_bf16()),
        "cst_blk2": np.concatenate([np.eye(2, dtype=np.float32)[0].repeat(64)[:, None],
                                    np.eye(2, dtype=np.float32)[1].repeat(64)[:, None]], 1),
        "cst_e2": np.concatenate([np.ones((1, 64), np.float32).repeat(1, 0),
                                  np.zeros((1, 64), np.float32)], 1).reshape(1, 128).repeat(2, 0) * 0,
    }
    e2 = np.zeros((2, 128), np.float32)
    e2[0, 0:64] = 1.0
    e2[1, 64:128] = 1.0
    ins["cst_e2"] = e2
    blk2 = np.zeros((128, 2), np.float32)
    blk2[0:64, 0] = 1.0
    blk2[64:128, 1] = 1.0
    ins["cst_blk2"] = blk2
    ins.update(host_common_inputs())
    return ins


def build_l1_prep(TOK=TPC, nch=NCH, pp=None, io=None):
    if pp is None:
        p = Prog("l1prep")
        em = p.em
        x = p.din("x_fm", [nch * 128, TOK])
        h0 = p.dout("h0", [nch * 128, TOK])
        ps = alloc_psum_pool(p)
        sc = alloc_common(p, nch)
    else:
        p = pp
        em = p.em
        x, h0 = io["x"], io["h0"]
        ps, sc = io["ps"], io["sc"]
    mark = p.mark()
    stg = p.sbuf("stg", [128, nch, TOK], F32)
    emit_ln_fm(p, em, ps, sc, x, stg, TOK)
    em.dma("gpsimd",
           TV(h0.h.rearrange("(c p) t -> p c t", p=128), h0.name),
           stg.ap)
    em.fence("sync")
    if pp is not None:
        p.free_since(mark)
    return p


def build_l1_fin(TOK=TPC, nch=NCH, pp=None, io=None):
    if pp is None:
        p = Prog("l1fin")
        em = p.em
        x = p.din("x_fm", [nch * 128, TOK])
        h = p.din("h_fm", [nch * 128, TOK])
        xo = p.dout("xo_fm", [nch * 128, TOK])
    else:
        p = pp
        em = p.em
        x, h, xo = io["x"], io["h"], io["xo"]
    mark = p.mark()
    xa = p.sbuf("xa", [128, 512], F32)
    ha = p.sbuf("ha", [128, 512], F32)
    oa = p.sbuf("oa", [128, 512], F32)
    xr = x.h.rearrange("(c p) t -> p c t", p=128)
    hr = h.h.rearrange("(c p) t -> p c t", p=128)
    orr = xo.h.rearrange("(c p) t -> p c t", p=128)
    for c in range(nch):
        for s0 in range(0, TOK, 512):
            em.dma("sync", xa.ap, TV(xr[:, c, s0:s0 + 512], x.name))
            em.dma("sync", ha.ap, TV(hr[:, c, s0:s0 + 512], h.name))
            em.stt("vector", oa.ap, ha.ap, 0.5, xa.ap, ALU.mult, ALU.add)
            em.dma("gpsimd", TV(orr[:, c, s0:s0 + 512], xo.name), oa.ap)
    em.fence("sync")
    if pp is not None:
        p.free_since(mark)
    return p


def _core_ranges():
    return [(c // 4, (c % 4) * TPC) for c in range(N_CORES)]


def run_phase_l1_ffn(x_np, Wqkv6, qs6, ks6, Wo6, g6, b6, W16, b16, W26, b26,
                     depth=6):
    p1, nc1 = _get_prog("l1prep", build_l1_prep)
    pl, ncl = _get_prog("l1layer", build_l1_layer)
    p2, nc2 = _get_prog("l1fin", build_l1_fin)
    com = host_common_inputs()

    def fm(a):
        return np.ascontiguousarray(a.T)

    in_maps = []
    for b, t0 in _core_ranges():
        m = {"x_fm": fm(x_np[b, t0:t0 + TPC])}
        m.update(com)
        in_maps.append(m)
    res = run_bass_kernel_spmd(nc1, in_maps, core_ids=list(range(N_CORES)))
    h_np = np.empty_like(x_np)
    for c, (b, t0) in enumerate(_core_ranges()):
        h_np[b, t0:t0 + TPC] = res.results[c]["h0"].T
    for i in range(depth):
        in_maps = []
        for b, t0 in _core_ranges():
            in_maps.append(host_l1_layer_core_inputs(
                h_np[b], Wqkv6[i], Wo6[i], qs6[i], ks6[i], g6[i], b6[i],
                W16[i], b16[i], W26[i], b26[i], t0))
        res = run_bass_kernel_spmd(ncl, in_maps, core_ids=list(range(N_CORES)))
        for c, (b, t0) in enumerate(_core_ranges()):
            h_np[b, t0:t0 + TPC] = res.results[c]["h_out"].T
    in_maps = []
    for b, t0 in _core_ranges():
        in_maps.append({"x_fm": fm(x_np[b, t0:t0 + TPC]),
                        "h_fm": fm(h_np[b, t0:t0 + TPC])})
    res = run_bass_kernel_spmd(nc2, in_maps, core_ids=list(range(N_CORES)))
    out = np.empty_like(x_np)
    for c, (b, t0) in enumerate(_core_ranges()):
        out[b, t0:t0 + TPC] = res.results[c]["xo_fm"].T
    return out



def build_l0f_router(TOK=TPC, nch=NCH, H=4096, pp=None, io=None):
    if pp is None:
        p = Prog("l0fr")
        em = p.em
        x = p.din("x_fm", [nch * 128, TOK])
        rw1 = p.din("rw1", [H // 128, 128, nch * 128])
        rb1c = p.din("rb1c", [128, H // 128])
        rw2 = p.din("rw2", [H, 8])
        rb2 = p.din("rb2", [1, 8])
        lnx = p.dout("lnx", [nch * 128, TOK], BF16)
        logits = p.dout("logits", [TOK, 8])
        ps = alloc_psum_pool(p)
        sc = alloc_common(p, nch)
    else:
        p = pp
        em = p.em
        x, rw1, rb1c, rw2, rb2, lnx, logits = (io[k] for k in
            ("x", "rw1", "rb1c", "rw2", "rb2", "lnx", "logits"))
        ps, sc = io["ps"], io["sc"]
    mark = p.mark()
    xn = p.sbuf("xn", [128, nch, TOK], BF16)
    emit_ln_fm(p, em, ps, sc, x, xn, TOK)
    if lnx is not None:
        em.dma("gpsimd",
               TV(lnx.h.rearrange("(c p) t -> p c t", p=128), lnx.name),
               xn.ap)
    if io is not None and io.get("lnx_tm") is not None:
        emit_router_tm(p, em, ps, sc, xn, io["lnx_tm"], TOK, nch)
    wbuf = p.sbuf("wbuf", [128, 2, nch, 128], BF16)
    rw2b = p.sbuf("rw2b", [128, H // 128, 8], BF16)
    em.dma("gpsimd", rw2b.ap,
           TV(rw2.h.rearrange("(m p) e -> p m e", p=128), rw2.name))
    rb1col = p.sbuf("rb1col", [128, H // 128], F32)
    em.dma("sync", rb1col.ap, rb1c.ap)
    rb2s = p.sbuf("rb2s", [1, 8], F32)
    em.dma("sync", rb2s.ap, rb2.ap)
    lg = p.sbuf("lg", [128, 8], F32)
    rb2bc = p.sbuf("rb2bc", [128, 8], F32)
    h1 = p.sbuf("h1", [128, H // 128, 1024], BF16)
    NM = H // 128
    for tb in range(TOK // 1024):
        tb0 = tb * 1024
        for m in range(NM):
            wb = TV(wbuf.h[:, m % 2].rearrange("p c m -> p (c m)"),
                    f"wbuf.{m % 2}")
            em.dma("gpsimd", wb, TV(rw1.h[m], rw1.name))
            for s0 in range(0, 1024, 512):
                pst = ps["P0" if (s0 // 512) % 2 == 0 else "P1"]
                for c in range(nch):
                    em.mm(pst.ap,
                          TV(wbuf.h[:, m % 2, c, :], f"wbuf.{m % 2}"),
                          TV(xn.h[:, c, tb0 + s0:tb0 + s0 + 512], xn.name),
                          start=(c == 0), stop=(c == nch - 1))
                em.act(TV(h1.h[:, m, s0:s0 + 512], h1.name), pst.ap, AF.Relu,
                       bias=TV(rb1col.h[:, m:m + 1], rb1col.name))
        em.mm(TV(ps["P2"].h[:, 0:8], ps["P2"].name), sc["ones_row"].ap, rb2s.ap)
        em.copy("vector", rb2bc.ap, TV(ps["P2"].h[:, 0:8], ps["P2"].name))
        for tt_ in range(8):
            pst = ps["P3"]
            for m in range(NM):
                em.mm(TV(pst.h[:, 0:8], pst.name),
                      TV(h1.h[:, m, tt_ * 128:(tt_ + 1) * 128], h1.name),
                      TV(rw2b.h[:, m, :], rw2b.name),
                      start=(m == 0), stop=(m == NM - 1))
            em.copy("scalar", TV(lg.h[:, :], "lg_rb2"),
                    TV(ps["P2"].h[:, 0:8], ps["P2"].name)) if False else None
            em.tt("vector", lg.ap, TV(pst.h[:, 0:8], pst.name),
                  rb2bc.ap, ALU.add)
            em.dma("gpsimd",
                   TV(logits.h[tb0 + tt_ * 128:tb0 + (tt_ + 1) * 128, :],
                      logits.name),
                   lg.ap)
    em.fence("sync")
    if pp is not None:
        p.free_since(mark)
    return p


def emit_routing(p, em, ps, sc, logits_all, NT, CAP=4096, tril_tile=None):
    J = NT // 128
    _rt_tiles = []
    _sb = p.sbuf
    def _sbuf(name, shape, dtype=F32):
        t = _sb(name, shape, dtype)
        _rt_tiles.append(t)
        return t
    p_sbuf_saved = p.sbuf
    p.sbuf = _sbuf
    lgt = p.sbuf("lgt", [128, J, 8], F32)
    em.dma("sync", lgt.ap,
           TV(logits_all.h.rearrange("(pp j) e -> pp j e", pp=128),
              logits_all.name))
    m1v = p.sbuf("m1v", [128, J], F32)
    m2v = p.sbuf("m2v", [128, J], F32)
    m1 = p.sbuf("m1", [128, J, 8], F32)
    m2 = p.sbuf("m2", [128, J, 8], F32)
    s12 = p.sbuf("s12", [128, J, 8], F32)
    pex = p.sbuf("pex", [128, J, 8], F32)
    tot = p.sbuf("tot", [128, 8], F32)
    offs = p.sbuf("offs", [128, 8], F32)
    pos1 = p.sbuf("pos1", [128, J], F32)
    pos2 = p.sbuf("pos2", [128, J], F32)
    w1b = p.sbuf("w1b", [128, J], F32)
    w2b = p.sbuf("w2b", [128, J], F32)
    tmp = p.sbuf("rt_tmp", [128, J, 8], F32)
    tril = tril_tile if tril_tile is not None else \
        p.din("cst_trilstrict", [128, 128])
    trils = p.sbuf("trils", [128, 128], F32)
    em.dma("sync", trils.ap, tril.ap)
    zcol = p.sbuf("zcol", [128, J + 1], F32)

    b3 = lambda t: t.ap
    em.reduce("vector", m1v.ap, b3(lgt), ALU.max)
    em.tt("vector", b3(m1), b3(lgt), bcast(m1v.ap, 8), ALU.is_ge)
    em.stt("vector", b3(tmp), b3(m1), -1e30, b3(lgt), ALU.mult, ALU.add)
    em.reduce("vector", m2v.ap, b3(tmp), ALU.max)
    em.tt("vector", b3(m2), b3(tmp), bcast(m2v.ap, 8), ALU.is_ge)
    em.tt("vector", b3(s12), b3(m1), b3(m2), ALU.add)
    zrow = p.sbuf("zrow", [128, J], F32)
    em.memset("vector", zrow.ap, 0.0)
    em.memset("vector", TV(zcol.h[:, 0:1], zcol.name), 0.0)
    for e in range(8):
        sv = TV(s12.h[:].rearrange("p j e -> p e j")[:, e, :], s12.name)
        em.copy("vector", TV(zcol.h[:, 1:J + 1], zcol.name), sv)
        pv = TV(pex.h[:].rearrange("p j e -> p e j")[:, e, :], pex.name)
        zin = TV(zcol.h[:, 0:J], zcol.name)
        zr = zrow.ap
        em.op("vector",
              (lambda o, a, b: lambda eng: eng.tensor_tensor_scan(
                  o.ap, a.ap, b.ap, 0.0, ALU.add, ALU.add))(pv, zin, zr),
              reads=[zin, zr], writes=[pv])
    em.reduce("vector", tot.ap,
              TV(s12.h[:].rearrange("p j e -> p e j"), s12.name), ALU.add)
    em.mm(TV(ps["P2"].h[:, 0:8], ps["P2"].name), trils.ap, tot.ap)
    em.copy("vector", offs.ap, TV(ps["P2"].h[:, 0:8], ps["P2"].name))
    ob = TV(offs.h[:].rearrange("p e -> p e"), offs.name)
    from concourse.ap import AP as _AP
    oap = offs.h[:]
    oap2 = _AP(oap.tensor, oap.offset,
               [list(oap.ap[0]), [0, J], list(oap.ap[1])])
    em.tt("vector", b3(pex), b3(pex), TV(oap2, offs.name), ALU.add)
    em.tt("vector", b3(tmp), b3(pex), b3(m1), ALU.mult)
    em.reduce("vector", pos1.ap, b3(tmp), ALU.add)
    em.tt("vector", b3(tmp), b3(pex), b3(m2), ALU.mult)
    em.reduce("vector", pos2.ap, b3(tmp), ALU.add)
    k1 = p.sbuf("k1", [128, J], F32)
    k2 = p.sbuf("k2", [128, J], F32)
    em.ts("vector", k1.ap, pos1.ap, float(CAP), None, ALU.is_lt)
    em.ts("vector", k2.ap, pos2.ap, float(CAP), None, ALU.is_lt)
    em.tt("vector", w1b.ap, m1v.ap, m2v.ap, ALU.subtract)
    em.act(w1b.ap, w1b.ap, AF.Sigmoid)
    em.ts("vector", w2b.ap, w1b.ap, -1.0, 1.0, ALU.mult, ALU.add)
    em.tt("vector", w1b.ap, w1b.ap, k1.ap, ALU.mult)
    em.tt("vector", w2b.ap, w2b.ap, k2.ap, ALU.mult)
    p.sbuf = p_sbuf_saved
    return dict(m1=m1, m2=m2, w1b=w1b, w2b=w2b, pos1=pos1, pos2=pos2,
                k1=k1, k2=k2, J=J, tiles=_rt_tiles)


def build_l0f_expert(NT=B * T, nch=NCH, H=4096, TOKSUB=1024, pp=None, io=None):
    if pp is None:
        p = Prog("l0fe")
        em = p.em
        lnx = p.din("lnx_all", [nch * 128, NT], BF16)
        logits_all = p.din("logits_all", [NT, 8])
        ew1 = p.din("ew1", [H // 128, 128, nch * 128])
        eb1c = p.din("eb1c", [128, H // 128])
        ew2x = p.din("ew2x", [nch, 128, H])
        eb2r = p.din("eb2r", [1, nch * 128])
        esel = p.din("esel", [1, 8])
        ones512 = p.din("cst_ones512", [1, 512])
        y = p.dout("y_fm", [nch * 128, NT], BF16)
        ps = alloc_psum_pool(p)
        sc = alloc_common(p, nch)
        lr_ = lnx.h.rearrange("(c p) t -> p c t", p=128)
        yr_ = y.h.rearrange("(c p) t -> p c t", p=128)
        lnx_src = lambda tb0, n: TV(lr_[:, :, tb0:tb0 + n], lnx.name)
        y_dst = (lambda m2, t0_, w:
                 TV(yr_[:, m2, t0_:t0_ + w], y.name))
    else:
        p = pp
        em = p.em
        logits_all, ew1, eb1c, ew2x, esel, ones512 = (io[k] for k in
            ("logits_all", "ew1", "eb1c", "ew2x", "esel", "ones512"))
        eb2r = io["eb2r"]
        _trilt = io.get("trils")
        ps, sc = io["ps"], io["sc"]
        lnxb = io["lnx_blocks"]
        yb = io["y_blocks"]

        def lnx_src(tb0, n):
            blk, loc = tb0 // TPC, tb0 % TPC
            return TV(lnxb.h[blk].rearrange("(c p) t -> p c t", p=128)
                      [:, :, loc:loc + n], lnxb.name)

        def y_dst(m2, t0_, w):
            blk, loc = t0_ // TPC, t0_ % TPC
            return TV(yb.h[blk].rearrange("(c p) t -> p c t", p=128)
                      [:, m2, loc:loc + w], yb.name)
    mark = p.mark()
    wr2d = p.dram("wr2d", [128, NT // 128])
    wr2s = p.sbuf("wr2s", [1, 512], F32)
    rt = emit_routing(p, em, ps, sc, logits_all, NT,
                      tril_tile=(io.get("trils") if io else None))
    J = rt["J"]
    esel_s = p.sbuf("esel_s", [1, 8], F32)
    em.dma("sync", esel_s.ap, esel.ap)
    em.mm(TV(ps["P2"].h[:, 0:8], ps["P2"].name), sc["ones_row"].ap, esel_s.ap)
    eselb = p.sbuf("eselb", [128, 8], F32)
    em.copy("vector", eselb.ap, TV(ps["P2"].h[:, 0:8], ps["P2"].name))
    from concourse.ap import AP as _AP
    eap = eselb.h[:]
    eap2 = _AP(eap.tensor, eap.offset,
               [list(eap.ap[0]), [0, J], list(eap.ap[1])])
    tmp = p.sbuf("ce_tmp", [128, J, 8], F32)
    wown = p.sbuf("wown", [128, J], F32)
    b3 = lambda t: t.ap
    em.tt("vector", b3(tmp), b3(rt["m1"]), TV(eap2, eselb.name), ALU.mult)
    em.tt("vector", b3(tmp), b3(tmp), bcast(rt["w1b"].ap, 8), ALU.mult)
    tmp2 = p.sbuf("ce_tmp2", [128, J, 8], F32)
    em.tt("vector", b3(tmp2), b3(rt["m2"]), TV(eap2, eselb.name), ALU.mult)
    em.tt("vector", b3(tmp2), b3(tmp2), bcast(rt["w2b"].ap, 8), ALU.mult)
    em.tt("vector", b3(tmp), b3(tmp), b3(tmp2), ALU.add)
    em.reduce("vector", wown.ap, b3(tmp), ALU.add)
    em.dma("scalar", wr2d.ap, TV(wown.h[:, :], wown.name))
    p.free(*(rt["tiles"] + [esel_s, eselb, tmp, tmp2, wown]))
    wr2flat = TV(wr2d.h.rearrange("pp j -> (pp j)"), wr2d.name)

    o512 = p.sbuf("o512", [1, 512], BF16)
    em.dma("gpsimd", o512.ap, ones512.ap)
    eb1col = p.sbuf("eb1col", [128, H // 128], F32)
    em.dma("sync", eb1col.ap, eb1c.ap)
    wbuf = p.sbuf("wbufE", [128, 2, nch, 128], BF16)
    w2buf = p.sbuf("w2bufE", [128, 2, nch, 128], BF16)
    h1 = p.sbuf("h1E", [128, H // 128, TOKSUB], BF16)
    bcs = p.sbuf("bcsE", [128, 512], F32)
    yout = p.sbuf("youtE", [128, 512], BF16)
    NM = H // 128
    xnsub = p.sbuf("xnsubE", [128, nch, TOKSUB], BF16)
    b2row = p.sbuf("b2rowE", [1, nch * 128], BF16)
    em.dma("gpsimd", b2row.ap, eb2r.ap)
    for tb in range(NT // TOKSUB):
        tb0 = tb * TOKSUB
        em.dma("sync", xnsub.ap, lnx_src(tb0, TOKSUB))
        for m in range(NM):
            wb = TV(wbuf.h[:, m % 2].rearrange("p c m -> p (c m)"),
                    f"wbufE.{m % 2}")
            em.dma("gpsimd", wb, TV(ew1.h[m], ew1.name))
            for s0 in range(0, TOKSUB, 512):
                pst = ps["P0" if (s0 // 512) % 2 == 0 else "P1"]
                for c in range(nch):
                    em.mm(pst.ap,
                          TV(wbuf.h[:, m % 2, c, :], f"wbufE.{m % 2}"),
                          TV(xnsub.h[:, c, s0:s0 + 512], xnsub.name),
                          start=(c == 0), stop=(c == nch - 1))
                em.act(TV(h1.h[:, m, s0:s0 + 512], h1.name), pst.ap, AF.Relu,
                       bias=TV(eb1col.h[:, m:m + 1], eb1col.name))
        for s0 in range(0, TOKSUB, 512):
            em.dma("sync", wr2s.ap,
                   TV(wr2d.h.rearrange("pp j -> (pp j)")
                      [tb0 + s0:tb0 + s0 + 512], wr2d.name))
            em.mm(TV(ps["P2"].h[:, :], ps["P2"].name), sc["ones_row"].ap,
                  wr2s.ap)
            em.copy("vector", bcs.ap, TV(ps["P2"].h[:, :], ps["P2"].name))
            for m2 in range(nch):
                pst = ps["P3"]
                for cc0 in range(0, NM, nch):
                    slot = (cc0 // nch) % 2
                    w2b_ = TV(w2buf.h[:, slot].rearrange("p c m -> p (c m)"),
                              f"w2bufE.{slot}")
                    em.dma("gpsimd", w2b_,
                           TV(ew2x.h[m2][:, cc0 * 128:(cc0 + nch) * 128],
                              ew2x.name))
                    for c in range(nch):
                        em.mm(pst.ap,
                              TV(w2buf.h[:, slot, c, :], f"w2bufE.{slot}"),
                              TV(h1.h[:, cc0 + c, s0:s0 + 512], h1.name),
                              start=(cc0 + c == 0), stop=False)
                em.mm(pst.ap,
                      TV(b2row.h[:, m2 * 128:(m2 + 1) * 128], b2row.name),
                      o512.ap, start=False, stop=True)
                em.tt("vector", yout.ap, pst.ap, bcs.ap, ALU.mult)
                em.dma("gpsimd", y_dst(m2, tb0 + s0, 512), yout.ap)
    em.fence("sync")
    if pp is not None:
        p.free_since(mark)
    return p


def build_l0f_combine(TOK=TPC, nch=NCH):
    p = Prog("l0fc")
    em = p.em
    x = p.din("x_fm", [nch * 128, TOK])
    parts = p.din("parts", [8, nch * 128, TOK], BF16)
    xo = p.dout("xo_fm", [nch * 128, TOK])
    xa = p.sbuf("xaC", [128, 512], F32)
    pa = p.sbuf("paC", [128, 8, 512], BF16)
    acc = p.sbuf("accC", [128, 512], F32)
    xr = x.h.rearrange("(c p) t -> p c t", p=128)
    pr = parts.h.rearrange("ee (c p) t -> p ee c t", p=128)
    orr = xo.h.rearrange("(c p) t -> p c t", p=128)
    for c in range(nch):
        for s0 in range(0, TOK, 512):
            em.dma("sync", xa.ap, TV(xr[:, c, s0:s0 + 512], x.name))
            em.dma("sync", pa.ap, TV(pr[:, :, c, s0:s0 + 512], parts.name))
            em.tt("vector", acc.ap, TV(pa.h[:, 0, :], pa.name),
                  TV(pa.h[:, 1, :], pa.name), ALU.add)
            for e in range(2, 8):
                em.tt("vector", acc.ap, acc.ap, TV(pa.h[:, e, :], pa.name),
                      ALU.add)
            em.stt("vector", acc.ap, acc.ap, 0.5, xa.ap, ALU.mult, ALU.add)
            em.dma("gpsimd", TV(orr[:, c, s0:s0 + 512], xo.name), acc.ap)
    em.fence("sync")
    return p


def host_l0f_common():
    tril = np.tril(np.ones((128, 128), np.float32), -1)
    trilT = np.triu(np.ones((128, 128), np.float32), 1).T
    trils = np.triu(np.ones((128, 128), np.float32), 1)
    return {"cst_trilstrict": trils,
            "cst_ones512": np.ones((1, 512), np.float32)}


def run_phase_l0_ffn(x_np, rw1, rb1, rw2, rb2, ew1, eb1, ew2, eb2):
    pr, ncr = _get_prog("l0fr", build_l0f_router)
    pe, nce = _get_prog("l0fe", build_l0f_expert)
    pc, ncc = _get_prog("l0fc", build_l0f_combine)
    com = host_common_inputs()
    rb1c = np.ascontiguousarray(rb1.reshape(-1, 128).T)
    rw1t = tile_w(rw1)
    in_maps = []
    for b, t0 in _core_ranges():
        m = {"x_fm": np.ascontiguousarray(x_np[b, t0:t0 + TPC].T),
             "rw1": rw1t, "rb1c": rb1c, "rw2": rw2,
             "rb2": rb2.reshape(1, 8)}
        m.update(com)
        in_maps.append(m)
    res = run_bass_kernel_spmd(ncr, in_maps, core_ids=list(range(N_CORES)))
    NT = B * T
    lnx_all = np.empty((D, NT), dtype=res.results[0]["lnx"].dtype)
    logits_all = np.empty((NT, 8), np.float32)
    for c, (b, t0) in enumerate(_core_ranges()):
        g0 = b * T + t0
        lnx_all[:, g0:g0 + TPC] = res.results[c]["lnx"]
        logits_all[g0:g0 + TPC] = res.results[c]["logits"]
    in_maps = []
    for e in range(8):
        esel = np.zeros((1, 8), np.float32)
        esel[0, e] = 1.0
        H_ = ew1.shape[2]
        ew2t = np.ascontiguousarray(
            ew2[e].reshape(H_ // 128, 128, 8, 128).transpose(2, 1, 0, 3)
            .reshape(8, 128, H_))
        m = {"lnx_all": lnx_all, "logits_all": logits_all,
             "ew1": tile_w(ew1[e]),
             "eb1c": np.ascontiguousarray(eb1[e].reshape(-1, 128).T),
             "ew2x": ew2t, "esel": esel,
             "eb2r": eb2[e].reshape(1, -1)}
        m.update(com)
        m.update(host_l0f_common())
        in_maps.append(m)
    res = run_bass_kernel_spmd(nce, in_maps, core_ids=list(range(N_CORES)))
    ys = [res.results[e]["y_fm"] for e in range(8)]
    in_maps = []
    for b, t0 in _core_ranges():
        g0 = b * T + t0
        parts = np.stack([ys[e][:, g0:g0 + TPC] for e in range(8)], 0)
        m = {"x_fm": np.ascontiguousarray(x_np[b, t0:t0 + TPC].T),
             "parts": parts}
        in_maps.append(m)
    res = run_bass_kernel_spmd(ncc, in_maps, core_ids=list(range(N_CORES)))
    out = np.empty_like(x_np)
    for c, (b, t0) in enumerate(_core_ranges()):
        out[b, t0:t0 + TPC] = res.results[c]["xo_fm"].T
    return out



def build_ln_bf16(TOK=TPC, nch=NCH, pp=None, io=None):
    if pp is None:
        p = Prog("lnbf")
        em = p.em
        x = p.din("x_fm", [nch * 128, TOK])
        lnx = p.dout("lnx", [nch * 128, TOK], BF16)
        ps = alloc_psum_pool(p)
        sc = alloc_common(p, nch)
    else:
        p = pp
        em = p.em
        x, lnx = io["x"], io["lnx"]
        ps, sc = io["ps"], io["sc"]
    mark = p.mark()
    xn = p.sbuf("xn", [128, nch, TOK], BF16)
    emit_ln_fm(p, em, ps, sc, x, xn, TOK)
    em.dma("gpsimd", TV(lnx.h.rearrange("(c p) t -> p c t", p=128), lnx.name),
           xn.ap)
    em.fence("sync")
    if pp is not None:
        p.free_since(mark)
    return p


def build_l1_attn(TT=T, nch=NCH, DK=256, CH=128, pp=None, io=None):
    dm = nch * 128
    NCH2 = DK // 128
    NCK = TT // CH
    if pp is None:
        p = Prog("l1attn")
        em = p.em
        hx = p.din("hx", [dm, TT], BF16)
        wq = p.din("wq", [dm, DK])
        wk = p.din("wk", [dm, DK])
        wv = p.din("wv", [dm, DK])
        wb = p.din("wb", [128, nch])
        qcw = p.din("qcw", [128, NCH2, 4])
        kcw = p.din("kcw", [128, NCH2, 4])
        vcw = p.din("vcw", [128, NCH2, 4])
        firs = p.din("firs", [128, NCH2, 3])
        firl = p.din("firl", [128, NCH2, 31])
        gw1 = p.din("gw1", [5120, 512])
        gw1h = p.din("gw1h", [4, 128, 1024])
        gw1b = p.din("gw1b", [4, 128, 1024])
        gb1c = p.din("gb1c", [128, 4])
        gw2 = p.din("gw2", [512, 4])
        gb2 = p.din("gb2", [1, 4])
        temp = p.din("temp", [1, 1])
        epsf = p.din("epsf", [1, 4])
        onwr = p.din("onwr", [1, DK])
        trilm = p.din("cst_trilm", [128, 128], BF16)
        trium = p.din("cst_trium", [128, 128], BF16)
        triuI = p.din("cst_triuI", [128, 128], BF16)
        o_out = p.dout("o_h", [DK, TT], BF16)
        ps = alloc_psum_pool(p)
        sc = alloc_common(p, nch)
    else:
        p = pp
        em = p.em
        (hx, wq, wk, wv, wb, qcw, kcw, vcw, firs, firl, gw1, gw1h, gw1b,
         gb1c, gw2, gb2, temp, epsf, onwr, trilm, trium,
         triuI, o_out) = (io[k] for k in
            ("hx", "wq", "wk", "wv", "wb", "qcw", "kcw", "vcw", "firs",
             "firl", "gw1", "gw1h", "gw1b", "gb1c", "gw2", "gb2", "temp",
             "epsf", "onwr", "trilm", "trium", "triuI", "o_out"))
        ps, sc = io["ps"], io["sc"]
    mark = p.mark()

    br_tm = p.dram("br_tm", [128, NCK, 4, DK], BF16)
    br_fm = p.dram("br_fm", [3, DK, TT], BF16)
    g1r_d = p.dram("g1r_d", [12, 512])

    q_fm = p.sbuf("q_fm", [128, NCH2, TT], BF16)
    k_fm = p.sbuf("k_fm", [128, NCH2, TT], BF16)
    v_fm = p.sbuf("v_fm", [128, NCH2, 32 + TT], BF16)
    tmpf = p.sbuf("tmpf", [128, NCH2, TT], BF16)
    beta_d = p.dram("beta_d", [1, TT])
    bsta = p.sbuf("bsta", [1, 512], F32)
    bst = p.sbuf("bst", [1, 128], F32)
    S_sb = p.sbuf("S_sb", [128, NCH2, DK], F32)
    S_bf = p.sbuf("S_bf", [128, NCH2, DK], BF16)
    YB = [p.sbuf(f"Y_sb{i}", [128, 2 * DK], BF16) for i in range(2)]
    XB = [p.sbuf(f"X_sb{i}", [128, 128], BF16) for i in range(2)]
    XTB = [p.sbuf(f"XT_sb{i}", [128, 128], BF16) for i in range(2)]
    u_sb = p.sbuf("u_sb", [128, DK], BF16)
    WF = [p.sbuf(f"w_fmc{i}", [128, NCH2, 128], BF16) for i in range(2)]
    KT = [p.sbuf(f"k_tmc{i}", [128, DK], BF16) for i in range(2)]
    df_fm = p.sbuf("df_fm", [128, NCH2, 128], BF16)
    at_sb = p.sbuf("at_sb", [128, 128], BF16)
    bb_sb = p.sbuf("bb_sb", [128, 128], F32)
    trilms = p.sbuf("trilms", [128, 128], BF16)
    triums = p.sbuf("triums", [128, 128], BF16)
    triuIs = p.sbuf("triuIs", [128, 128], BF16)
    xnsub = p.sbuf("xnsubA", [128, nch, 512], BF16)
    wstage = p.sbuf("wstageA", [128, nch, 128], BF16)
    em.dma("sync", trilms.ap, trilm.ap)
    em.dma("sync", triums.ap, trium.ap)
    em.dma("sync", triuIs.ap, triuI.ap)

    hr = hx.h.rearrange("(c p) t -> p c t", p=128)

    def project(dst, wsrc, conv_w, pad):
        for mt in range(DK // 128):
            em.dma("gpsimd", TV(wstage.h[:, :, :], wstage.name),
                   TV(wsrc.h.rearrange("(c p) m -> p c m", p=128)
                      [:, :, mt * 128:(mt + 1) * 128], wsrc.name))
            for s0 in range(0, TT, 512):
                pst = ps["P0" if (s0 // 512) % 2 == 0 else "P1"]
                em.dma("sync", xnsub.ap, TV(hr[:, :, s0:s0 + 512], hx.name))
                for c in range(nch):
                    em.mm(pst.ap, TV(wstage.h[:, c, :], wstage.name),
                          TV(xnsub.h[:, c, :], xnsub.name),
                          start=(c == 0), stop=(c == nch - 1))
                em.copy("vector",
                        TV(tmpf.h[:, mt, s0:s0 + 512], tmpf.name), pst.ap)
        for c in range(NCH2):
            if pad:
                em.memset("vector", TV(dst.h[:, c, 0:pad], dst.name), 0.0)
            em.ts("vector", TV(dst.h[:, c, pad:pad + TT], dst.name),
                  TV(tmpf.h[:, c, 0:TT], tmpf.name),
                  TV(conv_w.h[:, c, 3:4], conv_w.name), None, ALU.mult)
            for j in range(1, 4):
                em.stt("vector", TV(dst.h[:, c, pad + j:pad + TT], dst.name),
                       TV(tmpf.h[:, c, 0:TT - j], tmpf.name),
                       TV(conv_w.h[:, c, 3 - j:4 - j], conv_w.name),
                       TV(dst.h[:, c, pad + j:pad + TT], dst.name),
                       ALU.mult, ALU.add)
            em.act(TV(dst.h[:, c, pad:pad + TT], dst.name),
                   TV(dst.h[:, c, pad:pad + TT], dst.name), AF.Silu)

    qcs = p.sbuf("qcs", [128, NCH2, 4], F32)
    kcs = p.sbuf("kcs", [128, NCH2, 4], F32)
    vcs = p.sbuf("vcs", [128, NCH2, 4], F32)
    em.dma("sync", qcs.ap, qcw.ap)
    em.dma("sync", kcs.ap, kcw.ap)
    em.dma("sync", vcs.ap, vcw.ap)
    project(q_fm, wq, qcs, 0)
    project(k_fm, wk, kcs, 0)
    project(v_fm, wv, vcs, 32)
    wbs = p.sbuf("wbs", [128, nch], BF16)
    em.dma("gpsimd", wbs.ap, wb.ap)
    for s0 in range(0, TT, 512):
        em.dma("sync", xnsub.ap, TV(hr[:, :, s0:s0 + 512], hx.name))
        pst = ps["R1"]
        for c in range(nch):
            em.mm(TV(pst.h[0:1, :], pst.name),
                  TV(wbs.h[:, c:c + 1], wbs.name),
                  TV(xnsub.h[:, c, :], xnsub.name),
                  start=(c == 0), stop=(c == nch - 1))
        em.act(bsta.ap, TV(pst.h[0:1, :], pst.name), AF.Sigmoid)
        em.dma("gpsimd", TV(beta_d.h[:, s0:s0 + 512], beta_d.name), bsta.ap)

    def l2n(dst, off):
        for s0 in range(0, TT, 512):
            for c in range(NCH2):
                em.act(TV(sc["xsq"].h[:, :], sc["xsq"].name),
                       TV(dst.h[:, c, off + s0:off + s0 + 512], dst.name),
                       AF.Square)
                em.mm(TV(ps["R2"].h[0:1, :], ps["R2"].name), sc["ones_col"].ap,
                      TV(sc["xsq"].h[:, :], sc["xsq"].name),
                      start=(c == 0), stop=(c == NCH2 - 1))
            rw = lambda t: TV(t.h[:, 0:512], t.name)
            em.ts("vector", rw(sc["r_var"]),
                  TV(ps["R2"].h[0:1, :], ps["R2"].name), 1e-6, None, ALU.add)
            em.act(rw(sc["r_sd"]), rw(sc["r_var"]), AF.Sqrt)
            em.recip(rw(sc["r_rs"]), rw(sc["r_sd"]))
            em.mm(TV(ps["P2"].h[:, :], ps["P2"].name), sc["ones_row"].ap,
                  rw(sc["r_rs"]))
            for c in range(NCH2):
                em.tt("vector", TV(dst.h[:, c, off + s0:off + s0 + 512],
                                   dst.name),
                      TV(dst.h[:, c, off + s0:off + s0 + 512], dst.name),
                      TV(ps["P2"].h[:, :], ps["P2"].name), ALU.mult)
    l2n(q_fm, 0)
    l2n(k_fm, 0)

    def to_tm(src_tile, src_cols, ck, slot):
        for c in range(NCH2):
            em.transpose(TV(ps["PT"].h[:, c * 128:(c + 1) * 128], ps["PT"].name),
                         TV(src_tile.h[:, c, src_cols + ck * 128:
                                       src_cols + (ck + 1) * 128],
                            src_tile.name), sb_ident(sc))
        em.copy("vector", TV(u_sb.h[:, :], u_sb.name),
                TV(ps["PT"].h[:, 0:DK], ps["PT"].name))
        em.dma("gpsimd", TV(br_tm.h[:, ck, slot, :], br_tm.name),
               TV(u_sb.h[:, :], u_sb.name))

    def fir(ntap, fw, slot):
        for c in range(NCH2):
            em.ts("vector", TV(tmpf.h[:, c, :], tmpf.name),
                  TV(v_fm.h[:, c, 32:32 + TT], v_fm.name),
                  TV(fw.h[:, c, ntap - 1:ntap], fw.name), None, ALU.mult)
            for j in range(1, ntap):
                eng = "vector"
                em.stt(eng, TV(tmpf.h[:, c, :], tmpf.name),
                       TV(v_fm.h[:, c, 32 - j:32 + TT - j], v_fm.name),
                       TV(fw.h[:, c, ntap - 1 - j:ntap - j], fw.name),
                       TV(tmpf.h[:, c, :], tmpf.name), ALU.mult, ALU.add)
        em.dma("gpsimd",
               TV(br_fm.h[slot].rearrange("(c p) t -> p c t", p=128),
                  br_fm.name), tmpf.ap)
        for ck in range(NCK):
            to_tm(tmpf, 0, ck, slot)
    firss = p.sbuf("firss", [128, NCH2, 3], F32)
    firls = p.sbuf("firls", [128, NCH2, 31], F32)
    em.dma("sync", firss.ap, firs.ap)
    em.dma("sync", firls.ap, firl.ap)
    fir(3, firss, 0)
    fir(31, firls, 1)
    for ck in range(NCK):
        to_tm(v_fm, 32, ck, 3)

    em.memset("vector", S_sb.ap, 0.0)
    em.memset("vector", S_bf.ap, 0.0)
    kb_c = p.sbuf("kb_c", [128, NCH2, 128], BF16)
    vb_c = p.sbuf("vb_c", [128, NCH2, 128], BF16)
    up_sb = p.sbuf("up_sb", [128, DK], BF16)
    beta_sb = p.sbuf("beta_sb", [1, TT], F32)
    em.dma("sync", beta_sb.ap, beta_d.ap)

    def _prep(ck):
        i = ck % 2
        t0 = ck * 128
        Y_sb, X_sb, XT_sb = YB[i], XB[i], XTB[i]
        em.mm(TV(ps["P2"].h[:, 0:128], ps["P2"].name), sc["ones_row"].ap,
              TV(beta_sb.h[:, t0:t0 + 128], beta_sb.name))
        em.copy("scalar", bb_sb.ap, TV(ps["P2"].h[:, 0:128], ps["P2"].name))
        for c in range(NCH2):
            em.tt("vector", TV(kb_c.h[:, c, :], kb_c.name),
                  TV(k_fm.h[:, c, t0:t0 + 128], k_fm.name),
                  bb_sb.ap, ALU.mult)
            em.tt("vector", TV(vb_c.h[:, c, :], vb_c.name),
                  TV(v_fm.h[:, c, 32 + t0:32 + t0 + 128], v_fm.name),
                  bb_sb.ap, ALU.mult)
        for c in range(NCH2):
            em.transpose(TV(ps["PT"].h[:, c * 128:(c + 1) * 128], ps["PT"].name),
                         TV(vb_c.h[:, c, :], vb_c.name), sb_ident(sc))
            em.transpose(TV(ps["PT"].h[:, 256 + c * 128:256 + (c + 1) * 128],
                            ps["PT"].name),
                         TV(kb_c.h[:, c, :], kb_c.name), sb_ident(sc))
        em.copy("vector", Y_sb.ap, TV(ps["PT"].h[:, 0:512], ps["PT"].name))
        for c in range(NCH2):
            em.mm(TV(ps["P3"].h[:, 0:128], ps["P3"].name),
                  TV(kb_c.h[:, c, :], kb_c.name),
                  TV(k_fm.h[:, c, t0:t0 + 128], k_fm.name),
                  start=(c == 0), stop=(c == NCH2 - 1))
        em.tt("vector", X_sb.ap, TV(ps["P3"].h[:, 0:128], ps["P3"].name),
              trilms.ap, ALU.mult)
        for c in range(NCH2):
            em.mm(TV(ps["P3"].h[:, 0:128], ps["P3"].name),
                  TV(k_fm.h[:, c, t0:t0 + 128], k_fm.name),
                  TV(kb_c.h[:, c, :], kb_c.name),
                  start=(c == 0), stop=(c == NCH2 - 1))
        em.tt("vector", XT_sb.ap, TV(ps["P3"].h[:, 0:128], ps["P3"].name),
              triums.ap, ALU.mult)
        for it in range(7):
            em.mm(TV(ps["P0"].h[:, :], ps["P0"].name), XT_sb.ap, Y_sb.ap)
            em.tt("vector", Y_sb.ap, Y_sb.ap,
                  TV(ps["P0"].h[:, :], ps["P0"].name), ALU.add)
            if it < 6:
                em.mm(TV(ps["P3"].h[:, 0:128], ps["P3"].name), XT_sb.ap,
                      X_sb.ap)
                em.mm(TV(ps["P1"].h[:, 0:128], ps["P1"].name), X_sb.ap,
                      XT_sb.ap)
                em.copy("vector", X_sb.ap,
                        TV(ps["P3"].h[:, 0:128], ps["P3"].name))
                em.copy("vector", XT_sb.ap,
                        TV(ps["P1"].h[:, 0:128], ps["P1"].name))
        for c in range(NCH2):
            em.transpose(TV(ps["PT2"].h[:, c * 128:(c + 1) * 128],
                            ps["PT2"].name),
                         TV(Y_sb.h[:, DK + c * 128:DK + (c + 1) * 128],
                            Y_sb.name), sb_ident(sc))
            em.transpose(TV(ps["PT2"].h[:, 256 + c * 128:256 + (c + 1) * 128],
                            ps["PT2"].name),
                         TV(k_fm.h[:, c, t0:t0 + 128], k_fm.name),
                         sb_ident(sc))
        em.copy("vector", TV(WF[i].h[:, :, :], WF[i].name),
                TV(ps["PT2"].h[:, 0:256].rearrange("p (c t) -> p c t",
                                                   c=NCH2),
                   ps["PT2"].name))
        em.copy("vector", KT[i].ap, TV(ps["PT2"].h[:, 256:512],
                                       ps["PT2"].name))

    def _tail(ck):
        i = ck % 2
        t0 = ck * 128
        Y_sb, w_fmc, k_tmc = YB[i], WF[i], KT[i]
        for c in range(NCH2):
            em.mm(TV(ps["P2"].h[:, 0:DK], ps["P2"].name),
                  TV(w_fmc.h[:, c, :], w_fmc.name),
                  TV(S_bf.h[:, c, :], S_bf.name),
                  start=(c == 0), stop=(c == NCH2 - 1))
        em.tt("vector", up_sb.ap, TV(Y_sb.h[:, 0:DK], Y_sb.name),
              TV(ps["P2"].h[:, 0:DK], ps["P2"].name), ALU.subtract)
        for c in range(NCH2):
            em.mm(TV(ps["P3"].h[:, 0:128], ps["P3"].name),
                  TV(k_fm.h[:, c, t0:t0 + 128], k_fm.name),
                  TV(q_fm.h[:, c, t0:t0 + 128], q_fm.name),
                  start=(c == 0), stop=(c == NCH2 - 1))
        em.tt("vector", at_sb.ap, TV(ps["P3"].h[:, 0:128], ps["P3"].name),
              triuIs.ap, ALU.mult)
        for c in range(NCH2):
            em.mm(TV(ps["P2"].h[:, 0:DK], ps["P2"].name),
                  TV(q_fm.h[:, c, t0:t0 + 128], q_fm.name),
                  TV(S_bf.h[:, c, :], S_bf.name),
                  start=(c == 0), stop=False)
        em.mm(TV(ps["P2"].h[:, 0:DK], ps["P2"].name), at_sb.ap, up_sb.ap,
              start=False, stop=True)
        em.copy("vector", u_sb.ap, TV(ps["P2"].h[:, 0:DK], ps["P2"].name))
        em.dma("gpsimd", TV(br_tm.h[:, ck, 2, :], br_tm.name), u_sb.ap)
        for c in range(NCH2):
            em.transpose(TV(ps["PT"].h[:, 512 + c * 128:512 + (c + 1) * 128],
                            ps["PT"].name),
                         TV(u_sb.h[:, c * 128:(c + 1) * 128], u_sb.name),
                         sb_ident(sc))
        em.copy("vector", TV(df_fm.h[:, :, :], df_fm.name),
                TV(ps["PT"].h[:, 512:768].rearrange("p (c t) -> p c t",
                                                    c=NCH2),
                   ps["PT"].name))
        em.dma("gpsimd",
               TV(br_fm.h[2].rearrange("(c p) t -> p c t", p=128)
                  [:, :, t0:t0 + 128], br_fm.name),
               TV(df_fm.h[:, :, :], df_fm.name))
        for c in range(NCH2):
            em.mm(TV(ps["P2"].h[:, 0:DK], ps["P2"].name),
                  TV(k_tmc.h[:, c * 128:(c + 1) * 128], k_tmc.name),
                  up_sb.ap)
            em.tt("vector", TV(S_sb.h[:, c, :], S_sb.name),
                  TV(S_sb.h[:, c, :], S_sb.name),
                  TV(ps["P2"].h[:, 0:DK], ps["P2"].name), ALU.add)
            em.copy("scalar", TV(S_bf.h[:, c, :], S_bf.name),
                    TV(S_sb.h[:, c, :], S_sb.name))

    _prep(0)
    for ck in range(NCK):
        if ck + 1 < NCK:
            _prep(ck + 1)
        _tail(ck)
    p.free(wstage, qcs, kcs, vcs, wbs, firss, firls, kb_c, vb_c, up_sb,
           beta_sb)

    g1stage = p.sbuf("g1stage", [128, 512], F32)
    for grp in range(12):
        for cc in range(2):
            em.dma("sync", g1stage.ap,
                   TV(gw1.h[1024 + grp * 256 + cc * 128:
                            1024 + grp * 256 + (cc + 1) * 128, :], gw1.name))
            em.mm(TV(ps["R1"].h[0:1, :], ps["R1"].name), sc["ones_col"].ap,
                  g1stage.ap, start=(cc == 0), stop=(cc == 1))
        em.copy("vector", TV(sc["r_mu"].h[:, :], sc["r_mu"].name),
                TV(ps["R1"].h[0:1, :], ps["R1"].name))
        row = (grp % 3) * 4 + grp // 3
        em.dma("gpsimd", TV(g1r_d.h[row:row + 1, :], g1r_d.name),
               TV(sc["r_mu"].h[:, :], sc["r_mu"].name))
    g1r_sb = p.sbuf("g1r_sb", [12, 512], BF16)
    em.dma("gpsimd", g1r_sb.ap, g1r_d.ap)

    tclip = p.sbuf("tclip", [1, 1], F32)
    em.dma("sync", tclip.ap, temp.ap)
    em.ts("vector", tclip.ap, tclip.ap, 0.2, 10.0, ALU.max, ALU.min)
    em.recip(tclip.ap, tclip.ap)
    invt = p.sbuf("invt", [128, 1], F32)
    em.mm(TV(ps["P2"].h[:, 0:1], ps["P2"].name), sc["ones_row"].ap, tclip.ap)
    em.copy("vector", invt.ap, TV(ps["P2"].h[:, 0:1], ps["P2"].name))
    epsc = p.sbuf("epsc", [1, 4], F32)
    em.dma("sync", epsc.ap, epsf.ap)
    em.ts("vector", epsc.ap, epsc.ap, 1e-7, 0.1, ALU.max, ALU.min)
    flrb = p.sbuf("flrb", [128, 4], F32)
    em.mm(TV(ps["P2"].h[:, 0:4], ps["P2"].name), sc["ones_row"].ap, epsc.ap)
    em.copy("vector", flrb.ap, TV(ps["P2"].h[:, 0:4], ps["P2"].name))
    gb2b = p.sbuf("gb2b", [128, 4], F32)
    gb2s = p.sbuf("gb2s", [1, 4], F32)
    em.dma("sync", gb2s.ap, gb2.ap)
    em.mm(TV(ps["P2"].h[:, 0:4], ps["P2"].name), sc["ones_row"].ap, gb2s.ap)
    em.copy("vector", gb2b.ap, TV(ps["P2"].h[:, 0:4], ps["P2"].name))
    onws = p.sbuf("onws", [1, DK], F32)
    em.dma("sync", onws.ap, onwr.ap)
    onwb = p.sbuf("onwb", [128, DK], F32)
    em.mm(TV(ps["P2"].h[:, 0:DK], ps["P2"].name), sc["ones_row"].ap, onws.ap)
    em.copy("vector", onwb.ap, TV(ps["P2"].h[:, 0:DK], ps["P2"].name))
    gb1s = p.sbuf("gb1s", [128, 4], F32)
    em.dma("sync", gb1s.ap, gb1c.ap)
    gw2s = p.sbuf("gw2s", [128, 4, 4], BF16)
    em.dma("gpsimd", gw2s.ap,
           TV(gw2.h.rearrange("(c p) e -> p c e", p=128), gw2.name))

    GRP = 512
    NG = GRP // 128
    brt_s = p.sbuf("brt_s", [128, NG, 4, DK], BF16)
    stat_s = p.sbuf("stat_s", [128, NG, 12], F32)
    gsq = p.sbuf("gsq", [128, NG * DK], BF16)
    statf = p.sbuf("statf", [12, GRP], BF16)
    stat_bf = p.sbuf("stat_bf", [128, NG, 12], BF16)
    hmid = p.sbuf("hmid", [128, 4, GRP], BF16)
    lg_f = p.sbuf("lg_f", [4, GRP], BF16)
    lgt_s = p.sbuf("lgt_s", [128, NG, 4], F32)
    mxg = p.sbuf("mxg", [128, NG], F32)
    smg = p.sbuf("smg", [128, NG], F32)
    wgt_s = p.sbuf("wgt_s", [128, NG, 4], F32)
    o_tmS = p.sbuf("o_tmS", [128, NG, DK], F32)
    rr = p.sbuf("rr", [128, NG], F32)
    g1w = p.sbuf("g1w", [128, 2, nch, 128], BF16)
    brf_s = p.sbuf("brf_s", [128, 3, NCH2, GRP], BF16)
    orr = o_out.h.rearrange("(c p) t -> p c t", p=128)
    for gi in range(TT // GRP):
        t0 = gi * GRP
        em.dma("sync", TV(brt_s.h[:].rearrange("p g b d -> p (g b) d"),
                          brt_s.name),
               TV(br_tm.h[:, gi * NG:(gi + 1) * NG, :, :]
                  .rearrange("t ck b d -> t (ck b) d"), br_tm.name))
        bv = TV(brt_s.h[:].rearrange("p g b d -> p (g b) d"), brt_s.name)
        em.reduce("vector", TV(stat_s.h[:, :, 0:4], stat_s.name), bv, ALU.add)
        for b_ in range(4):
            bslice = TV(brt_s.h[:, :, b_, :], brt_s.name)
            em.tt("vector", TV(gsq.h[:].rearrange("p (g d) -> p g d", d=DK),
                               gsq.name), bslice, bslice, ALU.mult)
            em.reduce("vector", TV(stat_s.h[:, :, 4 + b_:5 + b_], stat_s.name),
                      TV(gsq.h[:].rearrange("p (g d) -> p g d", d=DK),
                         gsq.name), ALU.add)
        em.reduce("vector", TV(stat_s.h[:, :, 8:12], stat_s.name), bv, ALU.max)
        sv = lambda sl: TV(stat_s.h[:, :, sl], stat_s.name)
        em.ts("vector", sv(slice(0, 4)), sv(slice(0, 4)), 1.0 / DK, None,
              ALU.mult)
        em.ts("vector", sv(slice(4, 8)), sv(slice(4, 8)), 1.0 / DK, None,
              ALU.mult)
        em.ts("vector", sv(slice(4, 8)), sv(slice(4, 8)), 1e-8, None, ALU.max)
        em.act(sv(slice(4, 8)), sv(slice(4, 8)), AF.Sqrt)
        em.copy("scalar", stat_bf.ap, stat_s.ap)
        for g in range(NG):
            em.transpose(TV(ps["PT"].h[0:12, g * 128:(g + 1) * 128],
                            ps["PT"].name),
                         TV(stat_bf.h[:, g, :], stat_bf.name), sb_ident(sc))
        em.copy("vector", statf.ap, TV(ps["PT"].h[0:12, 0:GRP], ps["PT"].name))
        em.dma("sync", xnsub.ap, TV(hr[:, :, t0:t0 + GRP], hx.name))
        em.dma("sync", TV(brf_s.h[:].rearrange("p s c t -> p (s c) t"),
                          brf_s.name),
               TV(br_fm.h.rearrange("s (c p) t -> p (s c) t", p=128)
                  [:, :, t0:t0 + GRP], br_fm.name))
        for m4 in range(4):
            slot = m4 % 2
            em.dma("gpsimd",
                   TV(g1w.h[:, slot].rearrange("p c m -> p (c m)"),
                      f"g1w.{slot}"),
                   TV(gw1h.h[m4], gw1h.name))
            pst = ps["P0"]
            for c in range(nch):
                em.mm(TV(pst.h[:, 0:GRP], pst.name),
                      TV(g1w.h[:, slot, c, :], f"g1w.{slot}"),
                      TV(xnsub.h[:, c, :], xnsub.name),
                      start=(c == 0), stop=False)
            em.mm(TV(pst.h[:, 0:GRP], pst.name),
                  TV(g1r_sb.h[:, m4 * 128:(m4 + 1) * 128], g1r_sb.name),
                  statf.ap, start=False, stop=False)
            em.dma("gpsimd",
                   TV(g1w.h[:, slot].rearrange("p c m -> p (c m)"),
                      f"g1w.{slot}"),
                   TV(gw1b.h[m4], gw1b.name))
            for b in range(4):
                for c in range(NCH2):
                    if b < 3:
                        rhs = TV(brf_s.h[:, b, c, :], brf_s.name)
                    else:
                        rhs = TV(v_fm.h[:, c, 32 + t0:32 + t0 + GRP], v_fm.name)
                    em.mm(TV(pst.h[:, 0:GRP], pst.name),
                          TV(g1w.h[:, slot, b * NCH2 + c, :], f"g1w.{slot}"),
                          rhs, start=False,
                          stop=(b == 3 and c == NCH2 - 1))
            em.act(TV(hmid.h[:, m4, :], hmid.name), pst.ap, AF.Gelu,
                   bias=TV(gb1s.h[:, m4:m4 + 1], gb1s.name))
        pst = ps["P3"]
        for m4 in range(4):
            em.mm(TV(pst.h[0:4, 0:GRP], pst.name),
                  TV(gw2s.h[:, m4, :], gw2s.name),
                  TV(hmid.h[:, m4, :], hmid.name),
                  start=(m4 == 0), stop=(m4 == 3))
        em.copy("vector", lg_f.ap, TV(pst.h[0:4, 0:GRP], pst.name))
        for g in range(NG):
            em.transpose(TV(ps["PT"].h[0:128, g * 4:(g + 1) * 4], ps["PT"].name),
                         TV(lg_f.h[:, g * 128:(g + 1) * 128], lg_f.name),
                         TV(sc["ident_bf"].h[0:4, 0:4], sc["ident_bf"].name))
        em.copy("vector", TV(lgt_s.h[:, :, :], lgt_s.name),
                TV(ps["PT"].h[0:128, 0:NG * 4], ps["PT"].name))
        em.tt("vector", lgt_s.ap, lgt_s.ap,
              TV(_expand_mid(gb2b.h[:], NG), gb2b.name), ALU.add)
        em.ts("vector", lgt_s.ap, lgt_s.ap, invt.ap, None, ALU.mult)
        em.reduce("vector", mxg.ap, lgt_s.ap, ALU.max)
        em.tt("vector", lgt_s.ap, lgt_s.ap, bcast(mxg.ap, 4), ALU.subtract)
        em.act(TV(wgt_s.h[:, :, :], wgt_s.name), lgt_s.ap, AF.Exp)
        em.reduce("vector", smg.ap, wgt_s.ap, ALU.add)
        em.recip(smg.ap, smg.ap)
        em.tt("vector", wgt_s.ap, wgt_s.ap, bcast(smg.ap, 4), ALU.mult)
        em.tt("vector", wgt_s.ap, wgt_s.ap,
              TV(_expand_mid(flrb.h[:], NG), flrb.name), ALU.max)
        em.reduce("vector", smg.ap, wgt_s.ap, ALU.add)
        em.recip(smg.ap, smg.ap)
        em.tt("vector", wgt_s.ap, wgt_s.ap, bcast(smg.ap, 4), ALU.mult)
        for g in range(NG):
            for b in range(4):
                wcol = TV(wgt_s.h[:, g, b:b + 1], wgt_s.name)
                if b == 0:
                    em.ts("vector", TV(o_tmS.h[:, g, :], o_tmS.name),
                          TV(brt_s.h[:, g, 0, :], brt_s.name), wcol, None,
                          ALU.mult)
                else:
                    em.stt("vector", TV(o_tmS.h[:, g, :], o_tmS.name),
                           TV(brt_s.h[:, g, b, :], brt_s.name), wcol,
                           TV(o_tmS.h[:, g, :], o_tmS.name), ALU.mult, ALU.add)
        em.tt("vector", TV(gsq.h[:, :], gsq.name),
              TV(o_tmS.h[:].rearrange("p g d -> p (g d)"), o_tmS.name),
              TV(o_tmS.h[:].rearrange("p g d -> p (g d)"), o_tmS.name),
              ALU.mult)
        em.reduce("vector", rr.ap,
                  TV(gsq.h[:].rearrange("p (g d) -> p g d", d=DK),
                     gsq.name), ALU.add)
        em.ts("vector", rr.ap, rr.ap, 1.0 / DK, 1e-5, ALU.mult, ALU.add)
        em.act(rr.ap, rr.ap, AF.Sqrt)
        em.recip(rr.ap, rr.ap)
        for g in range(NG):
            em.ts("vector", TV(o_tmS.h[:, g, :], o_tmS.name),
                  TV(o_tmS.h[:, g, :], o_tmS.name),
                  TV(rr.h[:, g:g + 1], rr.name), None, ALU.mult)
        em.tt("vector", TV(o_tmS.h[:].rearrange("p g d -> p (g d)"),
                           o_tmS.name),
              TV(o_tmS.h[:].rearrange("p g d -> p (g d)"), o_tmS.name),
              TV(_expand_mid(onwb.h[:], NG), onwb.name), ALU.mult)
        for g in range(NG):
            em.copy("vector", TV(u_sb.h[:, :], u_sb.name),
                    TV(o_tmS.h[:, g, :], o_tmS.name))
            for c in range(NCH2):
                em.transpose(TV(ps["PT"].h[:, c * 128:(c + 1) * 128],
                                ps["PT"].name),
                             TV(u_sb.h[:, c * 128:(c + 1) * 128], u_sb.name),
                             sb_ident(sc))
            em.copy("vector", TV(df_fm.h[:, :, :], df_fm.name),
                    TV(ps["PT"].h[:, 0:256].rearrange("p (c t) -> p c t",
                                                      c=NCH2), ps["PT"].name))
            em.dma("gpsimd",
                   TV(orr[:, :, t0 + g * 128:t0 + (g + 1) * 128], o_out.name),
                   TV(df_fm.h[:, :, :], df_fm.name))
    em.fence("sync")
    if pp is not None:
        p.free_since(mark)
    return p


def _expand_mid(ap, n):
    from concourse.ap import AP as _AP
    return _AP(ap.tensor, ap.offset,
               [list(ap.ap[0]), [0, n], list(ap.ap[1])])


def build_wo_residual(TOK=TPC, nch=NCH, pp=None, io=None):
    if pp is None:
        p = Prog("wores")
        em = p.em
        x = p.din("x_fm", [nch * 128, TOK])
        o = p.din("o_fm", [nch * 128, TOK], BF16)
        wo = p.din("wo", [nch * 128, nch * 128])
        xo = p.dout("xo_fm", [nch * 128, TOK])
        ps = alloc_psum_pool(p)
    else:
        p = pp
        em = p.em
        x, o, wo, xo = io["x"], io["o"], io["wo"], io["xo"]
        ps = io["ps"]
    mark = p.mark()
    o_sb = p.sbuf("o_sbW", [128, nch, TOK], BF16)
    em.dma("gpsimd", o_sb.ap,
           TV(o.h.rearrange("(c p) t -> p c t", p=128), o.name))
    wbuf = p.sbuf("wbufW", [128, 2, nch, 128], BF16)
    xres = p.sbuf("xresW", [128, 512], F32)
    xout = p.sbuf("xoutW", [128, 512], F32)
    xr = x.h.rearrange("(c p) t -> p c t", p=128)
    orr = xo.h.rearrange("(c p) t -> p c t", p=128)
    for m in range(nch):
        em.dma("gpsimd", TV(wbuf.h[:, m % 2], f"wbufW.{m % 2}"),
               TV(wo.h.rearrange("(c p) m -> p c m", p=128)
                  [:, :, m * 128:(m + 1) * 128], wo.name))
        for s0 in range(0, TOK, 512):
            pst = ps["P0" if (s0 // 512) % 2 == 0 else "P1"]
            for c in range(nch):
                em.mm(pst.ap, TV(wbuf.h[:, m % 2, c, :], f"wbufW.{m % 2}"),
                      TV(o_sb.h[:, c, s0:s0 + 512], o_sb.name),
                      start=(c == 0), stop=(c == nch - 1))
            em.dma("sync", xres.ap, TV(xr[:, m, s0:s0 + 512], x.name))
            em.tt("vector", xout.ap, pst.ap, xres.ap, ALU.add)
            em.dma("gpsimd", TV(orr[:, m, s0:s0 + 512], xo.name), xout.ap)
    em.fence("sync")
    if pp is not None:
        p.free_since(mark)
    return p


def _fmcols(v):
    n = v.shape[0] // 128
    return np.ascontiguousarray(v.reshape(n, 128, -1).transpose(1, 0, 2))


def host_l1_attn_core_inputs(lnx_b, b, h, Wq, Wk, Wv, Wb, qc, kc, vc,
                             fs, fl, gW1, gb1, gW2, gb2, temp, epsf, onw, DK=256):
    sl = slice(h * DK, (h + 1) * DK)
    tril_m = (-np.tril(np.ones((128, 128), np.float32), -1)).astype(_bf16())
    triu_m = (-np.triu(np.ones((128, 128), np.float32), 1)).astype(_bf16())
    triu_i = np.triu(np.ones((128, 128), np.float32), 0).astype(_bf16())
    ins = {
        "hx": lnx_b,
        "wq": np.ascontiguousarray(Wq[:, sl]),
        "wk": np.ascontiguousarray(Wk[:, sl]),
        "wv": np.ascontiguousarray(Wv[:, sl]),
        "wb": np.ascontiguousarray(Wb[:, h].reshape(8, 128).T),
        "qcw": _fmcols(qc[sl]), "kcw": _fmcols(kc[sl]), "vcw": _fmcols(vc[sl]),
        "firs": _fmcols(fs[h].reshape(DK, 3)),
        "firl": _fmcols(fl[h].reshape(DK, 31)),
        "gw1": gW1,
        "gw1h": tile_w(gW1[0:1024]),
        "gw1b": tile_w(gW1[4096:5120]),
        "gb1c": np.ascontiguousarray(gb1.reshape(4, 128).T),
        "gw2": gW2, "gb2": gb2.reshape(1, 4),
        "temp": np.array([[temp[h]]], np.float32),
        "epsf": epsf[h].reshape(1, 4),
        "onwr": onw.reshape(1, DK),
        "cst_trilm": tril_m, "cst_trium": triu_m, "cst_triuI": triu_i,
    }
    ins.update(host_common_inputs())
    return ins


def run_phase_l1_attn(x_np, Wq, Wk, Wv, Wb, qc, kc, vc, fs, fl,
                      gW1, gb1, gW2, gb2, temp, epsf, onw, Wo):
    pl, ncl = _get_prog("lnbf", build_ln_bf16)
    pa, nca = _get_prog("l1attn", build_l1_attn)
    pw, ncw = _get_prog("wores", build_wo_residual)
    com = host_common_inputs()
    in_maps = []
    for b, t0 in _core_ranges():
        m = {"x_fm": np.ascontiguousarray(x_np[b, t0:t0 + TPC].T)}
        m.update(com)
        in_maps.append(m)
    res = run_bass_kernel_spmd(ncl, in_maps, core_ids=list(range(N_CORES)))
    lnx = [np.empty((D, T), res.results[0]["lnx"].dtype) for _ in range(B)]
    for c, (b, t0) in enumerate(_core_ranges()):
        lnx[b][:, t0:t0 + TPC] = res.results[c]["lnx"]
    in_maps = []
    for core in range(N_CORES):
        b, h = core // 4, core % 4
        in_maps.append(host_l1_attn_core_inputs(
            lnx[b], b, h, Wq, Wk, Wv, Wb, qc, kc, vc, fs, fl,
            gW1, gb1, gW2, gb2, temp, epsf, onw))
    res = run_bass_kernel_spmd(nca, in_maps, core_ids=list(range(N_CORES)))
    o_all = [np.concatenate([res.results[b * 4 + h]["o_h"] for h in range(4)],
                            axis=0) for b in range(B)]
    in_maps = []
    for b, t0 in _core_ranges():
        in_maps.append({"x_fm": np.ascontiguousarray(x_np[b, t0:t0 + TPC].T),
                        "o_fm": np.ascontiguousarray(o_all[b][:, t0:t0 + TPC]),
                        "wo": Wo})
    res = run_bass_kernel_spmd(ncw, in_maps, core_ids=list(range(N_CORES)))
    out = np.empty_like(x_np)
    for c, (b, t0) in enumerate(_core_ranges()):
        out[b, t0:t0 + TPC] = res.results[c]["xo_fm"].T
    return out



def kernel(L0_attn_x, L0_attn_w_qkv, L0_attn_w_out,
           L0_ffn_rw1, L0_ffn_rb1, L0_ffn_rw2, L0_ffn_rb2,
           L0_ffn_ew1, L0_ffn_eb1, L0_ffn_ew2, L0_ffn_eb2,
           L1_attn_Wq, L1_attn_Wk, L1_attn_Wv, L1_attn_Wb,
           L1_attn_qc_w, L1_attn_kc_w, L1_attn_vc_w,
           L1_attn_fir_s, L1_attn_fir_l,
           L1_attn_gW1, L1_attn_gb1, L1_attn_gW2, L1_attn_gb2,
           L1_attn_temp, L1_attn_eps_floor, L1_attn_onorm_w, L1_attn_Wo,
           L1_ffn_Wqkv, L1_ffn_q_scale, L1_ffn_k_scale, L1_ffn_Wo,
           L1_ffn_ln_g, L1_ffn_ln_b, L1_ffn_W1, L1_ffn_b1,
           L1_ffn_W2, L1_ffn_b2):
    _I = dict(L0_attn_x=L0_attn_x, L0_attn_w_qkv=L0_attn_w_qkv,
              L0_attn_w_out=L0_attn_w_out, L0_ffn_rw1=L0_ffn_rw1,
              L0_ffn_rb1=L0_ffn_rb1, L0_ffn_rw2=L0_ffn_rw2,
              L0_ffn_rb2=L0_ffn_rb2, L0_ffn_ew1=L0_ffn_ew1,
              L0_ffn_eb1=L0_ffn_eb1, L0_ffn_ew2=L0_ffn_ew2,
              L0_ffn_eb2=L0_ffn_eb2, L1_attn_Wq=L1_attn_Wq,
              L1_attn_Wk=L1_attn_Wk, L1_attn_Wv=L1_attn_Wv,
              L1_attn_Wb=L1_attn_Wb, L1_attn_qc_w=L1_attn_qc_w,
              L1_attn_kc_w=L1_attn_kc_w, L1_attn_vc_w=L1_attn_vc_w,
              L1_attn_fir_s=L1_attn_fir_s, L1_attn_fir_l=L1_attn_fir_l,
              L1_attn_gW1=L1_attn_gW1, L1_attn_gb1=L1_attn_gb1,
              L1_attn_gW2=L1_attn_gW2, L1_attn_gb2=L1_attn_gb2,
              L1_attn_temp=L1_attn_temp, L1_attn_eps_floor=L1_attn_eps_floor,
              L1_attn_onorm_w=L1_attn_onorm_w, L1_attn_Wo=L1_attn_Wo,
              L1_ffn_Wqkv=L1_ffn_Wqkv, L1_ffn_q_scale=L1_ffn_q_scale,
              L1_ffn_k_scale=L1_ffn_k_scale, L1_ffn_Wo=L1_ffn_Wo,
              L1_ffn_ln_g=L1_ffn_ln_g, L1_ffn_ln_b=L1_ffn_ln_b,
              L1_ffn_W1=L1_ffn_W1, L1_ffn_b1=L1_ffn_b1, L1_ffn_W2=L1_ffn_W2,
              L1_ffn_b2=L1_ffn_b2)
    _I = {k: np.asarray(v) for k, v in _I.items()}
    import os
    if os.environ.get("KERNEL_UNFUSED", "0") != "1":
        return kernel_fused(_I)
    x = np.ascontiguousarray(np.asarray(L0_attn_x, dtype=np.float32))

    def l0ffn(x):
        return run_phase_l0_ffn(x, L0_ffn_rw1, L0_ffn_rb1, L0_ffn_rw2,
                                L0_ffn_rb2, L0_ffn_ew1, L0_ffn_eb1,
                                L0_ffn_ew2, L0_ffn_eb2)

    def l1ffn(x):
        return run_phase_l1_ffn(x, L1_ffn_Wqkv, L1_ffn_q_scale,
                                L1_ffn_k_scale, L1_ffn_Wo, L1_ffn_ln_g,
                                L1_ffn_ln_b, L1_ffn_W1, L1_ffn_b1,
                                L1_ffn_W2, L1_ffn_b2)

    x = l0ffn(x)
    x = run_phase_l0_attn(x, L0_attn_w_qkv, L0_attn_w_out)
    x = l0ffn(x)
    x = l1ffn(x)
    x = run_phase_l1_attn(x, L1_attn_Wq, L1_attn_Wk, L1_attn_Wv, L1_attn_Wb,
                          L1_attn_qc_w, L1_attn_kc_w, L1_attn_vc_w,
                          L1_attn_fir_s, L1_attn_fir_l,
                          L1_attn_gW1, L1_attn_gb1, L1_attn_gW2, L1_attn_gb2,
                          L1_attn_temp, L1_attn_eps_floor, L1_attn_onorm_w,
                          L1_attn_Wo)
    x = l1ffn(x)
    return x



RG8 = [list(range(8))]
RG4 = [[0, 1, 2, 3], [4, 5, 6, 7]]


def emit_axpy(p, em, x_d, y_d, alpha, TOK=TPC, nch=NCH, ydtype=BF16):
    mark = p.mark()
    xa = p.sbuf("ax_xa", [128, 512], F32)
    ya = p.sbuf("ax_ya", [128, 512], ydtype)
    oa = p.sbuf("ax_oa", [128, 512], F32)
    xr = x_d.h.rearrange("(c p) t -> p c t", p=128)
    yr = y_d.h.rearrange("(c p) t -> p c t", p=128)
    for c in range(nch):
        for s0 in range(0, TOK, 512):
            em.dma("sync", xa.ap, TV(xr[:, c, s0:s0 + 512], x_d.name))
            em.dma("sync", ya.ap, TV(yr[:, c, s0:s0 + 512], y_d.name))
            em.stt("vector", oa.ap, ya.ap, alpha, xa.ap, ALU.mult, ALU.add)
            em.dma("gpsimd", TV(xr[:, c, s0:s0 + 512], x_d.name), oa.ap)
    p.free_since(mark)


def emit_halo_exchange(p, em, src_d, W, dtype, idx_prev, halo_on, tag,
                       src_cast=None):
    mark = p.mark()
    tail_d = p.dram("tail_h", [1024, W], dtype)
    ag_d = p.dram("agt_h", [8 * 1024, W], dtype)
    halo_d = p.dram("halo_h", [1024, W], dtype)
    if src_cast is None:
        src_cast = dtype
    em.dma("gpsimd", tail_d.ap,
           TV(src_d.h[:, TPC - W:TPC], src_d.name))
    em.collective("AllGather", ALU.bypass, RG8, [tail_d.ap], [ag_d.ap])
    stage = p.sbuf("hstage", [128, W], dtype)
    for k in range(8):
        em.indirect(stage.ap, None, ag_d.ap,
                    TV(idx_prev.h[:, k:k + 1], idx_prev.name))
        em.ts("vector", stage.ap, stage.ap,
              TV(halo_on.h[:, 0:1], halo_on.name), None, ALU.mult)
        em.dma("gpsimd", TV(halo_d.h[k * 128:(k + 1) * 128, :], halo_d.name),
               stage.ap)
    p.free_since(mark)
    return halo_d


def build_fused(nch=NCH):
    dm = nch * 128
    p = Prog("fused")
    em = p.em
    X = p.din("x_in", [dm, TPC])
    xo_out = p.dout("x_out", [dm, TPC])
    rw1 = p.din("rw1", [32, 128, dm])
    rb1c = p.din("rb1c", [128, 32])
    rw2 = p.din("rw2", [4096, 8])
    rb2 = p.din("rb2", [1, 8])
    ew1 = p.din("ew1", [32, 128, dm])
    eb1c = p.din("eb1c", [128, 32])
    ew2x = p.din("ew2x", [8, 128, 4096])
    eb2r = p.din("eb2r", [1, dm])
    esel = p.din("esel", [1, 8])
    ones512 = p.din("cst_ones512", [1, 512])
    tril_s = p.din("cst_trilstrict", [128, 128])
    l0_wqkv = p.din("l0_wqkv", [16, 128, dm])
    l0_vw = p.din("l0_vw", [128, 8, dm])
    l0_wout = p.din("l0_wout", [8, 128, dm])
    l0_ropeA = p.din("l0_ropeA", [2, 128, 512 + TPC])
    l0_ropeB = p.din("l0_ropeB", [2, 128, 512 + TPC])
    l0_maskT = p.din("l0_maskT", [2, 8, 128, 640], BF16)
    w6 = {}
    w6["wqkv"] = p.din("l1_wqkv6", [6, 16, 128, dm])
    w6["vw_in"] = p.din("l1_vw6", [6, 128, 8, dm])
    w6["wo"] = p.din("l1_wo6", [6, 8, 128, dm])
    for nm in ("qscol", "qspcol", "kscol", "kspcol"):
        w6[nm] = p.din("l1_" + nm + "6", [6, 128, 1])
    w6["ln_g"] = p.din("l1_lng6", [6, 128, nch])
    w6["ln_b"] = p.din("l1_lnb6", [6, 128, nch])
    w6["w1"] = p.din("l1_w16", [6, 44, 128, dm])
    w6["b1c"] = p.din("l1_b1c6", [6, 128, 44])
    w6["w2"] = p.din("l1_w26", [6, 8, 128, 2816])
    w6["b2c"] = p.din("l1_b2c6", [6, 128, nch])
    l1_ropes = {nm: p.din("l1_rope" + nm, [128, 256 + 128 + TPC])
                for nm in ("Aq", "Bq", "Ak", "Bk")}
    l1_maskT = p.din("l1_maskT", [TPC // 128 + 1, 128, 384], BF16)
    cst_blk2 = p.din("cst_blk2", [128, 2])
    cst_e2 = p.din("cst_e2", [2, 128])
    A = {}
    A["wq"] = p.din("a_wq", [dm, 256])
    A["wk"] = p.din("a_wk", [dm, 256])
    A["wv"] = p.din("a_wv", [dm, 256])
    A["wb"] = p.din("a_wb", [128, nch])
    for nm in ("qcw", "kcw", "vcw"):
        A[nm] = p.din("a_" + nm, [128, 2, 4])
    A["firs"] = p.din("a_firs", [128, 2, 3])
    A["firl"] = p.din("a_firl", [128, 2, 31])
    A["gw1"] = p.din("a_gw1", [5120, 512])
    A["gw1h"] = p.din("a_gw1h", [4, 128, 1024])
    A["gw1b"] = p.din("a_gw1b", [4, 128, 1024])
    A["gb1c"] = p.din("a_gb1c", [128, 4])
    A["gw2"] = p.din("a_gw2", [512, 4])
    A["gb2"] = p.din("a_gb2", [1, 4])
    A["temp"] = p.din("a_temp", [1, 1])
    A["epsf"] = p.din("a_epsf", [1, 4])
    A["onwr"] = p.din("a_onwr", [1, 256])
    A["trilm"] = p.din("cst_trilm", [128, 128], BF16)
    A["trium"] = p.din("cst_trium", [128, 128], BF16)
    A["triuI"] = p.din("cst_triuI", [128, 128], BF16)
    a_wo_slice = p.din("a_wo_slice", [256, dm])
    l1a_wo_full_unused = None
    idx_prev = p.din("idx_prev", [128, 8], I32)
    halo_on = p.din("halo_on", [128, 1])
    tokid_in = p.din("cst_tokid", [128, 128], I32)
    iota8_in = p.din("cst_iota8", [1, 8])
    identf_in = p.din("cst_identf", [128, 128])
    dumpc_in = p.din("cst_dumpc", [128, 256])
    cons_idx = p.din("cons_idx", [16, 4], I32)

    ps = alloc_psum_pool(p)
    sc = alloc_common(p, nch)
    idxp_sb = p.sbuf("idxp_sb", [128, 8], I32)
    em.dma("sync", idxp_sb.ap, idx_prev.ap)
    hon_sb = p.sbuf("hon_sb", [128, 1], F32)
    em.dma("sync", hon_sb.ap, halo_on.ap)
    idf_sb = p.sbuf("idf_sb", [128, 128], F32)
    em.dma("sync", idf_sb.ap, identf_in.ap)
    ci_sb = p.sbuf("ci_sb", [16, 4], I32)
    em.dma("sync", ci_sb.ap, cons_idx.ap)

    x_d = p.dram("x_d", [dm, TPC])
    em.dma("sync", x_d.ap, X.ap)
    logits_d = p.dram("logits_d", [TPC, 8])
    logits_ag = p.dram("logits_ag", [8 * TPC, 8])
    lnx_tm_d = p.dram("lnx_tm_d", [TPC, dm], BF16)
    lnx_tm_ag = p.dram("lnx_tm_ag", [8 * TPC, dm], BF16)
    y_tm_d = p.dram("y_tm_d", [4096, dm], BF16)
    y_tm_ag = p.dram("y_tm_ag", [8 * 4096, dm], BF16)
    ymine_tm_d = p.dram("ymine_tm_d", [TPC, dm], BF16)
    dense_l0f = os.environ.get("L0F_DENSE", "0") == "1"
    if dense_l0f:
        lnx_d = p.dram("lnx_d", [dm, TPC], BF16)
        lnx_ag = p.dram("lnx_ag", [8, dm, TPC], BF16)
        y_blocks = p.dram("y_blocks", [8, dm, TPC], BF16)
        ymine_d = p.dram("ymine_d", [dm, TPC], BF16)

    def l0_ffn_phase():
        em.fence("gpsimd", streams=[em._qname(y_tm_ag.name),
                                    em._qname(lnx_tm_ag.name),
                                    "q|cn_pa"])
        if dense_l0f:
            build_l0f_router(pp=p, io=dict(
                x=x_d, rw1=rw1, rb1c=rb1c, rw2=rw2, rb2=rb2,
                lnx=lnx_d, logits=logits_d, ps=ps, sc=sc))
            em.collective("AllGather", ALU.bypass, RG8, [lnx_d.ap],
                          [TV(lnx_ag.h.rearrange("s d t -> (s d) t"),
                              lnx_ag.name)])
            em.collective("AllGather", ALU.bypass, RG8, [logits_d.ap],
                          [logits_ag.ap])
            build_l0f_expert(pp=p, io=dict(
                logits_all=logits_ag, ew1=ew1, eb1c=eb1c, ew2x=ew2x,
                eb2r=eb2r, esel=esel, ones512=ones512, lnx_blocks=lnx_ag,
                y_blocks=y_blocks, trils=tril_s, ps=ps, sc=sc))
            em.collective("ReduceScatter", ALU.add, RG8,
                          [TV(y_blocks.h.rearrange("s d t -> (s d) t"),
                              y_blocks.name)],
                          [ymine_d.ap])
            emit_axpy(p, em, x_d, ymine_d, 0.5)
            return
        build_l0f_router(pp=p, io=dict(
            x=x_d, rw1=rw1, rb1c=rb1c, rw2=rw2, rb2=rb2,
            lnx=None, lnx_tm=lnx_tm_d, logits=logits_d, ps=ps, sc=sc))
        em.fence("gpsimd", streams=[em._qname(lnx_tm_d.name)])
        em.collective("AllGather", ALU.bypass, RG8, [lnx_tm_d.ap],
                      [lnx_tm_ag.ap])
        em.collective("AllGather", ALU.bypass, RG8, [logits_d.ap],
                      [logits_ag.ap])
        cons_d = emit_l0f_sparse(p, em, ps, sc, io=dict(
            logits_all=logits_ag, ew1=ew1, eb1c=eb1c, ew2x=ew2x, eb2r=eb2r,
            esel=esel, ones512=ones512, trils=tril_s, tokid=tokid_in,
            iota8=iota8_in, lnx_tm_ag=lnx_tm_ag, y_tm=y_tm_d,
            dumpc=dumpc_in))
        em.fence("gpsimd", streams=[em._qname(y_tm_d.name)])
        em.collective("AllGather", ALU.bypass, RG8, [y_tm_d.ap],
                      [y_tm_ag.ap])
        emit_l0f_consume(p, em, ps, sc, io=dict(
            cons_d=cons_d, y_tm_ag=y_tm_ag, x=x_d, idf_sb=idf_sb,
            cons_idx=ci_sb, ymine_tm=ymine_tm_d))

    l0_ffn_phase()
    halo = emit_halo_exchange(p, em, x_d, 512, F32, idxp_sb, hon_sb, "l0a")
    xfull_d = p.dram("xfull", [dm, 512 + TPC])
    em.dma("sync", TV(xfull_d.h[:, 0:512], xfull_d.name), halo.ap)
    em.dma("sync", TV(xfull_d.h[:, 512:512 + TPC], xfull_d.name), x_d.ap)
    build_l0_attn(pp=p, io=dict(
        x=xfull_d, wqkv=l0_wqkv, vw_in=l0_vw, wout=l0_wout, ropeA=l0_ropeA,
        ropeB=l0_ropeB, maskT=l0_maskT, xo=x_d, ps=ps, sc=sc))
    l0_ffn_phase()

    h_d = p.dram("h_d", [dm, TPC])
    hfull_d = p.dram("hfull", [dm, 512 + TPC])

    def l1_ffn_phase():
        build_l1_prep(pp=p, io=dict(x=x_d, h0=h_d, ps=ps, sc=sc))
        for i in range(6):
            hb = emit_halo_exchange(p, em, h_d, 512, BF16, idxp_sb, hon_sb,
                                    f"l1f{i}")
            em.dma("gpsimd", TV(hfull_d.h[:, 0:512], hfull_d.name), hb.ap)
            em.dma("sync", TV(hfull_d.h[:, 512:512 + TPC], hfull_d.name),
                   h_d.ap)
            io = dict(h_in=hfull_d, h_out=h_d, ps=ps, sc=sc,
                      cst_blk2=cst_blk2, cst_e2=cst_e2,
                      maskT=l1_maskT)
            for nm in ("wqkv", "vw_in", "wo", "qscol", "qspcol", "kscol",
                       "kspcol", "ln_g", "ln_b", "w1", "b1c", "w2", "b2c"):
                io[nm] = Tile(w6[nm].h[i], w6[nm].name)
            for nm in ("Aq", "Bq", "Ak", "Bk"):
                io["rope" + nm] = l1_ropes[nm]
            build_l1_layer(pp=p, io=io)
        emit_axpy(p, em, x_d, h_d, 0.5, ydtype=F32)

    l1_ffn_phase()
    lnx2_d = p.dram("lnx2_d", [dm, TPC], BF16)
    build_ln_bf16(pp=p, io=dict(x=x_d, lnx=lnx2_d, ps=ps, sc=sc))
    hxag_d = p.dram("hxag", [4, dm, TPC], BF16)
    em.collective("AllGather", ALU.bypass, RG4, [lnx2_d.ap],
                  [TV(hxag_d.h.rearrange("s d t -> (s d) t"), hxag_d.name)])
    hx_d = p.dram("hx_d", [dm, T], BF16)
    for s in range(4):
        em.dma("sync", TV(hx_d.h[:, s * TPC:(s + 1) * TPC], hx_d.name),
               TV(hxag_d.h[s], hxag_d.name))
    o_d = p.dram("o_d", [256, T], BF16)
    build_l1_attn(pp=p, io=dict(
        hx=hx_d, wq=A["wq"], wk=A["wk"], wv=A["wv"], wb=A["wb"],
        qcw=A["qcw"], kcw=A["kcw"], vcw=A["vcw"], firs=A["firs"],
        firl=A["firl"], gw1=A["gw1"], gw1h=A["gw1h"], gw1b=A["gw1b"],
        gb1c=A["gb1c"], gw2=A["gw2"],
        gb2=A["gb2"], temp=A["temp"], epsf=A["epsf"], onwr=A["onwr"],
        trilm=A["trilm"], trium=A["trium"], triuI=A["triuI"], o_out=o_d,
        ps=ps, sc=sc))
    part_d = p.dram("part_d", [T, dm], BF16)
    mark = p.mark()
    o_sb = p.sbuf("wp_o", [128, 2, T], BF16)
    wo_sb = p.sbuf("wp_w", [128, 2, dm], BF16)
    pout = p.sbuf("wp_out", [128, 512], BF16)
    em.dma("gpsimd", o_sb.ap,
           TV(o_d.h.rearrange("(c p) t -> p c t", p=128), o_d.name))
    em.dma("gpsimd", wo_sb.ap,
           TV(a_wo_slice.h.rearrange("(c p) m -> p c m", p=128),
              a_wo_slice.name))
    for tt_ in range(T // 128):
        for n2 in range(2):
            pst = ps["P0" if n2 == 0 else "P1"]
            for c in range(2):
                em.mm(pst.ap,
                      TV(o_sb.h[:, c, tt_ * 128:(tt_ + 1) * 128], o_sb.name),
                      TV(wo_sb.h[:, c, n2 * 512:(n2 + 1) * 512], wo_sb.name),
                      start=(c == 0), stop=(c == 1))
            em.copy("vector", pout.ap, pst.ap)
            em.dma("gpsimd",
                   TV(part_d.h[tt_ * 128:(tt_ + 1) * 128,
                               n2 * 512:(n2 + 1) * 512], part_d.name),
                   pout.ap)
    p.free_since(mark)
    pmine_d = p.dram("pmine_d", [TPC, dm], BF16)
    em.collective("ReduceScatter", ALU.add, RG4, [part_d.ap], [pmine_d.ap])
    mark = p.mark()
    xa = p.sbuf("wr_xa", [128, 512], F32)
    pa = p.sbuf("wr_pa", [128, 512], BF16)
    for m in range(nch):
        for s0 in range(0, TPC, 512):
            em.dma("sync", xa.ap,
                   TV(x_d.h[m * 128:(m + 1) * 128, s0:s0 + 512], x_d.name))
            em.op("sync",
                  (lambda o, i: lambda e: e.dma_start_transpose(o, i))(
                      pa.h[:, :],
                      pmine_d.h[s0:s0 + 512, m * 128:(m + 1) * 128]),
                  reads=[TV(None, pmine_d.name)], writes=[TV(None, pa.name)],
                  dma_queue=em._qname(pa.name))
            em.tt("vector", xa.ap, xa.ap, pa.ap, ALU.add)
            em.dma("gpsimd",
                   TV(x_d.h[m * 128:(m + 1) * 128, s0:s0 + 512], x_d.name),
                   xa.ap)
    p.free_since(mark)
    l1_ffn_phase()
    em.dma("gpsimd", xo_out.ap, x_d.ap)
    em.fence("sync")
    em.fence("gpsimd")
    return p


_FUSED_SHARED = {}


def host_fused_core_inputs(core, I):
    b, t0 = core // 4, (core % 4) * TPC
    S = _FUSED_SHARED
    x_np = np.asarray(I["L0_attn_x"], dtype=np.float32)
    dm = D
    e = core
    h = core % 4
    ins = {}
    ins["x_in"] = np.ascontiguousarray(x_np[b, t0:t0 + TPC].T)
    ins["rw1"] = tile_w(I["L0_ffn_rw1"])
    ins["rb1c"] = np.ascontiguousarray(I["L0_ffn_rb1"].reshape(-1, 128).T)
    ins["rw2"] = I["L0_ffn_rw2"]
    ins["rb2"] = I["L0_ffn_rb2"].reshape(1, 8)
    ins["ew1"] = tile_w(I["L0_ffn_ew1"][e])
    ins["eb1c"] = np.ascontiguousarray(I["L0_ffn_eb1"][e].reshape(-1, 128).T)
    H_ = 4096
    ins["ew2x"] = np.ascontiguousarray(
        I["L0_ffn_ew2"][e].reshape(H_ // 128, 128, 8, 128)
        .transpose(2, 1, 0, 3).reshape(8, 128, H_))
    ins["eb2r"] = I["L0_ffn_eb2"][e].reshape(1, -1)
    esel = np.zeros((1, 8), np.float32)
    esel[0, e] = 1.0
    ins["esel"] = esel
    ins.update(host_l0f_common())
    if "sparse_cst" not in S:
        S["sparse_cst"] = {
            "cst_tokid": np.ascontiguousarray(
                np.arange(128, dtype=np.int32)[:, None] * 128
                + np.arange(128, dtype=np.int32)[None, :]),
            "cst_iota8": np.arange(8, dtype=np.float32).reshape(1, 8),
            "cst_identf": np.eye(128, dtype=np.float32),
            "cst_dumpc": (4096.0
                          + np.arange(256, dtype=np.float32)[None, :] * 128
                          + np.arange(128, dtype=np.float32)[:, None]),
        }
    ins.update(S["sparse_cst"])
    ins["cons_idx"] = np.ascontiguousarray(
        core * 16 + np.arange(16, dtype=np.int32)[:, None]
        + 128 * np.arange(4, dtype=np.int32)[None, :])
    lk = ("l0a", t0)
    if lk not in S:
        l0a = host_l0_attn_core_inputs(
            np.zeros((512 + TPC, D), np.float32), I["L0_attn_w_qkv"],
            I["L0_attn_w_out"], 512, T_half=1024)
        pos = np.maximum(np.arange(t0 - 512, t0 + TPC), 0)
        A1, B1 = host_l0_rope_tables(pos)
        l0a["ropeA"] = np.stack([A1, A1 * 0.125], 0)
        l0a["ropeB"] = np.stack([B1, B1 * 0.125], 0)
        m0 = host_window_maskT(t0, 8, 640, 512)
        m1 = host_window_maskT(t0 + 1024, 8, 640, 512)
        l0a["maskT"] = np.stack([m0, m1], 0).astype(_bf16())
        S[lk] = l0a
        S.setdefault("l0_w", (S[lk]["wqkv"], S[lk]["wout"]))
    ins["l0_wqkv"], ins["l0_wout"] = S["l0_w"]
    ins["l0_vw"] = S[lk]["vw_in"]
    ins["l0_ropeA"] = S[lk]["ropeA"]
    ins["l0_ropeB"] = S[lk]["ropeB"]
    ins["l0_maskT"] = S[lk]["maskT"]
    ins.update(host_common_inputs())
    DIN = 2730
    if "l1_w16" not in S:
        l1s = [host_l1_layer_core_inputs(
            None, I["L1_ffn_Wqkv"][i], I["L1_ffn_Wo"][i],
            I["L1_ffn_q_scale"][i], I["L1_ffn_k_scale"][i],
            I["L1_ffn_ln_g"][i], I["L1_ffn_ln_b"][i],
            I["L1_ffn_W1"][i], I["L1_ffn_b1"][i], I["L1_ffn_W2"][i],
            I["L1_ffn_b2"][i], t0, skip_h=True) for i in range(6)]
        S["l1_wqkv6"] = np.stack([s["wqkv"] for s in l1s])
        S["l1_vw6"] = np.stack([s["vw_in"] for s in l1s])
        S["l1_wo6"] = np.stack([s["wo"] for s in l1s])
        for nm in ("qscol", "qspcol", "kscol", "kspcol"):
            S["l1_" + nm + "6"] = np.stack([s[nm] for s in l1s])
        S["l1_lng6"] = np.stack([s["ln_g"] for s in l1s])
        S["l1_lnb6"] = np.stack([s["ln_b"] for s in l1s])
        S["l1_w16"] = np.stack([s["w1"] for s in l1s])
        S["l1_b1c6"] = np.stack([s["b1c"] for s in l1s])
        S["l1_w26"] = np.stack([s["w2"] for s in l1s])
        S["l1_b2c6"] = np.stack([s["b2c"] for s in l1s])
        S["cst_blk2"] = l1s[0]["cst_blk2"]
        S["cst_e2"] = l1s[0]["cst_e2"]
    for k in ("l1_wqkv6", "l1_vw6", "l1_wo6", "l1_qscol6", "l1_qspcol6", "l1_kscol6",
              "l1_kspcol6", "l1_lng6", "l1_lnb6", "l1_w16", "l1_b1c6",
              "l1_w26", "l1_b2c6", "cst_blk2", "cst_e2"):
        ins[k] = S[k]
    rk = ("l1rm", t0)
    if rk not in S:
        pos = np.arange(t0 - 256 - 128, t0 + TPC)
        Aq, Bq, Ak, Bk = host_l1_rope_tables(pos)
        QT_ALL = TPC // 128 + 1
        mT = host_window_maskT(t0 - 128, QT_ALL, 384, 257, lookback=256)
        S[rk] = dict(Aq=Aq, Bq=Bq, Ak=Ak, Bk=Bk, maskT=mT.astype(_bf16()))
    for nm in ("Aq", "Bq", "Ak", "Bk"):
        ins["l1_rope" + nm] = S[rk][nm]
    ins["l1_maskT"] = S[rk]["maskT"]
    a = host_l1_attn_core_inputs(
        np.zeros((D, 1), _bf16()), b, h, I["L1_attn_Wq"], I["L1_attn_Wk"],
        I["L1_attn_Wv"], I["L1_attn_Wb"], I["L1_attn_qc_w"],
        I["L1_attn_kc_w"], I["L1_attn_vc_w"], I["L1_attn_fir_s"],
        I["L1_attn_fir_l"], I["L1_attn_gW1"], I["L1_attn_gb1"],
        I["L1_attn_gW2"], I["L1_attn_gb2"], I["L1_attn_temp"],
        I["L1_attn_eps_floor"], I["L1_attn_onorm_w"])
    for k in ("wq", "wk", "wv", "wb", "qcw", "kcw", "vcw", "firs", "firl",
              "gw1", "gw1h", "gw1b", "gb1c", "gw2", "gb2", "temp", "epsf",
              "onwr"):
        ins["a_" + k] = a[k]
    ins["cst_trilm"] = a["cst_trilm"]
    ins["cst_trium"] = a["cst_trium"]
    ins["cst_triuI"] = a["cst_triuI"]
    ins["a_wo_slice"] = np.ascontiguousarray(
        I["L1_attn_Wo"][h * 256:(h + 1) * 256, :])
    prev = (core - 1) % 8
    pcol = np.arange(128, dtype=np.int32)[:, None]
    ins["idx_prev"] = (prev * 1024 + np.arange(8, dtype=np.int32)[None, :] * 128
                       + pcol).astype(np.int32)
    ins["halo_on"] = np.full((128, 1), 0.0 if core in (0, 4) else 1.0,
                             np.float32)
    return ins


def kernel_fused(I):
    p, nc = _get_prog("fused", build_fused)
    _FUSED_SHARED.clear()
    in_maps = [host_fused_core_inputs(c, I) for c in range(N_CORES)]
    _FUSED_SHARED.clear()
    res = run_bass_kernel_spmd(nc, in_maps, core_ids=list(range(N_CORES)))
    x_np = np.asarray(I["L0_attn_x"])
    out = np.empty((B, T, D), np.float32)
    for c, (b, t0) in enumerate(_core_ranges()):
        out[b, t0:t0 + TPC] = res.results[c]["x_out"].T
    return out





def emit_router_tm(p, em, ps, sc, xn, lnx_tm_d, TOK=TPC, nch=NCH):
    mark = p.mark()
    stg = p.sbuf("rtm_stg", [128, nch * 128], BF16)
    stg2 = p.sbuf("rtm_stg2", [128, nch * 128], BF16)
    for tt_ in range(TOK // 128):
        PS_T = ps["PT" if tt_ % 2 == 0 else "PT2"]
        sg = stg if tt_ % 2 == 0 else stg2
        for c in range(nch):
            em.transpose(TV(PS_T.h[:, c * 128:(c + 1) * 128], PS_T.name),
                         TV(xn.h[:, c, tt_ * 128:(tt_ + 1) * 128], xn.name),
                         sb_ident(sc))
        em.copy("vector", sg.ap, TV(PS_T.h[:, 0:nch * 128], PS_T.name))
        em.dma("gpsimd",
               TV(lnx_tm_d.h[tt_ * 128:(tt_ + 1) * 128, :],
                  f"{lnx_tm_d.name}.w{tt_}"),
               sg.ap, q=em._qname(lnx_tm_d.name))
    p.free_since(mark)


def emit_l0f_sparse(p, em, ps, sc, io, NT=B * T, nch=NCH, H=4096, CAP=4096):
    dm = nch * 128
    J = NT // 128
    rt = emit_routing(p, em, ps, sc, io["logits_all"], NT, CAP=CAP,
                      tril_tile=io["trils"])
    esel_s = p.sbuf("sp_esel", [1, 8], F32)
    em.dma("sync", esel_s.ap, io["esel"].ap)
    em.mm(TV(ps["P2"].h[:, 0:8], ps["P2"].name), sc["ones_row"].ap, esel_s.ap)
    eselb = p.sbuf("sp_eselb", [128, 8], F32)
    em.copy("vector", eselb.ap, TV(ps["P2"].h[:, 0:8], ps["P2"].name))
    e8 = p.sbuf("sp_e8", [1, 8], F32)
    em.dma("sync", e8.ap, io["iota8"].ap)
    em.mm(TV(ps["P3"].h[:, 0:8], ps["P3"].name), sc["ones_row"].ap, e8.ap)
    e8b = p.sbuf("sp_e8b", [128, 8], F32)
    em.copy("vector", e8b.ap, TV(ps["P3"].h[:, 0:8], ps["P3"].name))
    from concourse.ap import AP as _AP

    def _b8(t):
        a = t.h[:]
        return TV(_AP(a.tensor, a.offset,
                      [list(a.ap[0]), [0, J], list(a.ap[1])]), t.name)
    tmp3 = p.sbuf("sp_tmp3", [128, J, 8], F32)
    m1m = p.sbuf("sp_m1m", [128, J], F32)
    m2m = p.sbuf("sp_m2m", [128, J], F32)
    ev = p.sbuf("sp_ev", [128, J], F32)
    yrow1 = p.sbuf("sp_yr1", [128, J], F32)
    yrow2 = p.sbuf("sp_yr2", [128, J], F32)
    offs1 = p.sbuf("sp_of1", [128, J], F32)
    offs2 = p.sbuf("sp_of2", [128, J], F32)
    offs_i = p.sbuf("sp_ofi", [128, 2 * J], I32)
    for (mm_, k_, dst) in ((rt["m1"], rt["k1"], m1m), (rt["m2"], rt["k2"], m2m)):
        em.tt("vector", tmp3.ap, mm_.ap, _b8(eselb), ALU.mult)
        em.reduce("vector", dst.ap, tmp3.ap, ALU.add)
        em.tt("vector", dst.ap, dst.ap, k_.ap, ALU.mult)
    for (mm_, pos_, k_, dst) in ((rt["m1"], rt["pos1"], rt["k1"], yrow1),
                                 (rt["m2"], rt["pos2"], rt["k2"], yrow2)):
        em.tt("vector", tmp3.ap, mm_.ap, _b8(e8b), ALU.mult)
        em.reduce("vector", ev.ap, tmp3.ap, ALU.add)
        em.ts("vector", ev.ap, ev.ap, float(CAP), None, ALU.mult)
        em.tt("vector", ev.ap, ev.ap, pos_.ap, ALU.add)
        em.tt("vector", dst.ap, ev.ap, k_.ap, ALU.mult)
    dumpc_sb = p.sbuf("sp_dumpc", [128, 2 * J], F32)
    em.dma("sync", dumpc_sb.ap, io["dumpc"].ap)
    for (mask, pos_, dst, dc0) in ((m1m, rt["pos1"], offs1, 0),
                                   (m2m, rt["pos2"], offs2, J)):
        em.tt("vector", dst.ap, pos_.ap, mask.ap, ALU.mult)
        em.ts("vector", ev.ap, mask.ap, -1.0, 1.0, ALU.mult, ALU.add)
        em.tt("vector", ev.ap, ev.ap,
              TV(dumpc_sb.h[:, dc0:dc0 + J], dumpc_sb.name), ALU.mult)
        em.tt("vector", dst.ap, dst.ap, ev.ap, ALU.add)
    em.copy("vector", TV(offs_i.h[:, 0:J], offs_i.name), offs1.ap)
    em.copy("vector", TV(offs_i.h[:, J:2 * J], offs_i.name), offs2.ap)
    if io.get("dbg") is not None:
        em.dma("sync", io["dbg"].ap, offs_i.ap)
    cons_d = p.dram("cons_d", [4, 128, J])
    em.dma("gpsimd", TV(cons_d.h[0], cons_d.name), rt["w1b"].ap)
    em.dma("gpsimd", TV(cons_d.h[1], cons_d.name), rt["w2b"].ap)
    em.dma("gpsimd", TV(cons_d.h[2], cons_d.name), yrow1.ap)
    em.dma("gpsimd", TV(cons_d.h[3], cons_d.name), yrow2.ap)
    glist_d = p.dram("glist_d", [CAP + 2 * J * 128, 1], I32)
    zi = p.sbuf("sp_zi", [128, CAP // 128], I32)
    em.memset("vector", zi.ap, 0)
    em.dma("gpsimd",
           TV(glist_d.h[0:CAP, :].rearrange("(pp k) o -> pp (k o)", pp=128),
              glist_d.name), zi.ap)
    tokid = p.sbuf("sp_tokid", [128, J], I32)
    em.dma("sync", tokid.ap, io["tokid"].ap)
    gq = em._qname(glist_d.name)
    em.fence("gpsimd", streams=[gq])
    for j in range(J):
        for off0 in (0, J):
            em.indirect(TV(glist_d.h[:, :], f"{glist_d.name}.s{off0 + j}"),
                        TV(offs_i.h[:, off0 + j:off0 + j + 1], offs_i.name),
                        TV(tokid.h[:, j:j + 1], tokid.name), None, q=gq)
    p.free(*(rt["tiles"] + [esel_s, eselb, e8, e8b, tmp3, m1m, m2m, ev,
                            yrow1, yrow2, offs1, offs2, offs_i, dumpc_sb,
                            zi, tokid]))
    p._dbg_glist = glist_d
    mark2 = p.mark()
    xg_fm = p.sbuf("sp_xgfm", [128, nch, CAP], BF16)
    gl_sb = p.sbuf("sp_gl", [128, CAP // 128], I32)
    em.fence("sync", streams=[em._qname(glist_d.name)])
    em.dma("sync", gl_sb.ap,
           TV(glist_d.h[0:CAP, :].rearrange("(k pp) o -> pp (k o)", pp=128),
              glist_d.name), allow_slow_non_contiguous=True)
    growA = p.sbuf("sp_growA", [128, dm], BF16)
    growB = p.sbuf("sp_growB", [128, dm], BF16)
    for k in range(CAP // 128):
        g = growA if k % 2 == 0 else growB
        PS_T = ps["PT" if k % 2 == 0 else "PT2"]
        em.indirect(g.ap, None, io["lnx_tm_ag"].ap,
                    TV(gl_sb.h[:, k:k + 1], gl_sb.name))
        for c in range(nch):
            em.transpose(TV(PS_T.h[:, c * 128:(c + 1) * 128], PS_T.name),
                         TV(g.h[:, c * 128:(c + 1) * 128], g.name),
                         sb_ident(sc))
        em.copy("vector",
                TV(xg_fm.h[:, :, k * 128:(k + 1) * 128], xg_fm.name),
                TV(PS_T.h[:, 0:nch * 128].rearrange("p (c t) -> p c t",
                                                    c=nch),
                   PS_T.name))
    o512 = p.sbuf("sp_o512", [1, 512], BF16)
    em.dma("gpsimd", o512.ap, io["ones512"].ap)
    eb1col = p.sbuf("sp_eb1col", [128, H // 128], F32)
    em.dma("sync", eb1col.ap, io["eb1c"].ap)
    wbuf = p.sbuf("sp_wbuf", [128, 2, nch, 128], BF16)
    w2buf = p.sbuf("sp_w2buf", [128, 2, nch, 128], BF16)
    h1 = p.sbuf("sp_h1", [128, H // 128, 1024], BF16)
    b2row = p.sbuf("sp_b2row", [1, dm], BF16)
    em.dma("gpsimd", b2row.ap, io["eb2r"].ap)
    ybfA = p.sbuf("sp_ybfA", [128, 512], BF16)
    ybfB = p.sbuf("sp_ybfB", [128, 512], BF16)
    ystA = p.sbuf("sp_ystA", [128, 4, 128], BF16)
    ystB = p.sbuf("sp_ystB", [128, 4, 128], BF16)
    y_tm = io["y_tm"]
    NM = H // 128
    ew1, ew2x = io["ew1"], io["ew2x"]
    for tb in range(CAP // 1024):
        tb0 = tb * 1024
        for m in range(NM):
            wb = TV(wbuf.h[:, m % 2].rearrange("p c m -> p (c m)"),
                    f"sp_wbuf.{m % 2}")
            em.dma("gpsimd", wb, TV(ew1.h[m], ew1.name))
            for s0 in range(0, 1024, 512):
                pst = ps["P0" if (s0 // 512) % 2 == 0 else "P1"]
                for c in range(nch):
                    em.mm(pst.ap,
                          TV(wbuf.h[:, m % 2, c, :], f"sp_wbuf.{m % 2}"),
                          TV(xg_fm.h[:, c, tb0 + s0:tb0 + s0 + 512],
                             xg_fm.name),
                          start=(c == 0), stop=(c == nch - 1))
                em.act(TV(h1.h[:, m, s0:s0 + 512], h1.name), pst.ap, AF.Relu,
                       bias=TV(eb1col.h[:, m:m + 1], eb1col.name))
        for m2 in range(nch):
            for cc0 in range(0, NM, nch):
                slot = (cc0 // nch) % 2
                w2b_ = TV(w2buf.h[:, slot].rearrange("p c m -> p (c m)"),
                          f"sp_w2buf.{slot}")
                em.dma("gpsimd", w2b_,
                       TV(ew2x.h[m2][:, cc0 * 128:(cc0 + nch) * 128],
                          ew2x.name))
                for s0 in range(0, 1024, 512):
                    pst = ps["P2" if s0 == 0 else "P3"]
                    for c in range(nch):
                        em.mm(pst.ap,
                              TV(w2buf.h[:, slot, c, :], f"sp_w2buf.{slot}"),
                              TV(h1.h[:, cc0 + c, s0:s0 + 512], h1.name),
                              start=(cc0 + c == 0), stop=False)
            for s0 in range(0, 1024, 512):
                pst = ps["P2" if s0 == 0 else "P3"]
                em.mm(pst.ap,
                      TV(b2row.h[:, m2 * 128:(m2 + 1) * 128], b2row.name),
                      o512.ap, start=False, stop=True)
                ybf = ybfA if s0 == 0 else ybfB
                yst = ystA if s0 == 0 else ystB
                PS_T = ps["PT" if s0 == 0 else "PT2"]
                em.copy("vector", ybf.ap, pst.ap)
                for j in range(4):
                    em.transpose(TV(PS_T.h[:, j * 128:(j + 1) * 128],
                                    PS_T.name),
                                 TV(ybf.h[:, j * 128:(j + 1) * 128], ybf.name),
                                 sb_ident(sc))
                em.copy("scalar",
                        TV(yst.h[:, :, :], yst.name),
                        TV(PS_T.h[:, 0:512].rearrange("p (j t) -> p j t", j=4),
                           PS_T.name))
                r0 = tb0 + s0
                em.dma("gpsimd",
                       TV(y_tm.h[r0:r0 + 512, m2 * 128:(m2 + 1) * 128]
                          .rearrange("(j pp) t -> pp j t", pp=128),
                          f"{y_tm.name}.w{r0}_{m2}"),
                       yst.ap, q=em._qname(y_tm.name))
    em.fence("sync")
    p.free_since(mark2)
    return cons_d


def emit_l0f_consume(p, em, ps, sc, io, nch=NCH, CAP=4096):
    dm = nch * 128
    cons_d, y_ag, x_d = io["cons_d"], io["y_tm_ag"], io["x"]
    idf = io["idf_sb"]
    mark = p.mark()
    crows = p.sbuf("cn_crows", [16, 128], F32)
    cc = [p.sbuf(f"cn_cc{k}", [128, 16], F32) for k in range(2)]
    yri = [p.sbuf(f"cn_yri{k}", [128, 16], I32) for k in range(2)]
    ccf = p.sbuf("cn_ccf", [128, 16], F32)
    cons_flat = TV(cons_d.h.rearrange("s p j -> (s p) j"), cons_d.name)
    ci = io["cons_idx"]
    for k in range(4):
        em.indirect(crows.ap, None, cons_flat,
                    TV(ci.h[:, k:k + 1], ci.name))
        em.transpose(TV(ps["P2"].h[:, 0:16], ps["P2"].name), crows.ap,
                     TV(idf.h[0:16, 0:16], idf.name))
        dst = cc[k] if k < 2 else ccf
        em.copy("vector", dst.ap, TV(ps["P2"].h[:, 0:16], ps["P2"].name))
        if k >= 2:
            em.copy("vector", yri[k - 2].ap, ccf.ap)
    y1g = [p.sbuf(f"cn_y1g{i}", [128, dm], BF16) for i in range(2)]
    y2g = [p.sbuf(f"cn_y2g{i}", [128, dm], BF16) for i in range(2)]
    ycomb = [p.sbuf(f"cn_ycomb{i}", [128, dm], F32) for i in range(2)]
    for i in range(2):
        em.memset("vector", y1g[i].ap, 0.0)
        em.memset("vector", y2g[i].ap, 0.0)
    ymine_tm = io["ymine_tm"]
    for tt_ in range(TPC // 128):
        a = tt_ % 2
        em.indirect(y1g[a].ap, None, y_ag.ap,
                    TV(yri[0].h[:, tt_:tt_ + 1], yri[0].name))
        em.indirect(y2g[a].ap, None, y_ag.ap,
                    TV(yri[1].h[:, tt_:tt_ + 1], yri[1].name))
        em.ts("vector", ycomb[a].ap, y1g[a].ap,
              TV(cc[0].h[:, tt_:tt_ + 1], cc[0].name), None, ALU.mult)
        em.stt("vector", ycomb[a].ap, y2g[a].ap,
               TV(cc[1].h[:, tt_:tt_ + 1], cc[1].name), ycomb[a].ap,
               ALU.mult, ALU.add)
        em.dma("gpsimd",
               TV(ymine_tm.h[tt_ * 128:(tt_ + 1) * 128, :],
                  f"{ymine_tm.name}.w{tt_}"),
               ycomb[a].ap, q=em._qname(ymine_tm.name))
    em.fence("sync", streams=[em._qname(ymine_tm.name)])
    xa = p.sbuf("cn_xa", [128, 512], F32)
    pa = p.sbuf("cn_pa", [128, 512], BF16)
    for m in range(nch):
        for s0 in range(0, TPC, 512):
            em.dma("sync", xa.ap,
                   TV(x_d.h[m * 128:(m + 1) * 128, s0:s0 + 512], x_d.name))
            em.op("sync",
                  (lambda o, i: lambda e: e.dma_start_transpose(o, i))(
                      pa.h[:, :],
                      ymine_tm.h[s0:s0 + 512, m * 128:(m + 1) * 128]),
                  reads=[TV(None, ymine_tm.name)], writes=[TV(None, pa.name)],
                  dma_queue=em._qname(pa.name))
            em.stt("vector", xa.ap, pa.ap, 0.5, xa.ap, ALU.mult, ALU.add)
            em.dma("scalar",
                   TV(x_d.h[m * 128:(m + 1) * 128, s0:s0 + 512], x_d.name),
                   xa.ap)
    em.fence("sync")
    p.free_since(mark)


# version 25 (final)
# speedup vs baseline: 1.0413x; 1.0000x over previous
import contextlib
import math
import os
import numpy as np

import concourse.bass as bass
import concourse.mybir as mybir
from concourse.bass_utils import run_bass_kernel_spmd
from concourse.bass_core_rust import IndirectOffsetOnAxis

F32 = mybir.dt.float32
BF16 = mybir.dt.bfloat16
I32 = mybir.dt.int32
AF = mybir.ActivationFunctionType
ALU = mybir.AluOpType
AX = mybir.AxisListType

N_CORES = 8
D = 1024
B = 2
T = 8192
TPC = T * B // N_CORES
NCH = D // 128

ENGINES = ("tensor", "vector", "scalar", "gpsimd", "sync")


class TV:
    __slots__ = ("ap", "key")

    def __init__(self, ap, key):
        self.ap = ap
        self.key = key


class Tile:

    def __init__(self, handle, name):
        self.h = handle
        self.name = name

    def __getitem__(self, idx):
        return TV(self.h[idx], self.name)

    @property
    def ap(self):
        return TV(self.h[:], self.name)

    def tag(self, tagname, idx=None):
        ap = self.h[:] if idx is None else self.h[idx]
        return TV(ap, f"{self.name}.{tagname}")


class Em:

    def __init__(self):
        self.counts = {}
        self.clocks = {e: {} for e in ENGINES}
        self.last_write = {}
        self.readers = {}
        self.stream_keys = {}
        self.tape = []
        self.n_waits = 0

    def _bump(self, stream, inc):
        self.counts[stream] = self.counts.get(stream, 0) + inc
        return self.counts[stream]

    def _record_wait(self, engine, waits, stream, count, snapshot):
        if stream.startswith("q|"):
            count = self.counts.get(stream, 0)
        ck = self.clocks[engine]
        if ck.get(stream, 0) >= count:
            return
        waits.append((stream, count))
        self.n_waits += 1
        ck[stream] = count
        if snapshot:
            for s, c in snapshot.items():
                if ck.get(s, 0) < c:
                    ck[s] = c

    def op(self, engine, emit, reads=(), writes=(), dma_queue=None, n_inc=1,
           inc_override=None):
        rkeys = [r.key if isinstance(r, TV) else r for r in reads]
        wkeys = [w.key if isinstance(w, TV) else w for w in writes]
        stream = dma_queue if dma_queue is not None else engine
        inc = 16 * n_inc if dma_queue is not None else n_inc
        if inc_override is not None:
            inc = inc_override

        waits = []
        for k in rkeys:
            lw = self.last_write.get(k)
            if lw is not None:
                s, c, ck = lw
                self._record_wait(engine, waits, s, c, ck)
        for k in wkeys:
            lw = self.last_write.get(k)
            if lw is not None:
                s, c, ck = lw
                self._record_wait(engine, waits, s, c, ck)
            for (s, c, ck) in self.readers.get(k, ()):
                self._record_wait(engine, waits, s, c, ck)
        if dma_queue is not None:
            prior = self.stream_keys.setdefault(stream, set())
            for k2 in prior:
                if k2 in wkeys:
                    continue
                for (s, c, ck) in self.readers.get(k2, ()):
                    self._record_wait(engine, waits, s, c, ck)
            prior.update(wkeys)

        cnt = self._bump(stream, inc)
        if dma_queue is None:
            snapshot = dict(self.clocks[engine])
            snapshot[stream] = cnt
            if engine == "tensor":
                self.clocks[engine][stream] = cnt
        else:
            snapshot = dict(self.clocks[engine])
            snapshot[stream] = cnt
        for k in rkeys:
            self.readers.setdefault(k, []).append((stream, cnt, snapshot))
        for k in wkeys:
            self.last_write[k] = (stream, cnt, snapshot)
            self.readers[k] = []
        self.tape.append((engine, waits, emit, (stream, inc)))


    @staticmethod
    def _qname(key):
        import re as _re
        return "q|" + _re.sub(r"_u\d+", "", key)

    CST_MERGE = {
        "sb_ident_bf", "sb_ones_col", "sb_ones_row", "trils", "trilms",
        "triums", "triuIs", "idf_sb", "ci_sb", "idxp_sb", "hon_sb", "e2",
        "blk2", "epsc", "tclip", "onws", "gb1s", "gb2s", "gw2s", "rb2s",
        "rb1col", "sp_e8", "sp_esel", "sp_o512", "sp_tokid", "sp_b2row",
        "sp_eb1col", "sp_dumpc", "firls", "firss", "wbs", "lgt",
    }

    def dma(self, engine, out, in_, q=None, **kw):
        if q is None:
            q = self._qname(out.key if isinstance(out, TV) else str(out))
            if q[2:] in Em.CST_MERGE:
                q = "q|cstg" if engine == "gpsimd" else "q|cst"
        oap, iap = out.ap, in_.ap
        self.op(engine, lambda e: e.dma_start(out=oap, in_=iap, **kw),
                reads=[in_], writes=[out], dma_queue=q)

    def mm(self, out, lhsT, rhs, start=True, stop=True, **kw):
        oap, lap, rap = out.ap, lhsT.ap, rhs.ap
        self.op("tensor",
                lambda e: e.matmul(oap, lap, rap, start=start, stop=stop, **kw),
                reads=[lhsT, rhs], writes=[out])

    def transpose(self, out, in_, ident, **kw):
        oap, iap, dap = out.ap, in_.ap, ident.ap
        self.op("tensor", lambda e: e.transpose(oap, iap, dap, **kw),
                reads=[in_, ident], writes=[out])

    def act(self, out, in_, func, bias=0.0, scale=1.0, accum_out=None,
            engine="scalar"):
        reads = [in_]
        writes = [out]
        kw = {}
        if isinstance(bias, TV):
            reads.append(bias)
            kw["bias"] = bias.ap
        else:
            kw["bias"] = bias
        if isinstance(scale, TV):
            reads.append(scale)
            kw["scale"] = scale.ap
        else:
            kw["scale"] = scale
        if accum_out is not None:
            writes.append(accum_out)
            kw["accum_out"] = accum_out.ap
        oap, iap = out.ap, in_.ap
        self.op(engine, lambda e: e.activation(oap, iap, func, **kw),
                reads=reads, writes=writes)

    def tt(self, engine, out, a, b, alu):
        oap, aap, bap = out.ap, a.ap, b.ap
        self.op(engine, lambda e: e.tensor_tensor(oap, aap, bap, alu),
                reads=[a, b], writes=[out])

    def ts(self, engine, out, in0, s1, s2, op0, op1=None, accum_out=None):
        reads = [in0]
        writes = [out]
        a1 = s1
        if isinstance(s1, TV):
            reads.append(s1)
            a1 = s1.ap
        a2 = s2
        if isinstance(s2, TV):
            reads.append(s2)
            a2 = s2.ap
        kw = {}
        if accum_out is not None:
            writes.append(accum_out)
            kw["accum_out"] = accum_out.ap
        if op1 is None:
            op1 = ALU.bypass
        oap, iap = out.ap, in0.ap
        self.op(engine,
                lambda e: e.tensor_scalar(oap, iap, a1, a2, op0, op1, **kw),
                reads=reads, writes=writes)

    def stt(self, engine, out, in0, scalar, in1, op0, op1):
        reads = [in0, in1]
        sc = scalar
        if isinstance(scalar, TV):
            reads.append(scalar)
            sc = scalar.ap
        oap, i0, i1 = out.ap, in0.ap, in1.ap
        self.op(engine,
                lambda e: e.scalar_tensor_tensor(oap, i0, sc, i1, op0, op1),
                reads=reads, writes=[out])

    def reduce(self, engine, out, in_, alu, negate=False):
        oap, iap = out.ap, in_.ap
        self.op(engine,
                lambda e: e.tensor_reduce(oap, iap, AX.X, alu, negate=negate),
                reads=[in_], writes=[out])

    def copy(self, engine, out, in_):
        oap, iap = out.ap, in_.ap
        if engine == "scalar":
            self.op(engine, lambda e: e.copy(oap, iap),
                    reads=[in_], writes=[out])
        else:
            self.op(engine, lambda e: e.tensor_copy(oap, iap),
                    reads=[in_], writes=[out])

    def scopy(self, out, in_):
        self.act(out, in_, AF.Copy)

    def memset(self, engine, out, val):
        oap = out.ap
        self.op(engine, lambda e: e.memset(oap, val), reads=[], writes=[out])

    def recip(self, out, in_, engine="vector"):
        oap, iap = out.ap, in_.ap
        self.op(engine, lambda e: e.reciprocal(oap, iap),
                reads=[in_], writes=[out])

    def collective(self, kind, alu, rg, ins, outs):
        iaps = [i.ap for i in ins]
        oaps = [o.ap for o in outs]
        q = self._qname(outs[0].key)
        self.op("gpsimd",
                lambda e: e.collective_compute(kind, alu, rg, ins=iaps, outs=oaps),
                reads=list(ins), writes=list(outs), dma_queue=q,
                inc_override=1)

    def indirect(self, out, out_off, in_, in_off, q=None, extra_reads=None,
                 **kw):
        reads = [in_]
        writes = [out]
        ooap = out_off.ap if out_off is not None else None
        ioap = in_off.ap if in_off is not None else None
        if out_off is not None:
            reads.append(out_off)
        if in_off is not None:
            reads.append(in_off)
        if extra_reads:
            reads.extend(extra_reads)
        oap, iap = out.ap, in_.ap
        if q is None:
            q = self._qname(out.key)

        def emit(e):
            oo = IndirectOffsetOnAxis(ap=ooap, axis=0) if ooap is not None else None
            io = IndirectOffsetOnAxis(ap=ioap, axis=0) if ioap is not None else None
            try:
                return e.indirect_dma_start(oap, oo, iap, io, **kw)
            except Exception:
                print(f"indirect failed: out={oap} in={iap} "
                      f"oo={ooap} io={ioap} kw={kw}")
                raise
        self.op("gpsimd", emit, reads=reads, writes=writes, dma_queue=q)

    def fence(self, engine, streams=None):
        waits = []
        ck = self.clocks[engine]
        items = self.counts.items() if streams is None else \
            [(s, self.counts.get(s, 0)) for s in streams]
        for s, c in list(items):
            if s != engine and c and ck.get(s, 0) < c:
                waits.append((s, c))
                ck[s] = c
        if waits:
            self.tape.append((engine, waits, None, None))

    def barrier(self):
        for e in ENGINES:
            self.fence(e)


def bcast(tv, n):
    from concourse.ap import AP as _AP
    ap = tv.ap
    nap = _AP(ap.tensor, ap.offset,
              [list(d) for d in ap.ap] + [[0, n]])
    return TV(nap, tv.key)


class Prog:

    def __init__(self, name):
        self.name = name
        self.nc = bass.Bass("TRN2", target_bir_lowering=False)
        self.stack = contextlib.ExitStack()
        self.em = Em()
        self.inputs = {}
        self.outputs = {}
        self._cms = {}

    def din(self, name, shape, dtype=F32):
        t = self.nc.dram_tensor(name, list(shape), dtype, kind="ExternalInput")
        tv = Tile(t.ap(), "D." + name)
        self.inputs[name] = tv
        return tv

    def dout(self, name, shape, dtype=F32):
        t = self.nc.dram_tensor(name, list(shape), dtype, kind="ExternalOutput")
        tv = Tile(t.ap(), "D." + name)
        self.outputs[name] = tv
        return tv

    def dram(self, name, shape, dtype=F32, addr_space="Local"):
        self._serial = getattr(self, "_serial", 0) + 1
        name = f"{name}_u{self._serial}"
        t = self.nc.dram_tensor(name, list(shape), dtype, kind="Internal",
                                addr_space=addr_space)
        return Tile(t.ap(), "D." + name)

    def mark(self):
        self._order = getattr(self, "_order", [])
        return len(self._order)

    def free_since(self, mark):
        names = set(self._order[mark:])
        if not names:
            return
        self.em.barrier()
        for nm in reversed(self._order[mark:]):
            cm = self._cms.pop(nm)
            cm.__exit__(None, None, None)
        self._order = self._order[:mark]

    def sbuf(self, name, shape, dtype=F32):
        self._serial = getattr(self, "_serial", 0) + 1
        name = f"{name}_u{self._serial}"
        cm = self.nc.sbuf_tensor(name, list(shape), dtype)
        h = cm.__enter__()
        t = Tile(h, name)
        self._cms[name] = cm
        self._order = getattr(self, "_order", [])
        self._order.append(name)
        return t

    def free(self, *tiles):
        self.em.barrier()
        names = set(t.name for t in tiles)
        for nm in reversed(self._order):
            if nm in names:
                cm = self._cms.pop(nm)
                cm.__exit__(None, None, None)
                names.discard(nm)
        self._order = [n for n in self._order if n in self._cms]

    def psum(self, name, shape, dtype=F32):
        h = self.stack.enter_context(self.nc.psum_tensor(name, list(shape), dtype))
        return Tile(h, name)

    def breg(self, val):
        self._bregs = getattr(self, "_bregs", {})
        if val not in self._bregs:
            r = self.nc.gpsimd.alloc_register(f"bchk{len(self._bregs)}")
            self._bregs[val] = r
            self.em.op("gpsimd",
                       (lambda rr, vv: lambda e: e.reg_mov(rr, vv))(r, val),
                       reads=[], writes=[])
        return self._bregs[val]

    def finalize(self):
        em = self.em
        for name in list(self._cms):
            self.stack.push(self._cms.pop(name))
        with self.stack:
            sems = {}
            with contextlib.ExitStack() as semstack:
                for s in em.counts:
                    sems[s] = semstack.enter_context(
                        self.nc.semaphore(f"sem_{s}"))
                with self.nc.Block() as blk:
                    def runner(engine_name):
                        def run(eng):
                            for (e, waits, emit, inc) in em.tape:
                                if e != engine_name:
                                    continue
                                for (s, c) in waits:
                                    eng.wait_ge(sems[s], c)
                                if emit is not None:
                                    inst = emit(eng)
                                    st, iv = inc
                                    inst.then_inc(sems[st], iv)
                        return run
                    blk.tensor(runner("tensor"))
                    blk.vector(runner("vector"))
                    blk.scalar(runner("scalar"))
                    blk.gpsimd(runner("gpsimd"))
                    blk.sync(runner("sync"))
        return self.nc



def alloc_psum_pool(p):
    return {
        "P0": p.psum("ps_p0", [128, 512], F32),
        "P1": p.psum("ps_p1", [128, 512], F32),
        "P2": p.psum("ps_p2", [128, 512], F32),
        "P3": p.psum("ps_p3", [128, 512], F32),
        "PT": p.psum("ps_pt", [128, 1024], BF16),
        "PT2": p.psum("ps_pt2", [128, 1024], BF16),
        "R1": p.psum("ps_r1", [2, 512], F32),
        "R2": p.psum("ps_r2", [2, 512], F32),
    }


def alloc_common(p, nch=NCH):
    c = {}
    c["ones_col"] = p.din("cst_ones_col", [128, 1], F32)
    c["ones_row"] = p.din("cst_ones_row", [1, 128], F32)
    c["ident_bf"] = p.din("cst_ident_bf", [128, 128], BF16)
    sb = {}
    sb["ones_col"] = p.sbuf("sb_ones_col", [128, 1], F32)
    sb["ones_row"] = p.sbuf("sb_ones_row", [1, 128], F32)
    sb["ident_bf"] = p.sbuf("sb_ident_bf", [128, 128], BF16)
    em = p.em
    for k in ("ones_col", "ones_row", "ident_bf"):
        em.dma("sync", sb[k].ap, c[k].ap)
    sb["xf"] = p.sbuf("ln_xf", [128, 2, 512], F32)
    sb["xsq"] = p.sbuf("ln_xsq", [128, 512], F32)
    for r in ("r_mu", "r_e2", "r_var", "r_sd", "r_rs", "r_mb"):
        sb[r] = p.sbuf("ln_" + r, [1, 512], F32)
    sb["ap_t"] = p.sbuf("ln_apt", [128, 512], F32)
    return sb


def sb_ident(sc):
    return sc["ident_bf"].ap


def emit_ln_fm(p, em, ps, sc, x_dram, xn_sb, ntok, xcol0=0, ncol0=0,
               gb=None, nch=NCH, dmodel=None, resid_sb=None, eps=1e-5,
               shift_half=False):
    if dmodel is None:
        dmodel = nch * 128
    nsub = (ntok + 511) // 512
    if resid_sb is None:
        xr = x_dram.h.rearrange("(c p) t -> p c t", p=128)

    def src_chunk(c, t0, w, slot):
        sh = -1 if (shift_half and c >= nch // 2) else 0
        if resid_sb is not None:
            return TV(resid_sb.h[:, c, ncol0 + t0 + sh:ncol0 + t0 + sh + w],
                      resid_sb.name)
        dst = TV(sc["xf"].h[:, slot, 0:w], f"ln_xf.{slot}")
        em.dma("sync", dst,
               TV(xr[:, c, xcol0 + t0 + sh:xcol0 + t0 + sh + w], x_dram.name))
        return dst

    for s in range(nsub):
        w = min(512, ntok - s * 512)
        t0 = s * 512
        for c in range(nch):
            xf = src_chunk(c, t0, w, c % 2)
            em.act(TV(sc["xsq"].h[:, 0:w], sc["xsq"].name), xf, AF.Square)
            em.mm(TV(ps["R1"].h[0:1, 0:w], ps["R1"].name),
                  sc["ones_col"].ap, xf,
                  start=(c == 0), stop=(c == nch - 1))
            em.mm(TV(ps["R2"].h[0:1, 0:w], ps["R2"].name),
                  sc["ones_col"].ap,
                  TV(sc["xsq"].h[:, 0:w], sc["xsq"].name),
                  start=(c == 0), stop=(c == nch - 1))
        rw = lambda t: TV(t.h[:, 0:w], t.name)
        em.ts("vector", rw(sc["r_mu"]), TV(ps["R1"].h[0:1, 0:w], ps["R1"].name),
              1.0 / dmodel, None, ALU.mult)
        em.ts("vector", rw(sc["r_e2"]), TV(ps["R2"].h[0:1, 0:w], ps["R2"].name),
              1.0 / dmodel, None, ALU.mult)
        em.tt("vector", rw(sc["r_var"]), rw(sc["r_mu"]), rw(sc["r_mu"]), ALU.mult)
        em.tt("vector", rw(sc["r_var"]), rw(sc["r_e2"]), rw(sc["r_var"]), ALU.subtract)
        em.ts("vector", rw(sc["r_var"]), rw(sc["r_var"]), eps, None, ALU.add)
        em.act(rw(sc["r_sd"]), rw(sc["r_var"]), AF.Sqrt)
        em.recip(rw(sc["r_rs"]), rw(sc["r_sd"]))
        em.tt("vector", rw(sc["r_mb"]), rw(sc["r_mu"]), rw(sc["r_rs"]), ALU.mult)
        em.mm(TV(ps["P2"].h[:, 0:w], ps["P2"].name), sc["ones_row"].ap,
              rw(sc["r_rs"]))
        em.mm(TV(ps["P3"].h[:, 0:w], ps["P3"].name), sc["ones_row"].ap,
              rw(sc["r_mb"]))
        for c in range(nch):
            xf = src_chunk(c, t0, w, c % 2)
            em.tt("vector", TV(sc["ap_t"].h[:, 0:w], sc["ap_t"].name),
                  xf, TV(ps["P2"].h[:, 0:w], ps["P2"].name), ALU.mult)
            dst = TV(xn_sb.h[:, c, ncol0 + t0: ncol0 + t0 + w], xn_sb.name)
            if gb is None:
                em.tt("vector", dst, TV(sc["ap_t"].h[:, 0:w], sc["ap_t"].name),
                      TV(ps["P3"].h[:, 0:w], ps["P3"].name), ALU.subtract)
            else:
                g_sb, b_sb = gb
                em.tt("vector", TV(sc["ap_t"].h[:, 0:w], sc["ap_t"].name),
                      TV(sc["ap_t"].h[:, 0:w], sc["ap_t"].name),
                      TV(ps["P3"].h[:, 0:w], ps["P3"].name), ALU.subtract)
                em.ts("vector", dst, TV(sc["ap_t"].h[:, 0:w], sc["ap_t"].name),
                      TV(g_sb.h[:, c:c + 1], g_sb.name),
                      TV(b_sb.h[:, c:c + 1], b_sb.name), ALU.mult, ALU.add)


def emit_rope(p, em, cfg, sc, q_sb, ncols, colA0, tabA, tabB, scratch):
    nch = cfg["nch"]
    t1, t2, qsw = scratch["t1"], scratch["t2"], scratch["qsw"]
    npar = t1.h.shape[1] if len(t1.h.shape) == 3 else 1
    ver = cfg.get("rope_ver", 0)
    if ver is None:
        tA = TV(tabA.h[:, colA0:colA0 + ncols], tabA.name)
        tB = TV(tabB.h[:, colA0:colA0 + ncols], tabB.name)
    else:
        tA = TV(tabA.h[:, ver, colA0:colA0 + ncols], tabA.name)
        tB = TV(tabB.h[:, ver, colA0:colA0 + ncols], tabB.name)
    for c in range(nch):
        if npar == 1:
            t1v = TV(t1.h[:, 0:ncols], t1.name)
            t2v = TV(t2.h[:, 0:ncols], t2.name)
            qsv = lambda a, b_: TV(qsw.h[a:b_, 0:ncols], qsw.name)
            qsf = TV(qsw.h[:, 0:ncols], qsw.name)
        else:
            pr = c % npar
            t1v = TV(t1.h[:, pr, 0:ncols], f"{t1.name}.{pr}")
            t2v = TV(t2.h[:, pr, 0:ncols], f"{t2.name}.{pr}")
            qsv = lambda a, b_, pr=pr: TV(qsw.h[a:b_, pr, 0:ncols],
                                          f"{qsw.name}.{pr}")
            qsf = TV(qsw.h[:, pr, 0:ncols], f"{qsw.name}.{pr}")
        qc = TV(q_sb.h[:, c, 0:ncols], q_sb.name)
        for b in range(4):
            sb_ = b ^ 1
            em.dma("scalar", qsv(b * 32, (b + 1) * 32),
                   TV(q_sb.h[sb_ * 32:(sb_ + 1) * 32, c, 0:ncols],
                      q_sb.name))
        em.tt("vector", t1v, qc, tA, ALU.mult)
        em.tt("vector", t2v, qsf, tB, ALU.mult)
        em.tt("vector", qc, t1v, t2v, ALU.add)



def build_l0_attn(T_half=1024, HALO=512, WIN=512, nheads=16, nch=NCH,
                  debug=False, pp=None, io=None):
    dm = nch * 128
    KSPAN = WIN + 128
    NSLOT = KSPAN // 128
    QT = T_half // 128
    QGRP = min(4, QT)
    WGRP = QGRP * 128
    NTOK = HALO + T_half
    hpc = nheads // nch
    assert hpc == 2

    if pp is None:
        p = Prog("l0attn")
        em = p.em
        x = p.din("x_fm", [dm, HALO + 2 * T_half])
        wqkv = p.din("wqkv", [2 * nch, 128, nch * 128])
        vw_in = p.din("vw_in", [128, nch, dm])
        wout = p.din("wout", [nch, 128, nch * 128])
        ropeA = p.din("ropeA", [2, 128, HALO + 2 * T_half])
        ropeB = p.din("ropeB", [2, 128, HALO + 2 * T_half])
        maskT = p.din("maskT", [2, QT, 128, KSPAN], BF16)
        xo = p.dout("xo_fm", [dm, 2 * T_half])
        ps = alloc_psum_pool(p)
        sc = alloc_common(p, nch)
    else:
        p = pp
        em = p.em
        x, wqkv, vw_in, wout, ropeA, ropeB, maskT, xo = (io[k] for k in
            ("x", "wqkv", "vw_in", "wout", "ropeA", "ropeB", "maskT", "xo"))
        ps, sc = io["ps"], io["sc"]
    mark = p.mark()

    xn = p.sbuf("xn", [128, nch, NTOK], BF16)
    q_sb = p.sbuf("q_sb", [128, nch, T_half], BF16)
    k_sb = p.sbuf("k_sb", [128, nch, NTOK], BF16)
    v_tm = p.sbuf("v_tm", [128, NTOK // 128, nheads * 65], BF16)
    tabA = p.sbuf("tabA", [128, 2, NTOK], BF16)
    tabB = p.sbuf("tabB", [128, 2, NTOK], BF16)
    wbuf = p.sbuf("wbuf", [128, 4, nch, 128], BF16)
    vw = p.sbuf("vw", [128, nch, dm], BF16)
    msk = p.sbuf("msk", [128, QT, KSPAN], BF16)
    rsc = {"t1": p.sbuf("rp_t1", [128, NTOK], BF16),
           "t2": p.sbuf("rp_t2", [128, NTOK], BF16),
           "qsw": p.sbuf("rp_qsw", [128, NTOK], BF16)}
    p_sb = p.sbuf("p_sb", [128, KSPAN], BF16)
    pt_sb = p.sbuf("pt_sb", [128, KSPAN], BF16)
    o_tm = p.sbuf("o_tm", [128, 64], BF16)
    o_fm = p.sbuf("o_fm", [128, nch, 512], BF16)
    rec = p.sbuf("rec_s", [128, 1], F32)
    xres = p.sbuf("xres", [128, 512], F32)
    xout = p.sbuf("xout", [128, 512], F32)
    mx0 = p.sbuf("mx0", [128, 1], F32)
    mx1 = p.sbuf("mx1", [128, 1], F32)
    bias = p.sbuf("bias_s", [128, 1], F32)

    em.dma("gpsimd", vw.ap, vw_in.ap)


    cfg = {"nch": nch}
    for half in range(2):
        xc0 = half * T_half
        em.dma("gpsimd", tabA.ap,
               TV(ropeA.h.rearrange("v p t -> p v t")[:, :, xc0:xc0 + NTOK],
                  ropeA.name))
        em.dma("gpsimd", tabB.ap,
               TV(ropeB.h.rearrange("v p t -> p v t")[:, :, xc0:xc0 + NTOK],
                  ropeB.name))
        em.dma("gpsimd", msk.ap,
               TV(maskT.h[half].rearrange("q p k -> p q k"), maskT.name))
        emit_ln_fm(p, em, ps, sc, x, xn, NTOK, xcol0=xc0)
        for m in range(2 * nch):
            wb = TV(wbuf.h[:, m % 4].rearrange("p c m -> p (c m)"),
                    f"wbuf.{m % 4}")
            em.dma("gpsimd", wb, TV(wqkv.h[m], wqkv.name))
            is_q = m < nch
            ntok_m = T_half if is_q else NTOK
            tok0 = HALO if is_q else 0
            dst = q_sb if is_q else k_sb
            ch = m % nch
            for s0 in range(0, ntok_m, 512):
                w = min(512, ntok_m - s0)
                pst = ps["P0" if (s0 // 512) % 2 == 0 else "P1"]
                for c in range(nch):
                    em.mm(TV(pst.h[:, 0:w], pst.name),
                          TV(wbuf.h[:, m % 4, c, :], f"wbuf.{m % 4}"),
                          TV(xn.h[:, c, tok0 + s0:tok0 + s0 + w], xn.name),
                          start=(c == 0), stop=(c == nch - 1))
                em.copy("vector", TV(dst.h[:, ch, s0:s0 + w], dst.name),
                        TV(pst.h[:, 0:w], pst.name))
        for tt_ in range(NTOK // 128):
            em.memset("vector", TV(v_tm.h[:, tt_, :], v_tm.name), 1.0)
            for nh2 in range(2):
                pst = ps["P2"]
                for c in range(nch):
                    em.mm(TV(pst.h[:, :], pst.name),
                          TV(xn.h[:, c, tt_ * 128:(tt_ + 1) * 128], xn.name),
                          TV(vw.h[:, c, nh2 * 512:(nh2 + 1) * 512], vw.name),
                          start=(c == 0), stop=(c == nch - 1))
                em.copy("scalar",
                        TV(v_tm.h[:, tt_, :].rearrange(
                            "p (h e) -> p h e", h=nheads)[:, nh2 * 8:(nh2 + 1) * 8, 0:64],
                           v_tm.name),
                        TV(pst.h[:, :].rearrange("p (h e) -> p h e", e=64),
                           pst.name))
        if debug and half == 0:
            dbg_xn = p.dout("dbg_xn", [128, nch, NTOK], BF16)
            em.dma("gpsimd", dbg_xn.ap, xn.ap)
            dbg_q0 = p.dout("dbg_q0", [128, nch, T_half], BF16)
            em.dma("gpsimd", dbg_q0.ap, q_sb.ap)
            dbg_v = p.dout("dbg_v", [128, NTOK // 128, nheads * 65], BF16)
            em.dma("gpsimd", dbg_v.ap, v_tm.ap)
        emit_rope(p, em, dict(cfg, rope_ver=1), sc, q_sb, T_half, HALO,
                  tabA, tabB, rsc)
        emit_rope(p, em, dict(cfg, rope_ver=0), sc, k_sb, NTOK, 0,
                  tabA, tabB, rsc)
        if debug and half == 0:
            dbg_q = p.dout("dbg_q", [128, nch, T_half], BF16)
            em.dma("gpsimd", dbg_q.ap, q_sb.ap)
            dbg_k = p.dout("dbg_k", [128, nch, NTOK], BF16)
            em.dma("gpsimd", dbg_k.ap, k_sb.ap)
        for qt in range(QT):
            for h in range(nheads):
                ch, sub = h // hpc, h % hpc
                qv = TV(q_sb.h[sub * 64:(sub + 1) * 64, ch,
                               qt * 128:(qt + 1) * 128], q_sb.name)
                em.mm(TV(ps["P0"].h[:, :], ps["P0"].name), qv,
                      TV(k_sb.h[sub * 64:(sub + 1) * 64, ch,
                                qt * 128:qt * 128 + 512], k_sb.name))
                em.mm(TV(ps["P1"].h[:, 0:KSPAN - 512], ps["P1"].name), qv,
                      TV(k_sb.h[sub * 64:(sub + 1) * 64, ch,
                                qt * 128 + 512:qt * 128 + KSPAN], k_sb.name))
                em.reduce("vector", mx0.ap, ps["P0"].ap, ALU.max)
                em.reduce("vector", mx1.ap,
                          TV(ps["P1"].h[:, 0:KSPAN - 512], ps["P1"].name),
                          ALU.max)
                em.tt("vector", mx0.ap, mx0.ap, mx1.ap, ALU.max)
                em.ts("vector", bias.ap, mx0.ap, -1.0, None, ALU.mult)
                em.act(TV(p_sb.h[:, 0:512], p_sb.name), ps["P0"].ap, AF.Exp,
                       bias=bias.ap)
                em.act(TV(p_sb.h[:, 512:KSPAN], p_sb.name),
                       TV(ps["P1"].h[:, 0:KSPAN - 512], ps["P1"].name), AF.Exp,
                       bias=bias.ap)
                if debug and half == 0 and qt == 0 and h == 3:
                    dbg_p = p.dout("dbg_p", [128, KSPAN], BF16)
                    em.dma("gpsimd", dbg_p.ap, p_sb.ap)
                for s in range(NSLOT):
                    em.transpose(TV(ps["PT"].h[:, s * 128:(s + 1) * 128],
                                    ps["PT"].name),
                                 TV(p_sb.h[:, s * 128:(s + 1) * 128], p_sb.name),
                                 sb_ident(sc))
                em.tt("vector", pt_sb.ap,
                      TV(ps["PT"].h[:, 0:KSPAN], ps["PT"].name),
                      TV(msk.h[:, qt, :], msk.name), ALU.mult)
                for s in range(NSLOT):
                    em.mm(TV(ps["P2"].h[:, 0:65], ps["P2"].name),
                          TV(pt_sb.h[:, s * 128:(s + 1) * 128], pt_sb.name),
                          TV(v_tm.h[:, qt + s, h * 65:(h + 1) * 65], v_tm.name),
                          start=(s == 0), stop=(s == NSLOT - 1))
                em.recip(rec.ap, TV(ps["P2"].h[:, 64:65], ps["P2"].name))
                em.ts("vector", o_tm.ap, TV(ps["P2"].h[:, 0:64], ps["P2"].name),
                      rec.ap, None, ALU.mult)
                if debug and half == 0 and qt == 0 and h == 3:
                    dbg_pt = p.dout("dbg_pt", [128, KSPAN], BF16)
                    em.dma("gpsimd", dbg_pt.ap, pt_sb.ap)
                    dbg_ot = p.dout("dbg_ot", [128, 64], BF16)
                    em.dma("gpsimd", dbg_ot.ap, o_tm.ap)
                    dbg_sum = p.dout("dbg_sum", [128, 1], F32)
                    em.copy("vector", TV(sc["r_sd"].h[0:... if False else slice(None), :].__class__ and sc["r_sd"].h[:, 0:1], "dbgtmp"),
                            TV(ps["P2"].h[:, 64:65], ps["P2"].name)) if False else None
                    em.dma("gpsimd", dbg_sum.ap, rec.ap)
                em.transpose(TV(ps["PT"].h[0:64, 768:896], ps["PT"].name),
                             o_tm.ap, sb_ident(sc))
                em.copy("scalar",
                        TV(o_fm.h[sub * 64:(sub + 1) * 64, ch,
                                  (qt % QGRP) * 128:((qt % QGRP) + 1) * 128],
                           o_fm.name),
                        TV(ps["PT"].h[0:64, 768:896], ps["PT"].name))
            if debug and half == 0 and qt == QGRP - 1:
                dbg_of = p.dout("dbg_of", [128, nch, WGRP], BF16)
                em.dma("gpsimd", dbg_of.ap,
                       TV(o_fm.h[:, :, 0:WGRP], o_fm.name))
            if qt % QGRP == QGRP - 1:
                g0 = (qt - QGRP + 1) * 128
                xr = x.h.rearrange("(c p) t -> p c t", p=128)
                xov = xo.h.rearrange("(c p) t -> p c t", p=128)
                for m in range(nch):
                    wb = TV(wbuf.h[:, m % 2].rearrange("p c m -> p (c m)"),
                            f"wbuf.{m % 2}")
                    em.dma("gpsimd", wb, TV(wout.h[m], wout.name))
                    em.dma("sync", TV(xres.h[:, 0:WGRP], xres.name),
                           TV(xr[:, m, HALO + xc0 + g0:HALO + xc0 + g0 + WGRP],
                              x.name))
                    for c in range(nch):
                        em.mm(TV(ps["P3"].h[:, 0:WGRP], ps["P3"].name),
                              TV(wbuf.h[:, m % 2, c, :], f"wbuf.{m % 2}"),
                              TV(o_fm.h[:, c, 0:WGRP], o_fm.name),
                              start=(c == 0), stop=(c == nch - 1))
                    em.tt("vector", TV(xout.h[:, 0:WGRP], xout.name),
                          TV(ps["P3"].h[:, 0:WGRP], ps["P3"].name),
                          TV(xres.h[:, 0:WGRP], xres.name), ALU.add)
                    em.dma("gpsimd",
                           TV(xov[:, m, xc0 + g0:xc0 + g0 + WGRP], xo.name),
                           TV(xout.h[:, 0:WGRP], xout.name))
    em.fence("sync")
    if pp is not None:
        p.free_since(mark)
    return p



_PERM64 = np.concatenate([np.arange(0, 64, 2), np.arange(1, 64, 2)])


def tile_w(W):
    K, M = W.shape
    t = W.reshape(K // 128, 128, M // 128, 128).transpose(2, 1, 0, 3)
    return np.ascontiguousarray(t.reshape(M // 128, 128, (K // 128) * 128))


def host_common_inputs():
    return {
        "cst_ones_col": np.ones((128, 1), np.float32),
        "cst_ones_row": np.ones((1, 128), np.float32),
        "cst_ident_bf": np.eye(128, dtype=np.float32).astype(_bf16()),
    }


def _bf16():
    import ml_dtypes
    return ml_dtypes.bfloat16


def host_l0_rope_tables(pos, rope_base=10000.0):
    invf = (1.0 / rope_base ** (np.arange(0, 64, 2) / 64.0)).astype(np.float64)
    ang = pos[None, :].astype(np.float64) * invf[:, None]
    cos = np.cos(ang).astype(np.float32)
    sin = np.sin(ang).astype(np.float32)
    A64 = np.concatenate([cos, cos], 0)
    B64 = np.concatenate([-sin, sin], 0)
    return np.concatenate([A64, A64], 0), np.concatenate([B64, B64], 0)


def host_window_maskT(qbase, QT, KSPAN, win_len, lookback=None, seq_start=0):
    if lookback is None:
        lookback = KSPAN - 128
    out = np.zeros((QT, 128, KSPAN), np.float32)
    r = np.arange(128)
    for qt in range(QT):
        qg = qbase + qt * 128 + r
        for s in range(KSPAN // 128):
            c = np.arange(128)
            kg = qbase + qt * 128 - lookback + s * 128 + c
            valid = (qg[None, :] >= kg[:, None]) & \
                    (qg[None, :] - kg[:, None] < win_len) & \
                    (kg[:, None] >= seq_start) & (qg[None, :] >= seq_start)
            out[qt, :, s * 128:(s + 1) * 128] = valid.astype(np.float32)
    return out


def host_l0_attn_core_inputs(x_batch, w_qkv, w_out, t0, T_half=1024, HALO=512,
                             WIN=512):
    dm = x_batch.shape[1]
    Town = 2 * T_half
    lo = t0 - HALO
    if lo < 0:
        pad = np.zeros((-lo, dm), np.float32)
        xc = np.concatenate([pad, x_batch[0:t0 + Town]], 0)
    else:
        xc = x_batch[lo:t0 + Town]
    perm = np.concatenate([h * 64 + _PERM64 for h in range(dm // 64)])
    wq = w_qkv[:, 0:dm][:, perm]
    wk = w_qkv[:, dm:2 * dm][:, perm]
    wv = w_qkv[:, 2 * dm:3 * dm]
    wqkv_p = tile_w(np.concatenate([wq, wk], 1))
    vw_in = np.ascontiguousarray(
        wv.reshape(dm // 128, 128, dm).transpose(1, 0, 2))
    pos = np.maximum(np.arange(lo, t0 + Town), 0)
    A1, B1 = host_l0_rope_tables(pos)
    A = np.stack([A1, A1 * 0.125], 0)
    Bt = np.stack([B1, B1 * 0.125], 0)
    QT = T_half // 128
    KSPAN = WIN + 128
    m0 = host_window_maskT(t0, QT, KSPAN, WIN)
    m1 = host_window_maskT(t0 + T_half, QT, KSPAN, WIN)
    maskT = np.stack([m0, m1], 0).astype(_bf16())
    ins = {
        "x_fm": np.ascontiguousarray(xc.T),
        "wqkv": wqkv_p, "vw_in": vw_in,
        "wout": tile_w(w_out),
        "ropeA": A, "ropeB": Bt,
        "maskT": maskT,
    }
    ins.update(host_common_inputs())
    return ins



_PROG_CACHE = {}


def _get_prog(name, builder):
    if name not in _PROG_CACHE:
        p = builder()
        nc = p.finalize()
        _PROG_CACHE[name] = (p, nc)
    return _PROG_CACHE[name]


def run_phase_l0_attn(x_np, wqkv, wout):
    p, nc = _get_prog("l0attn", lambda: build_l0_attn())
    in_maps = []
    for core in range(N_CORES):
        b = core // 4
        t0 = (core % 4) * TPC
        in_maps.append(host_l0_attn_core_inputs(
            x_np[b], wqkv, wout, t0))
    res = run_bass_kernel_spmd(nc, in_maps, core_ids=list(range(N_CORES)))
    out = np.empty_like(x_np)
    for core in range(N_CORES):
        b = core // 4
        t0 = (core % 4) * TPC
        out[b, t0:t0 + TPC] = res.results[core]["xo_fm"].T
    return out



def build_l1_layer(nch=NCH, nheads=16, WIN=256, HALO_I=512, TOK=TPC,
                   DIN2=2816, debug=False, pp=None, io=None):
    dm = nch * 128
    KSPAN = WIN + 128
    NSLOT = KSPAN // 128
    QT_ALL = TOK // 128 + 1

    if pp is None:
        p = Prog("l1layer")
        em = p.em
        h_in = p.din("h_in", [dm, HALO_I + TOK])
        wqkv = p.din("wqkv", [2 * nch, 128, nch * 128])
        vw_in = p.din("vw_in", [128, nch, dm])
        wo = p.din("wo", [nch, 128, nch * 128])
        qscol = p.din("qscol", [128, 1])
        qspcol = p.din("qspcol", [128, 1])
        kscol = p.din("kscol", [128, 1])
        kspcol = p.din("kspcol", [128, 1])
        ln_g = p.din("ln_g", [128, nch])
        ln_b = p.din("ln_b", [128, nch])
        w1 = p.din("w1", [2 * DIN2 // 128, 128, nch * 128])
        b1c = p.din("b1c", [128, 2 * DIN2 // 128])
        w2 = p.din("w2", [nch, 128, (DIN2 // 128) * 128])
        b2c = p.din("b2c", [128, nch])
        ropeAq = p.din("ropeAq", [128, WIN + 128 + TOK])
        ropeBq = p.din("ropeBq", [128, WIN + 128 + TOK])
        ropeAk = p.din("ropeAk", [128, WIN + 128 + TOK])
        ropeBk = p.din("ropeBk", [128, WIN + 128 + TOK])
        maskT = p.din("maskT", [QT_ALL, 128, KSPAN], BF16)
        h_out = p.dout("h_out", [dm, TOK])
        ps = alloc_psum_pool(p)
        sc = alloc_common(p, nch)
        cst_blk2 = p.din("cst_blk2", [128, 2])
        cst_e2 = p.din("cst_e2", [2, 128])
    else:
        p = pp
        em = p.em
        (h_in, wqkv, vw_in, wo, qscol, qspcol, kscol, kspcol, ln_g, ln_b,
         w1, b1c, w2, b2c, ropeAq, ropeBq, ropeAk, ropeBk, maskT,
         h_out) = (io[k] for k in
            ("h_in", "wqkv", "vw_in", "wo", "qscol", "qspcol", "kscol",
             "kspcol", "ln_g", "ln_b", "w1", "b1c", "w2", "b2c", "ropeAq",
             "ropeBq", "ropeAk", "ropeBk", "maskT", "h_out"))
        ps, sc = io["ps"], io["sc"]
        cst_blk2, cst_e2 = io["cst_blk2"], io["cst_e2"]
    mark = p.mark()
    blk2 = p.sbuf("blk2", [128, 2])
    e2 = p.sbuf("e2", [2, 128])
    em.dma("sync", blk2.ap, cst_blk2.ap)
    em.dma("sync", e2.ap, cst_e2.ap)
    gcol = p.sbuf("gcol", [128, nch])
    bcol = p.sbuf("bcol", [128, nch])
    b1col = p.sbuf("b1col", [128, 2 * DIN2 // 128])
    b2col = p.sbuf("b2col", [128, nch])
    qsc = p.sbuf("qsc", [128, 4])
    em.dma("sync", gcol.ap, ln_g.ap)
    em.dma("sync", bcol.ap, ln_b.ap)
    em.dma("sync", b1col.ap, b1c.ap)
    em.dma("sync", b2col.ap, b2c.ap)
    em.dma("sync", TV(qsc.h[:, 0:1], qsc.name), qscol.ap)
    em.dma("sync", TV(qsc.h[:, 1:2], qsc.name), qspcol.ap)
    em.dma("sync", TV(qsc.h[:, 2:3], qsc.name), kscol.ap)
    em.dma("sync", TV(qsc.h[:, 3:4], qsc.name), kspcol.ap)

    hm = p.dram("hm", [dm, HALO_I + TOK])

    halves = [(0, (QT_ALL + 1) // 2), ((QT_ALL + 1) // 2, QT_ALL)]
    KT_MAX = max(q1 - q0 for q0, q1 in halves) * 128 + WIN
    QT_MAX = max(q1 - q0 for q0, q1 in halves) * 128

    wbuf = p.sbuf("wbuf", [128, 4, nch, 128], BF16)
    p_sb = p.sbuf("p_sb", [128, 2, KSPAN], BF16)
    pt_sb = p.sbuf("pt_sb", [128, 2, KSPAN], BF16)
    o_tm = p.sbuf("o_tm", [128, 2, 64], BF16)
    o_fm = p.sbuf("o_fm", [128, nch, 512], BF16)
    rec = p.sbuf("rec_s", [128, 2], F32)
    xres = p.sbuf("xres", [128, 512], F32)
    xout = p.sbuf("xout", [128, 512], F32)
    mx0 = p.sbuf("mx0", [128, 2], F32)
    bias = p.sbuf("bias_s", [128, 2], F32)
    o16 = p.sbuf("o16", [128, 16, 64], BF16)
    s16 = p.sbuf("s16", [128, 16], F32)
    r2a = p.sbuf("r2a", [2, 512], F32)
    r2b = p.sbuf("r2b", [2, 512], F32)
    xsq2 = p.sbuf("xsq2", [128, 512], F32)
    xn = p.sbuf("xn", [128, nch, KT_MAX], BF16)
    q_sb = p.sbuf("q_sb", [128, nch, QT_MAX], BF16)
    k_sb = p.sbuf("k_sb", [128, nch, KT_MAX], BF16)
    v_tm = p.sbuf("v_tm", [128, KT_MAX // 128, nheads * 65], BF16)
    tstage = p.sbuf("tstage", [128, KT_MAX], F32)
    tabs = {}
    for nm in ("Aq", "Bq", "Ak", "Bk"):
        tabs[nm] = p.sbuf("tab" + nm, [128, KT_MAX], BF16)
    msk = p.sbuf("msk", [128, QT_ALL, KSPAN], BF16)
    vw = p.sbuf("vw", [128, nch, dm], BF16)
    rsc = {"t1": p.sbuf("rp_t1", [128, KT_MAX], BF16),
           "t2": p.sbuf("rp_t2", [128, KT_MAX], BF16),
           "qsw": p.sbuf("rp_qsw", [128, KT_MAX], BF16)}

    em.dma("gpsimd", msk.ap,
           TV(maskT.h.rearrange("q p k -> p q k"), maskT.name))
    em.dma("gpsimd", vw.ap, vw_in.ap)
    hr = h_in.h.rearrange("(c p) t -> p c t", p=128)
    hmr = hm.h.rearrange("(c p) t -> p c t", p=128)
    hor = h_out.h.rearrange("(c p) t -> p c t", p=128)
    cfg = {"nch": nch}

    def l2norm_inplace(dst, ncols, which):
        it = 0
        for c in range(nch):
            for s0 in range(0, ncols, 512):
                pr = it % 2
                it += 1
                PS_R = ps["R1" if pr == 0 else "R2"]
                PS_B = ps["P2" if pr == 0 else "P3"]
                w = min(512, ncols - s0)
                xq = TV(sc["xsq"].h[:, 0:w] if pr == 0 else xsq2.h[:, 0:w],
                        "xsq0" if pr == 0 else "xsq1")
                rr_ = r2a if pr == 0 else r2b
                ra = TV(rr_.h[:, 0:w], rr_.name)
                dv = TV(dst.h[:, c, s0:s0 + w], dst.name)
                em.act(xq, dv, AF.Square)
                em.mm(TV(PS_R.h[0:2, 0:w], PS_R.name), blk2.ap, xq)
                em.act(ra, TV(PS_R.h[0:2, 0:w], PS_R.name), AF.Sqrt)
                em.ts("vector", ra, ra, 1e-12, None, ALU.max)
                em.recip(ra, ra)
                em.mm(TV(PS_B.h[:, 0:w], PS_B.name), e2.ap, ra)
                em.tt("vector", dv, dv, TV(PS_B.h[:, 0:w], PS_B.name),
                      ALU.mult)

    for hf, (hq0, hq1) in enumerate(halves):
        QTh = hq1 - hq0
        NQ = QTh * 128
        NK = NQ + WIN
        qtok0 = -128 + hq0 * 128
        ktok0 = qtok0 - WIN
        kcol0 = HALO_I + ktok0

        for c in range(nch):
            off = kcol0 if c < nch // 2 else kcol0 - 1
            em.dma("gpsimd", TV(xn.h[:, c, 0:NK], xn.name),
                   TV(hr[:, c, off:off + NK], h_in.name))
        tcol0 = ktok0 + WIN + 128
        for nm, src, scol_i in (("Aq", ropeAq, 0), ("Bq", ropeBq, 1),
                                ("Ak", ropeAk, 2), ("Bk", ropeBk, 3)):
            em.dma("sync", TV(tstage.h[:, 0:NK], tstage.name),
                   TV(src.h[:, tcol0:tcol0 + NK], src.name))
            em.ts("vector", TV(tabs[nm].h[:, 0:NK], tabs[nm].name),
                  TV(tstage.h[:, 0:NK], tstage.name),
                  TV(qsc.h[:, scol_i:scol_i + 1], qsc.name), None, ALU.mult)
        for m in range(2 * nch):
            wb = TV(wbuf.h[:, m % 4].rearrange("p c m -> p (c m)"),
                    f"wbuf.{m % 4}")
            em.dma("gpsimd", wb, TV(wqkv.h[m], wqkv.name))
            is_q = m < nch
            ntok_m = NQ if is_q else NK
            tok0 = WIN if is_q else 0
            dst = q_sb if is_q else k_sb
            ch = m % nch
            for s0 in range(0, ntok_m, 512):
                w = min(512, ntok_m - s0)
                pst = ps["P0" if (s0 // 512) % 2 == 0 else "P1"]
                for c in range(nch):
                    em.mm(TV(pst.h[:, 0:w], pst.name),
                          TV(wbuf.h[:, m % 4, c, :], f"wbuf.{m % 4}"),
                          TV(xn.h[:, c, tok0 + s0:tok0 + s0 + w], xn.name),
                          start=(c == 0), stop=(c == nch - 1))
                em.copy("vector", TV(dst.h[:, ch, s0:s0 + w], dst.name),
                        TV(pst.h[:, 0:w], pst.name))
        l2norm_inplace(q_sb, NQ, "q")
        l2norm_inplace(k_sb, NK, "k")
        emit_rope(p, em, dict(cfg, rope_ver=None), sc, q_sb, NQ, WIN,
                  tabs["Aq"], tabs["Bq"], rsc)
        emit_rope(p, em, dict(cfg, rope_ver=None), sc, k_sb, NK, 0,
                  tabs["Ak"], tabs["Bk"], rsc)
        for tt_ in range(NK // 128):
            em.memset("vector", TV(v_tm.h[:, tt_, :], v_tm.name), 1.0)
            for nh2 in range(2):
                pst = ps["P2"]
                for c in range(nch):
                    em.mm(TV(pst.h[:, :], pst.name),
                          TV(xn.h[:, c, tt_ * 128:(tt_ + 1) * 128], xn.name),
                          TV(vw.h[:, c, nh2 * 512:(nh2 + 1) * 512], vw.name),
                          start=(c == 0), stop=(c == nch - 1))
                em.copy("scalar",
                        TV(v_tm.h[:, tt_, :].rearrange(
                            "p (h e) -> p h e", h=nheads)
                           [:, nh2 * 8:(nh2 + 1) * 8, 0:64], v_tm.name),
                        TV(pst.h[:, :].rearrange("p (h e) -> p h e", e=64),
                           pst.name))
        hpc = nheads // nch
        for qtl in range(QTh):
            qt = hq0 + qtl

            def mk_S(h, pr):
                PS_S = ps["P0" if pr == 0 else "P1"]
                ch, sub = h // hpc, h % hpc
                qv = TV(q_sb.h[sub * 64:(sub + 1) * 64, ch,
                               qtl * 128:(qtl + 1) * 128], q_sb.name)
                em.mm(TV(PS_S.h[:, 0:KSPAN], PS_S.name), qv,
                      TV(k_sb.h[sub * 64:(sub + 1) * 64, ch,
                                qtl * 128:qtl * 128 + KSPAN], k_sb.name))

            def mk_rest(h, pr, qtl=qtl, qt=qt):
                def go():
                    PS_S = ps["P0" if pr == 0 else "P1"]
                    PS_V = ps["P2" if pr == 0 else "P3"]
                    PS_T = ps["PT" if pr == 0 else "PT2"]
                    psb = TV(p_sb.h[:, pr, :], f"p_sb.{pr}")
                    ptb = TV(pt_sb.h[:, pr, :], f"pt_sb.{pr}")
                    mxb = TV(mx0.h[:, pr:pr + 1], f"mx0.{pr}")
                    bib = TV(bias.h[:, pr:pr + 1], f"bias.{pr}")
                    em.reduce("vector", mxb,
                              TV(PS_S.h[:, 0:KSPAN], PS_S.name), ALU.max)
                    em.ts("vector", bib, mxb, -1.0, None, ALU.mult)
                    em.act(psb, TV(PS_S.h[:, 0:KSPAN], PS_S.name), AF.Exp,
                           bias=bib)
                    for s in range(NSLOT):
                        em.transpose(TV(PS_T.h[:, s * 128:(s + 1) * 128],
                                        PS_T.name),
                                     TV(p_sb.h[:, pr, s * 128:(s + 1) * 128],
                                        f"p_sb.{pr}"), sb_ident(sc))
                    em.tt("vector", ptb,
                          TV(PS_T.h[:, 0:KSPAN], PS_T.name),
                          TV(msk.h[:, qt, :], msk.name), ALU.mult)
                    for s in range(NSLOT):
                        em.mm(TV(PS_V.h[:, 0:65], PS_V.name),
                              TV(pt_sb.h[:, pr, s * 128:(s + 1) * 128],
                                 f"pt_sb.{pr}"),
                              TV(v_tm.h[:, qtl + s, h * 65:(h + 1) * 65],
                                 v_tm.name),
                              start=(s == 0), stop=(s == NSLOT - 1))
                    em.copy("vector", TV(o16.h[:, h, :], o16.name),
                            TV(PS_V.h[:, 0:64], PS_V.name))
                    em.copy("vector", TV(s16.h[:, h:h + 1], s16.name),
                            TV(PS_V.h[:, 64:65], PS_V.name))
                return go

            rest = None
            for h in range(nheads):
                pr = h % 2
                mk_S(h, pr)
                if rest is not None:
                    rest()
                rest = mk_rest(h, pr)
            rest()
            em.ts("vector", s16.ap, s16.ap, 1e-30, None, ALU.add)
            em.recip(s16.ap, s16.ap)
            for h in range(nheads):
                pr = h % 2
                ch, sub = h // hpc, h % hpc
                PS_T = ps["PT" if pr == 0 else "PT2"]
                otb = TV(o_tm.h[:, pr, :], f"o_tm.{pr}")
                em.ts("vector", otb, TV(o16.h[:, h, :], o16.name),
                      TV(s16.h[:, h:h + 1], s16.name), None, ALU.mult)
                em.transpose(TV(PS_T.h[0:64, 768:896], PS_T.name),
                             otb, sb_ident(sc))
                em.copy("scalar",
                        TV(o_fm.h[sub * 64:(sub + 1) * 64, ch,
                                  (qtl % 4) * 128:((qtl % 4) + 1) * 128],
                           o_fm.name),
                        TV(PS_T.h[0:64, 768:896], PS_T.name))
            if qtl % 4 == 3 or qtl == QTh - 1:
                ngt = (qtl % 4) + 1
                wg = ngt * 128
                g0 = (qtl - ngt + 1) * 128
                hcol = HALO_I + qtok0 + g0
                for m in range(nch):
                    slot = 2 + (m % 2)
                    wb = TV(wbuf.h[:, slot].rearrange("p c m -> p (c m)"),
                            f"wbuf.{slot}")
                    em.dma("gpsimd", wb, TV(wo.h[m], wo.name))
                    em.dma("sync", TV(xres.h[:, 0:wg], xres.name),
                           TV(hr[:, m, hcol:hcol + wg], h_in.name))
                    for c in range(nch):
                        em.mm(TV(ps["P3"].h[:, 0:wg], ps["P3"].name),
                              TV(wbuf.h[:, slot, c, :], f"wbuf.{slot}"),
                              TV(o_fm.h[:, c, 0:wg], o_fm.name),
                              start=(c == 0), stop=(c == nch - 1))
                    em.tt("vector", TV(xout.h[:, 0:wg], xout.name),
                          TV(ps["P3"].h[:, 0:wg], ps["P3"].name),
                          TV(xres.h[:, 0:wg], xres.name), ALU.add)
                    em.dma("gpsimd", TV(hmr[:, m, hcol:hcol + wg], hm.name),
                           TV(xout.h[:, 0:wg], xout.name))

    p.free(xn, q_sb, k_sb, v_tm, tstage, rsc["t1"], rsc["t2"], rsc["qsw"],
           tabs["Aq"], tabs["Bq"], tabs["Ak"], tabs["Bk"], msk, vw)
    FH = TOK // 2
    xn2 = p.sbuf("xn2", [128, nch, FH], BF16)
    xn2b = p.sbuf("xn2b", [128, nch, FH], BF16)
    glu = p.sbuf("glu", [128, DIN2 // 128, FH], BF16)
    gg = p.sbuf("gg", [128, FH], F32)
    NMT = 2 * DIN2 // 128
    NMT2 = DIN2 // 128
    emit_ln_fm(p, em, ps, sc, hm, xn2, FH,
               xcol0=HALO_I, gb=(gcol, bcol), shift_half=True)
    for hf in range(2):
        t0 = hf * FH
        xcur = xn2 if hf == 0 else xn2b
        for mi in range(NMT2):
            if hf == 0 and mi == 8:
                emit_ln_fm(p, em, ps, sc, hm, xn2b, FH,
                           xcol0=HALO_I + FH, gb=(gcol, bcol),
                           shift_half=True)
            for role in (1, 0):
                m = mi + role * NMT2
                wb = TV(wbuf.h[:, m % 4].rearrange("p c m -> p (c m)"),
                        f"wbuf.{m % 4}")
                em.dma("gpsimd", wb, TV(w1.h[m], w1.name))
                for s0 in range(0, FH, 512):
                    w = min(512, FH - s0)
                    pst = ps["P0" if (s0 // 512) % 2 == 0 else "P1"]
                    for c in range(nch):
                        em.mm(TV(pst.h[:, 0:w], pst.name),
                              TV(wbuf.h[:, m % 4, c, :], f"wbuf.{m % 4}"),
                              TV(xcur.h[:, c, s0:s0 + w], xcur.name),
                              start=(c == 0), stop=(c == nch - 1))
                    bcolv = TV(b1col.h[:, m:m + 1], b1col.name)
                    if role == 1:
                        em.act(TV(gg.h[:, s0:s0 + w], gg.name),
                               TV(pst.h[:, 0:w], pst.name), AF.Gelu,
                               bias=bcolv)
                    else:
                        em.stt("vector", TV(glu.h[:, mi, s0:s0 + w], glu.name),
                               TV(pst.h[:, 0:w], pst.name), bcolv,
                               TV(gg.h[:, s0:s0 + w], gg.name),
                               ALU.add, ALU.mult)
        for m in range(nch):
            for cc0 in range(0, NMT2, nch):
                ncc = min(nch, NMT2 - cc0)
                slot = (cc0 // nch) % 4
                wb2 = TV(wbuf.h[:, slot, 0:ncc, :]
                         .rearrange("p c m -> p (c m)"), f"wbuf.{slot}")
                em.dma("gpsimd", wb2,
                       TV(w2.h[m][:, cc0 * 128:(cc0 + ncc) * 128],
                          w2.name))
                for s0 in range(0, FH, 512):
                    w = min(512, FH - s0)
                    pst = ps["P2" if (s0 // 512) % 2 == 0 else "P3"]
                    for c in range(ncc):
                        em.mm(TV(pst.h[:, 0:w], pst.name),
                              TV(wbuf.h[:, slot, c, :], f"wbuf.{slot}"),
                              TV(glu.h[:, cc0 + c, s0:s0 + w], glu.name),
                              start=(cc0 + c == 0),
                              stop=(cc0 + c == NMT2 - 1))
            for s0 in range(0, FH, 512):
                w = min(512, FH - s0)
                pst = ps["P2" if (s0 // 512) % 2 == 0 else "P3"]
                em.dma("sync", TV(xres.h[:, 0:w], xres.name),
                       TV(hmr[:, m, HALO_I + t0 + s0:HALO_I + t0 + s0 + w],
                          hm.name))
                em.stt("vector", TV(xout.h[:, 0:w], xout.name),
                       TV(pst.h[:, 0:w], pst.name),
                       TV(b2col.h[:, m:m + 1], b2col.name),
                       TV(xres.h[:, 0:w], xres.name), ALU.add, ALU.add)
                em.dma("gpsimd", TV(hor[:, m, t0 + s0:t0 + s0 + w], h_out.name),
                       TV(xout.h[:, 0:w], xout.name))
    em.fence("sync")
    if pp is not None:
        p.free_since(mark)
    return p


def host_l1_rope_tables(pos, n=8192, dim_head=64, window=256, qk_scale=8.0):
    half = np.arange(0, dim_head, 2, dtype=np.float64)
    inv_freq = 1.0 / 10000.0 ** (half / dim_head)
    t = np.clip(pos, 0, None).astype(np.float64)
    fr = t[None, :] * inv_freq[:, None]
    base_scale = (half + 0.4 * dim_head) / (1.4 * dim_head)
    power = (t - n // 2) / (window // 2)
    scale = base_scale[:, None] ** power[None, :]
    cos, sin = np.cos(fr), np.sin(fr)
    Aq64 = np.concatenate([cos * scale, cos * scale], 0) * qk_scale
    Bq64 = np.concatenate([-sin * scale, sin * scale], 0) * qk_scale
    inv = 1.0 / scale
    Ak64 = np.concatenate([cos * inv, cos * inv], 0)
    Bk64 = np.concatenate([-sin * inv, sin * inv], 0)
    f32 = lambda a: np.concatenate([a, a], 0).astype(np.float32)
    return f32(Aq64), f32(Bq64), f32(Ak64), f32(Bk64)


def host_l1_layer_core_inputs(h_batch, wqkv, wo, qs, ks, g, b, w1, b1, w2, b2,
                              t0, TOK=TPC, HALO_I=512, WIN=256, DIN2=2816,
                              skip_h=False):
    dm = wqkv.shape[0]
    if skip_h:
        xc = np.zeros((1, dm), np.float32)
    else:
        lo = t0 - HALO_I
        if lo < 0:
            xc = np.concatenate([np.zeros((-lo, dm), np.float32),
                                 h_batch[0:t0 + TOK]], 0)
        else:
            xc = h_batch[lo:t0 + TOK]
    colmap = lambda v: np.ascontiguousarray(
        v.reshape(-1, 128).T.astype(np.float32))
    pcol = lambda v: np.ascontiguousarray(
        np.concatenate([v, v])[:, None].astype(np.float32))
    d2 = DIN2 - w2.shape[0]
    w1a = w1[:, 0:w1.shape[1] // 2]
    w1g = w1[:, w1.shape[1] // 2:]
    w1p = np.concatenate([w1a, np.zeros((dm, d2), np.float32),
                          w1g, np.zeros((dm, d2), np.float32)], 1)
    b1a = b1[0:b1.shape[0] // 2]
    b1g = b1[b1.shape[0] // 2:]
    b1p = np.concatenate([b1a, np.zeros(d2, np.float32),
                          b1g, np.zeros(d2, np.float32)])
    w2p = np.concatenate([w2, np.zeros((d2, dm), np.float32)], 0)
    pos = np.arange(t0 - WIN - 128, t0 + TOK)
    Aq, Bq, Ak, Bk = host_l1_rope_tables(pos)
    QT_ALL = TOK // 128 + 1
    maskT = host_window_maskT(t0 - 128, QT_ALL, WIN + 128, WIN + 1, lookback=WIN)
    perm_pair = (np.arange(128) % 64) ^ 32
    wv_ = wqkv[:, 2 * dm:3 * dm]
    ins = {
        "h_in": np.ascontiguousarray(xc.T),
        "wqkv": tile_w(np.ascontiguousarray(wqkv[:, 0:2 * dm])),
        "vw_in": np.ascontiguousarray(
            wv_.reshape(dm // 128, 128, dm).transpose(1, 0, 2)),
        "wo": tile_w(np.ascontiguousarray(wo)),
        "qscol": pcol(qs), "qspcol": pcol(qs[(np.arange(64)) ^ 32]),
        "kscol": pcol(ks), "kspcol": pcol(ks[(np.arange(64)) ^ 32]),
        "ln_g": colmap(g), "ln_b": colmap(b),
        "w1": tile_w(w1p), "b1c": colmap(b1p),
        "w2": np.ascontiguousarray(
            w2p.reshape(DIN2 // 128, 128, dm // 128, 128)
            .transpose(2, 1, 0, 3).reshape(dm // 128, 128, DIN2)),
        "b2c": colmap(b2),
        "ropeAq": Aq, "ropeBq": Bq, "ropeAk": Ak, "ropeBk": Bk,
        "maskT": maskT.astype(_bf16()),
        "cst_blk2": np.concatenate([np.eye(2, dtype=np.float32)[0].repeat(64)[:, None],
                                    np.eye(2, dtype=np.float32)[1].repeat(64)[:, None]], 1),
        "cst_e2": np.concatenate([np.ones((1, 64), np.float32).repeat(1, 0),
                                  np.zeros((1, 64), np.float32)], 1).reshape(1, 128).repeat(2, 0) * 0,
    }
    e2 = np.zeros((2, 128), np.float32)
    e2[0, 0:64] = 1.0
    e2[1, 64:128] = 1.0
    ins["cst_e2"] = e2
    blk2 = np.zeros((128, 2), np.float32)
    blk2[0:64, 0] = 1.0
    blk2[64:128, 1] = 1.0
    ins["cst_blk2"] = blk2
    ins.update(host_common_inputs())
    return ins


def build_l1_prep(TOK=TPC, nch=NCH, pp=None, io=None):
    if pp is None:
        p = Prog("l1prep")
        em = p.em
        x = p.din("x_fm", [nch * 128, TOK])
        h0 = p.dout("h0", [nch * 128, TOK])
        ps = alloc_psum_pool(p)
        sc = alloc_common(p, nch)
    else:
        p = pp
        em = p.em
        x, h0 = io["x"], io["h0"]
        ps, sc = io["ps"], io["sc"]
    mark = p.mark()
    stg = p.sbuf("stg", [128, nch, TOK], F32)
    emit_ln_fm(p, em, ps, sc, x, stg, TOK)
    em.dma("gpsimd",
           TV(h0.h.rearrange("(c p) t -> p c t", p=128), h0.name),
           stg.ap)
    em.fence("sync")
    if pp is not None:
        p.free_since(mark)
    return p


def build_l1_fin(TOK=TPC, nch=NCH, pp=None, io=None):
    if pp is None:
        p = Prog("l1fin")
        em = p.em
        x = p.din("x_fm", [nch * 128, TOK])
        h = p.din("h_fm", [nch * 128, TOK])
        xo = p.dout("xo_fm", [nch * 128, TOK])
    else:
        p = pp
        em = p.em
        x, h, xo = io["x"], io["h"], io["xo"]
    mark = p.mark()
    xa = p.sbuf("xa", [128, 512], F32)
    ha = p.sbuf("ha", [128, 512], F32)
    oa = p.sbuf("oa", [128, 512], F32)
    xr = x.h.rearrange("(c p) t -> p c t", p=128)
    hr = h.h.rearrange("(c p) t -> p c t", p=128)
    orr = xo.h.rearrange("(c p) t -> p c t", p=128)
    for c in range(nch):
        for s0 in range(0, TOK, 512):
            em.dma("sync", xa.ap, TV(xr[:, c, s0:s0 + 512], x.name))
            em.dma("sync", ha.ap, TV(hr[:, c, s0:s0 + 512], h.name))
            em.stt("vector", oa.ap, ha.ap, 0.5, xa.ap, ALU.mult, ALU.add)
            em.dma("gpsimd", TV(orr[:, c, s0:s0 + 512], xo.name), oa.ap)
    em.fence("sync")
    if pp is not None:
        p.free_since(mark)
    return p


def _core_ranges():
    return [(c // 4, (c % 4) * TPC) for c in range(N_CORES)]


def run_phase_l1_ffn(x_np, Wqkv6, qs6, ks6, Wo6, g6, b6, W16, b16, W26, b26,
                     depth=6):
    p1, nc1 = _get_prog("l1prep", build_l1_prep)
    pl, ncl = _get_prog("l1layer", build_l1_layer)
    p2, nc2 = _get_prog("l1fin", build_l1_fin)
    com = host_common_inputs()

    def fm(a):
        return np.ascontiguousarray(a.T)

    in_maps = []
    for b, t0 in _core_ranges():
        m = {"x_fm": fm(x_np[b, t0:t0 + TPC])}
        m.update(com)
        in_maps.append(m)
    res = run_bass_kernel_spmd(nc1, in_maps, core_ids=list(range(N_CORES)))
    h_np = np.empty_like(x_np)
    for c, (b, t0) in enumerate(_core_ranges()):
        h_np[b, t0:t0 + TPC] = res.results[c]["h0"].T
    for i in range(depth):
        in_maps = []
        for b, t0 in _core_ranges():
            in_maps.append(host_l1_layer_core_inputs(
                h_np[b], Wqkv6[i], Wo6[i], qs6[i], ks6[i], g6[i], b6[i],
                W16[i], b16[i], W26[i], b26[i], t0))
        res = run_bass_kernel_spmd(ncl, in_maps, core_ids=list(range(N_CORES)))
        for c, (b, t0) in enumerate(_core_ranges()):
            h_np[b, t0:t0 + TPC] = res.results[c]["h_out"].T
    in_maps = []
    for b, t0 in _core_ranges():
        in_maps.append({"x_fm": fm(x_np[b, t0:t0 + TPC]),
                        "h_fm": fm(h_np[b, t0:t0 + TPC])})
    res = run_bass_kernel_spmd(nc2, in_maps, core_ids=list(range(N_CORES)))
    out = np.empty_like(x_np)
    for c, (b, t0) in enumerate(_core_ranges()):
        out[b, t0:t0 + TPC] = res.results[c]["xo_fm"].T
    return out



def build_l0f_router(TOK=TPC, nch=NCH, H=4096, pp=None, io=None):
    if pp is None:
        p = Prog("l0fr")
        em = p.em
        x = p.din("x_fm", [nch * 128, TOK])
        rw1 = p.din("rw1", [H // 128, 128, nch * 128])
        rb1c = p.din("rb1c", [128, H // 128])
        rw2 = p.din("rw2", [H, 8])
        rb2 = p.din("rb2", [1, 8])
        lnx = p.dout("lnx", [nch * 128, TOK], BF16)
        logits = p.dout("logits", [TOK, 8])
        ps = alloc_psum_pool(p)
        sc = alloc_common(p, nch)
    else:
        p = pp
        em = p.em
        x, rw1, rb1c, rw2, rb2, lnx, logits = (io[k] for k in
            ("x", "rw1", "rb1c", "rw2", "rb2", "lnx", "logits"))
        ps, sc = io["ps"], io["sc"]
    mark = p.mark()
    xn = p.sbuf("xn", [128, nch, TOK], BF16)
    emit_ln_fm(p, em, ps, sc, x, xn, TOK)
    if lnx is not None:
        em.dma("gpsimd",
               TV(lnx.h.rearrange("(c p) t -> p c t", p=128), lnx.name),
               xn.ap)
    if io is not None and io.get("lnx_tm") is not None:
        emit_router_tm(p, em, ps, sc, xn, io["lnx_tm"], TOK, nch)
    wbuf = p.sbuf("wbuf", [128, 4, nch, 128], BF16)
    rw2b = p.sbuf("rw2b", [128, H // 128, 8], BF16)
    em.dma("gpsimd", rw2b.ap,
           TV(rw2.h.rearrange("(m p) e -> p m e", p=128), rw2.name))
    rb1col = p.sbuf("rb1col", [128, H // 128], F32)
    em.dma("sync", rb1col.ap, rb1c.ap)
    rb2s = p.sbuf("rb2s", [1, 8], F32)
    em.dma("sync", rb2s.ap, rb2.ap)
    lg = p.sbuf("lg", [128, 8], F32)
    rb2bc = p.sbuf("rb2bc", [128, 8], F32)
    h1 = p.sbuf("h1", [128, H // 128, 1024], BF16)
    NM = H // 128
    for tb in range(TOK // 1024):
        tb0 = tb * 1024
        for m in range(NM):
            wb = TV(wbuf.h[:, m % 4].rearrange("p c m -> p (c m)"),
                    f"wbuf.{m % 4}")
            em.dma("gpsimd", wb, TV(rw1.h[m], rw1.name))
            for s0 in range(0, 1024, 512):
                pst = ps["P0" if (s0 // 512) % 2 == 0 else "P1"]
                for c in range(nch):
                    em.mm(pst.ap,
                          TV(wbuf.h[:, m % 4, c, :], f"wbuf.{m % 4}"),
                          TV(xn.h[:, c, tb0 + s0:tb0 + s0 + 512], xn.name),
                          start=(c == 0), stop=(c == nch - 1))
                em.act(TV(h1.h[:, m, s0:s0 + 512], h1.name), pst.ap, AF.Relu,
                       bias=TV(rb1col.h[:, m:m + 1], rb1col.name))
        em.mm(TV(ps["P2"].h[:, 0:8], ps["P2"].name), sc["ones_row"].ap, rb2s.ap)
        em.copy("vector", rb2bc.ap, TV(ps["P2"].h[:, 0:8], ps["P2"].name))
        for tt_ in range(8):
            pst = ps["P3"]
            for m in range(NM):
                em.mm(TV(pst.h[:, 0:8], pst.name),
                      TV(h1.h[:, m, tt_ * 128:(tt_ + 1) * 128], h1.name),
                      TV(rw2b.h[:, m, :], rw2b.name),
                      start=(m == 0), stop=(m == NM - 1))
            em.copy("scalar", TV(lg.h[:, :], "lg_rb2"),
                    TV(ps["P2"].h[:, 0:8], ps["P2"].name)) if False else None
            em.tt("vector", lg.ap, TV(pst.h[:, 0:8], pst.name),
                  rb2bc.ap, ALU.add)
            em.dma("gpsimd",
                   TV(logits.h[tb0 + tt_ * 128:tb0 + (tt_ + 1) * 128, :],
                      logits.name),
                   lg.ap)
    em.fence("sync")
    if pp is not None:
        p.free_since(mark)
    return p


def emit_routing(p, em, ps, sc, logits_all, NT, CAP=4096, tril_tile=None):
    J = NT // 128
    _rt_tiles = []
    _sb = p.sbuf
    def _sbuf(name, shape, dtype=F32):
        t = _sb(name, shape, dtype)
        _rt_tiles.append(t)
        return t
    p_sbuf_saved = p.sbuf
    p.sbuf = _sbuf
    lgt = p.sbuf("lgt", [128, J, 8], F32)
    em.dma("sync", lgt.ap,
           TV(logits_all.h.rearrange("(pp j) e -> pp j e", pp=128),
              logits_all.name))
    m1v = p.sbuf("m1v", [128, J], F32)
    m2v = p.sbuf("m2v", [128, J], F32)
    m1 = p.sbuf("m1", [128, J, 8], F32)
    m2 = p.sbuf("m2", [128, J, 8], F32)
    s12 = p.sbuf("s12", [128, J, 8], F32)
    pex = p.sbuf("pex", [128, J, 8], F32)
    tot = p.sbuf("tot", [128, 8], F32)
    offs = p.sbuf("offs", [128, 8], F32)
    pos1 = p.sbuf("pos1", [128, J], F32)
    pos2 = p.sbuf("pos2", [128, J], F32)
    w1b = p.sbuf("w1b", [128, J], F32)
    w2b = p.sbuf("w2b", [128, J], F32)
    tmp = p.sbuf("rt_tmp", [128, J, 8], F32)
    tril = tril_tile if tril_tile is not None else \
        p.din("cst_trilstrict", [128, 128])
    trils = p.sbuf("trils", [128, 128], F32)
    em.dma("sync", trils.ap, tril.ap)
    zcol = p.sbuf("zcol", [128, J + 1], F32)

    b3 = lambda t: t.ap
    em.reduce("vector", m1v.ap, b3(lgt), ALU.max)
    em.tt("vector", b3(m1), b3(lgt), bcast(m1v.ap, 8), ALU.is_ge)
    em.stt("vector", b3(tmp), b3(m1), -1e30, b3(lgt), ALU.mult, ALU.add)
    em.reduce("vector", m2v.ap, b3(tmp), ALU.max)
    em.tt("vector", b3(m2), b3(tmp), bcast(m2v.ap, 8), ALU.is_ge)
    em.tt("vector", b3(s12), b3(m1), b3(m2), ALU.add)
    zrow = p.sbuf("zrow", [128, J], F32)
    em.memset("vector", zrow.ap, 0.0)
    em.memset("vector", TV(zcol.h[:, 0:1], zcol.name), 0.0)
    for e in range(8):
        sv = TV(s12.h[:].rearrange("p j e -> p e j")[:, e, :], s12.name)
        em.copy("vector", TV(zcol.h[:, 1:J + 1], zcol.name), sv)
        pv = TV(pex.h[:].rearrange("p j e -> p e j")[:, e, :], pex.name)
        zin = TV(zcol.h[:, 0:J], zcol.name)
        zr = zrow.ap
        em.op("vector",
              (lambda o, a, b: lambda eng: eng.tensor_tensor_scan(
                  o.ap, a.ap, b.ap, 0.0, ALU.add, ALU.add))(pv, zin, zr),
              reads=[zin, zr], writes=[pv])
    em.reduce("vector", tot.ap,
              TV(s12.h[:].rearrange("p j e -> p e j"), s12.name), ALU.add)
    em.mm(TV(ps["P2"].h[:, 0:8], ps["P2"].name), trils.ap, tot.ap)
    em.copy("vector", offs.ap, TV(ps["P2"].h[:, 0:8], ps["P2"].name))
    ob = TV(offs.h[:].rearrange("p e -> p e"), offs.name)
    from concourse.ap import AP as _AP
    oap = offs.h[:]
    oap2 = _AP(oap.tensor, oap.offset,
               [list(oap.ap[0]), [0, J], list(oap.ap[1])])
    em.tt("vector", b3(pex), b3(pex), TV(oap2, offs.name), ALU.add)
    em.tt("vector", b3(tmp), b3(pex), b3(m1), ALU.mult)
    em.reduce("vector", pos1.ap, b3(tmp), ALU.add)
    em.tt("vector", b3(tmp), b3(pex), b3(m2), ALU.mult)
    em.reduce("vector", pos2.ap, b3(tmp), ALU.add)
    k1 = p.sbuf("k1", [128, J], F32)
    k2 = p.sbuf("k2", [128, J], F32)
    em.ts("vector", k1.ap, pos1.ap, float(CAP), None, ALU.is_lt)
    em.ts("vector", k2.ap, pos2.ap, float(CAP), None, ALU.is_lt)
    em.tt("vector", w1b.ap, m1v.ap, m2v.ap, ALU.subtract)
    em.act(w1b.ap, w1b.ap, AF.Sigmoid)
    em.ts("vector", w2b.ap, w1b.ap, -1.0, 1.0, ALU.mult, ALU.add)
    em.tt("vector", w1b.ap, w1b.ap, k1.ap, ALU.mult)
    em.tt("vector", w2b.ap, w2b.ap, k2.ap, ALU.mult)
    p.sbuf = p_sbuf_saved
    return dict(m1=m1, m2=m2, w1b=w1b, w2b=w2b, pos1=pos1, pos2=pos2,
                k1=k1, k2=k2, J=J, tiles=_rt_tiles)


def build_l0f_expert(NT=B * T, nch=NCH, H=4096, TOKSUB=1024, pp=None, io=None):
    if pp is None:
        p = Prog("l0fe")
        em = p.em
        lnx = p.din("lnx_all", [nch * 128, NT], BF16)
        logits_all = p.din("logits_all", [NT, 8])
        ew1 = p.din("ew1", [H // 128, 128, nch * 128])
        eb1c = p.din("eb1c", [128, H // 128])
        ew2x = p.din("ew2x", [nch, 128, H])
        eb2r = p.din("eb2r", [1, nch * 128])
        esel = p.din("esel", [1, 8])
        ones512 = p.din("cst_ones512", [1, 512])
        y = p.dout("y_fm", [nch * 128, NT], BF16)
        ps = alloc_psum_pool(p)
        sc = alloc_common(p, nch)
        lr_ = lnx.h.rearrange("(c p) t -> p c t", p=128)
        yr_ = y.h.rearrange("(c p) t -> p c t", p=128)
        lnx_src = lambda tb0, n: TV(lr_[:, :, tb0:tb0 + n], lnx.name)
        y_dst = (lambda m2, t0_, w:
                 TV(yr_[:, m2, t0_:t0_ + w], y.name))
    else:
        p = pp
        em = p.em
        logits_all, ew1, eb1c, ew2x, esel, ones512 = (io[k] for k in
            ("logits_all", "ew1", "eb1c", "ew2x", "esel", "ones512"))
        eb2r = io["eb2r"]
        _trilt = io.get("trils")
        ps, sc = io["ps"], io["sc"]
        lnxb = io["lnx_blocks"]
        yb = io["y_blocks"]

        def lnx_src(tb0, n):
            blk, loc = tb0 // TPC, tb0 % TPC
            return TV(lnxb.h[blk].rearrange("(c p) t -> p c t", p=128)
                      [:, :, loc:loc + n], lnxb.name)

        def y_dst(m2, t0_, w):
            blk, loc = t0_ // TPC, t0_ % TPC
            return TV(yb.h[blk].rearrange("(c p) t -> p c t", p=128)
                      [:, m2, loc:loc + w], yb.name)
    mark = p.mark()
    wr2d = p.dram("wr2d", [128, NT // 128])
    wr2s = p.sbuf("wr2s", [1, 512], F32)
    rt = emit_routing(p, em, ps, sc, logits_all, NT,
                      tril_tile=(io.get("trils") if io else None))
    J = rt["J"]
    esel_s = p.sbuf("esel_s", [1, 8], F32)
    em.dma("sync", esel_s.ap, esel.ap)
    em.mm(TV(ps["P2"].h[:, 0:8], ps["P2"].name), sc["ones_row"].ap, esel_s.ap)
    eselb = p.sbuf("eselb", [128, 8], F32)
    em.copy("vector", eselb.ap, TV(ps["P2"].h[:, 0:8], ps["P2"].name))
    from concourse.ap import AP as _AP
    eap = eselb.h[:]
    eap2 = _AP(eap.tensor, eap.offset,
               [list(eap.ap[0]), [0, J], list(eap.ap[1])])
    tmp = p.sbuf("ce_tmp", [128, J, 8], F32)
    wown = p.sbuf("wown", [128, J], F32)
    b3 = lambda t: t.ap
    em.tt("vector", b3(tmp), b3(rt["m1"]), TV(eap2, eselb.name), ALU.mult)
    em.tt("vector", b3(tmp), b3(tmp), bcast(rt["w1b"].ap, 8), ALU.mult)
    tmp2 = p.sbuf("ce_tmp2", [128, J, 8], F32)
    em.tt("vector", b3(tmp2), b3(rt["m2"]), TV(eap2, eselb.name), ALU.mult)
    em.tt("vector", b3(tmp2), b3(tmp2), bcast(rt["w2b"].ap, 8), ALU.mult)
    em.tt("vector", b3(tmp), b3(tmp), b3(tmp2), ALU.add)
    em.reduce("vector", wown.ap, b3(tmp), ALU.add)
    em.dma("scalar", wr2d.ap, TV(wown.h[:, :], wown.name))
    p.free(*(rt["tiles"] + [esel_s, eselb, tmp, tmp2, wown]))
    wr2flat = TV(wr2d.h.rearrange("pp j -> (pp j)"), wr2d.name)

    o512 = p.sbuf("o512", [1, 512], BF16)
    em.dma("gpsimd", o512.ap, ones512.ap)
    eb1col = p.sbuf("eb1col", [128, H // 128], F32)
    em.dma("sync", eb1col.ap, eb1c.ap)
    wbuf = p.sbuf("wbufE", [128, 2, nch, 128], BF16)
    w2buf = p.sbuf("w2bufE", [128, 2, nch, 128], BF16)
    h1 = p.sbuf("h1E", [128, H // 128, TOKSUB], BF16)
    bcs = p.sbuf("bcsE", [128, 512], F32)
    yout = p.sbuf("youtE", [128, 512], BF16)
    NM = H // 128
    xnsub = p.sbuf("xnsubE", [128, nch, TOKSUB], BF16)
    b2row = p.sbuf("b2rowE", [1, nch * 128], BF16)
    em.dma("gpsimd", b2row.ap, eb2r.ap)
    for tb in range(NT // TOKSUB):
        tb0 = tb * TOKSUB
        em.dma("sync", xnsub.ap, lnx_src(tb0, TOKSUB))
        for m in range(NM):
            wb = TV(wbuf.h[:, m % 2].rearrange("p c m -> p (c m)"),
                    f"wbufE.{m % 2}")
            em.dma("gpsimd", wb, TV(ew1.h[m], ew1.name))
            for s0 in range(0, TOKSUB, 512):
                pst = ps["P0" if (s0 // 512) % 2 == 0 else "P1"]
                for c in range(nch):
                    em.mm(pst.ap,
                          TV(wbuf.h[:, m % 2, c, :], f"wbufE.{m % 2}"),
                          TV(xnsub.h[:, c, s0:s0 + 512], xnsub.name),
                          start=(c == 0), stop=(c == nch - 1))
                em.act(TV(h1.h[:, m, s0:s0 + 512], h1.name), pst.ap, AF.Relu,
                       bias=TV(eb1col.h[:, m:m + 1], eb1col.name))
        for s0 in range(0, TOKSUB, 512):
            em.dma("sync", wr2s.ap,
                   TV(wr2d.h.rearrange("pp j -> (pp j)")
                      [tb0 + s0:tb0 + s0 + 512], wr2d.name))
            em.mm(TV(ps["P2"].h[:, :], ps["P2"].name), sc["ones_row"].ap,
                  wr2s.ap)
            em.copy("vector", bcs.ap, TV(ps["P2"].h[:, :], ps["P2"].name))
            for m2 in range(nch):
                pst = ps["P3"]
                for cc0 in range(0, NM, nch):
                    slot = (cc0 // nch) % 2
                    w2b_ = TV(w2buf.h[:, slot].rearrange("p c m -> p (c m)"),
                              f"w2bufE.{slot}")
                    em.dma("gpsimd", w2b_,
                           TV(ew2x.h[m2][:, cc0 * 128:(cc0 + nch) * 128],
                              ew2x.name))
                    for c in range(nch):
                        em.mm(pst.ap,
                              TV(w2buf.h[:, slot, c, :], f"w2bufE.{slot}"),
                              TV(h1.h[:, cc0 + c, s0:s0 + 512], h1.name),
                              start=(cc0 + c == 0), stop=False)
                em.mm(pst.ap,
                      TV(b2row.h[:, m2 * 128:(m2 + 1) * 128], b2row.name),
                      o512.ap, start=False, stop=True)
                em.tt("vector", yout.ap, pst.ap, bcs.ap, ALU.mult)
                em.dma("gpsimd", y_dst(m2, tb0 + s0, 512), yout.ap)
    em.fence("sync")
    if pp is not None:
        p.free_since(mark)
    return p


def build_l0f_combine(TOK=TPC, nch=NCH):
    p = Prog("l0fc")
    em = p.em
    x = p.din("x_fm", [nch * 128, TOK])
    parts = p.din("parts", [8, nch * 128, TOK], BF16)
    xo = p.dout("xo_fm", [nch * 128, TOK])
    xa = p.sbuf("xaC", [128, 512], F32)
    pa = p.sbuf("paC", [128, 8, 512], BF16)
    acc = p.sbuf("accC", [128, 512], F32)
    xr = x.h.rearrange("(c p) t -> p c t", p=128)
    pr = parts.h.rearrange("ee (c p) t -> p ee c t", p=128)
    orr = xo.h.rearrange("(c p) t -> p c t", p=128)
    for c in range(nch):
        for s0 in range(0, TOK, 512):
            em.dma("sync", xa.ap, TV(xr[:, c, s0:s0 + 512], x.name))
            em.dma("sync", pa.ap, TV(pr[:, :, c, s0:s0 + 512], parts.name))
            em.tt("vector", acc.ap, TV(pa.h[:, 0, :], pa.name),
                  TV(pa.h[:, 1, :], pa.name), ALU.add)
            for e in range(2, 8):
                em.tt("vector", acc.ap, acc.ap, TV(pa.h[:, e, :], pa.name),
                      ALU.add)
            em.stt("vector", acc.ap, acc.ap, 0.5, xa.ap, ALU.mult, ALU.add)
            em.dma("gpsimd", TV(orr[:, c, s0:s0 + 512], xo.name), acc.ap)
    em.fence("sync")
    return p


def host_l0f_common():
    tril = np.tril(np.ones((128, 128), np.float32), -1)
    trilT = np.triu(np.ones((128, 128), np.float32), 1).T
    trils = np.triu(np.ones((128, 128), np.float32), 1)
    return {"cst_trilstrict": trils,
            "cst_ones512": np.ones((1, 512), np.float32)}


def run_phase_l0_ffn(x_np, rw1, rb1, rw2, rb2, ew1, eb1, ew2, eb2):
    pr, ncr = _get_prog("l0fr", build_l0f_router)
    pe, nce = _get_prog("l0fe", build_l0f_expert)
    pc, ncc = _get_prog("l0fc", build_l0f_combine)
    com = host_common_inputs()
    rb1c = np.ascontiguousarray(rb1.reshape(-1, 128).T)
    rw1t = tile_w(rw1)
    in_maps = []
    for b, t0 in _core_ranges():
        m = {"x_fm": np.ascontiguousarray(x_np[b, t0:t0 + TPC].T),
             "rw1": rw1t, "rb1c": rb1c, "rw2": rw2,
             "rb2": rb2.reshape(1, 8)}
        m.update(com)
        in_maps.append(m)
    res = run_bass_kernel_spmd(ncr, in_maps, core_ids=list(range(N_CORES)))
    NT = B * T
    lnx_all = np.empty((D, NT), dtype=res.results[0]["lnx"].dtype)
    logits_all = np.empty((NT, 8), np.float32)
    for c, (b, t0) in enumerate(_core_ranges()):
        g0 = b * T + t0
        lnx_all[:, g0:g0 + TPC] = res.results[c]["lnx"]
        logits_all[g0:g0 + TPC] = res.results[c]["logits"]
    in_maps = []
    for e in range(8):
        esel = np.zeros((1, 8), np.float32)
        esel[0, e] = 1.0
        H_ = ew1.shape[2]
        ew2t = np.ascontiguousarray(
            ew2[e].reshape(H_ // 128, 128, 8, 128).transpose(2, 1, 0, 3)
            .reshape(8, 128, H_))
        m = {"lnx_all": lnx_all, "logits_all": logits_all,
             "ew1": tile_w(ew1[e]),
             "eb1c": np.ascontiguousarray(eb1[e].reshape(-1, 128).T),
             "ew2x": ew2t, "esel": esel,
             "eb2r": eb2[e].reshape(1, -1)}
        m.update(com)
        m.update(host_l0f_common())
        in_maps.append(m)
    res = run_bass_kernel_spmd(nce, in_maps, core_ids=list(range(N_CORES)))
    ys = [res.results[e]["y_fm"] for e in range(8)]
    in_maps = []
    for b, t0 in _core_ranges():
        g0 = b * T + t0
        parts = np.stack([ys[e][:, g0:g0 + TPC] for e in range(8)], 0)
        m = {"x_fm": np.ascontiguousarray(x_np[b, t0:t0 + TPC].T),
             "parts": parts}
        in_maps.append(m)
    res = run_bass_kernel_spmd(ncc, in_maps, core_ids=list(range(N_CORES)))
    out = np.empty_like(x_np)
    for c, (b, t0) in enumerate(_core_ranges()):
        out[b, t0:t0 + TPC] = res.results[c]["xo_fm"].T
    return out



def build_ln_bf16(TOK=TPC, nch=NCH, pp=None, io=None):
    if pp is None:
        p = Prog("lnbf")
        em = p.em
        x = p.din("x_fm", [nch * 128, TOK])
        lnx = p.dout("lnx", [nch * 128, TOK], BF16)
        ps = alloc_psum_pool(p)
        sc = alloc_common(p, nch)
    else:
        p = pp
        em = p.em
        x, lnx = io["x"], io["lnx"]
        ps, sc = io["ps"], io["sc"]
    mark = p.mark()
    xn = p.sbuf("xn", [128, nch, TOK], BF16)
    emit_ln_fm(p, em, ps, sc, x, xn, TOK)
    em.dma("gpsimd", TV(lnx.h.rearrange("(c p) t -> p c t", p=128), lnx.name),
           xn.ap)
    em.fence("sync")
    if pp is not None:
        p.free_since(mark)
    return p


def build_l1_attn(TT=T, nch=NCH, DK=256, CH=128, pp=None, io=None):
    dm = nch * 128
    NCH2 = DK // 128
    NCK = TT // CH
    if pp is None:
        p = Prog("l1attn")
        em = p.em
        hx = p.din("hx", [dm, TT], BF16)
        wq = p.din("wq", [dm, DK])
        wk = p.din("wk", [dm, DK])
        wv = p.din("wv", [dm, DK])
        wb = p.din("wb", [128, nch])
        qcw = p.din("qcw", [128, NCH2, 4])
        kcw = p.din("kcw", [128, NCH2, 4])
        vcw = p.din("vcw", [128, NCH2, 4])
        firs = p.din("firs", [128, NCH2, 3])
        firl = p.din("firl", [128, NCH2, 31])
        gw1 = p.din("gw1", [5120, 512])
        gw1h = p.din("gw1h", [4, 128, 1024])
        gw1b = p.din("gw1b", [4, 128, 1024])
        gb1c = p.din("gb1c", [128, 4])
        gw2 = p.din("gw2", [512, 4])
        gb2 = p.din("gb2", [1, 4])
        temp = p.din("temp", [1, 1])
        epsf = p.din("epsf", [1, 4])
        onwr = p.din("onwr", [1, DK])
        trilm = p.din("cst_trilm", [128, 128], BF16)
        trium = p.din("cst_trium", [128, 128], BF16)
        triuI = p.din("cst_triuI", [128, 128], BF16)
        o_out = p.dout("o_h", [DK, TT], BF16)
        ps = alloc_psum_pool(p)
        sc = alloc_common(p, nch)
    else:
        p = pp
        em = p.em
        (hx, wq, wk, wv, wb, qcw, kcw, vcw, firs, firl, gw1, gw1h, gw1b,
         gb1c, gw2, gb2, temp, epsf, onwr, trilm, trium,
         triuI, o_out) = (io[k] for k in
            ("hx", "wq", "wk", "wv", "wb", "qcw", "kcw", "vcw", "firs",
             "firl", "gw1", "gw1h", "gw1b", "gb1c", "gw2", "gb2", "temp",
             "epsf", "onwr", "trilm", "trium", "triuI", "o_out"))
        ps, sc = io["ps"], io["sc"]
    mark = p.mark()

    br_tm = p.dram("br_tm", [128, NCK, 4, DK], BF16)
    br_fm = p.dram("br_fm", [3, DK, TT], BF16)
    g1r_d = p.dram("g1r_d", [12, 512])

    q_fm = p.sbuf("q_fm", [128, NCH2, TT], BF16)
    k_fm = p.sbuf("k_fm", [128, NCH2, TT], BF16)
    v_fm = p.sbuf("v_fm", [128, NCH2, 32 + TT], BF16)
    tmpf = p.sbuf("tmpf", [128, NCH2, TT], BF16)
    beta_d = p.dram("beta_d", [1, TT])
    bsta = p.sbuf("bsta", [1, 512], F32)
    bst = p.sbuf("bst", [1, 128], F32)
    S_sb = p.sbuf("S_sb", [128, NCH2, DK], F32)
    S_bf = p.sbuf("S_bf", [128, NCH2, DK], BF16)
    YB = [p.sbuf(f"Y_sb{i}", [128, 2 * DK], BF16) for i in range(2)]
    XB = [p.sbuf(f"X_sb{i}", [128, 128], BF16) for i in range(2)]
    XTB = [p.sbuf(f"XT_sb{i}", [128, 128], BF16) for i in range(2)]
    u_sb = p.sbuf("u_sb", [128, DK], BF16)
    WF = [p.sbuf(f"w_fmc{i}", [128, NCH2, 128], BF16) for i in range(2)]
    KT = [p.sbuf(f"k_tmc{i}", [128, DK], BF16) for i in range(2)]
    df_fm = p.sbuf("df_fm", [128, NCH2, 128], BF16)
    at_sb = p.sbuf("at_sb", [128, 128], BF16)
    bb_sb = p.sbuf("bb_sb", [128, 128], F32)
    trilms = p.sbuf("trilms", [128, 128], BF16)
    triums = p.sbuf("triums", [128, 128], BF16)
    triuIs = p.sbuf("triuIs", [128, 128], BF16)
    xnsub = p.sbuf("xnsubA", [128, nch, 512], BF16)
    wstage = p.sbuf("wstageA", [128, nch, 128], BF16)
    em.dma("sync", trilms.ap, trilm.ap)
    em.dma("sync", triums.ap, trium.ap)
    em.dma("sync", triuIs.ap, triuI.ap)

    hr = hx.h.rearrange("(c p) t -> p c t", p=128)

    def project(dst, wsrc, conv_w, pad):
        for mt in range(DK // 128):
            em.dma("gpsimd", TV(wstage.h[:, :, :], wstage.name),
                   TV(wsrc.h.rearrange("(c p) m -> p c m", p=128)
                      [:, :, mt * 128:(mt + 1) * 128], wsrc.name))
            for s0 in range(0, TT, 512):
                pst = ps["P0" if (s0 // 512) % 2 == 0 else "P1"]
                em.dma("sync", xnsub.ap, TV(hr[:, :, s0:s0 + 512], hx.name))
                for c in range(nch):
                    em.mm(pst.ap, TV(wstage.h[:, c, :], wstage.name),
                          TV(xnsub.h[:, c, :], xnsub.name),
                          start=(c == 0), stop=(c == nch - 1))
                em.copy("vector",
                        TV(tmpf.h[:, mt, s0:s0 + 512], tmpf.name), pst.ap)
        for c in range(NCH2):
            if pad:
                em.memset("vector", TV(dst.h[:, c, 0:pad], dst.name), 0.0)
            em.ts("vector", TV(dst.h[:, c, pad:pad + TT], dst.name),
                  TV(tmpf.h[:, c, 0:TT], tmpf.name),
                  TV(conv_w.h[:, c, 3:4], conv_w.name), None, ALU.mult)
            for j in range(1, 4):
                em.stt("vector", TV(dst.h[:, c, pad + j:pad + TT], dst.name),
                       TV(tmpf.h[:, c, 0:TT - j], tmpf.name),
                       TV(conv_w.h[:, c, 3 - j:4 - j], conv_w.name),
                       TV(dst.h[:, c, pad + j:pad + TT], dst.name),
                       ALU.mult, ALU.add)
            em.act(TV(dst.h[:, c, pad:pad + TT], dst.name),
                   TV(dst.h[:, c, pad:pad + TT], dst.name), AF.Silu)

    qcs = p.sbuf("qcs", [128, NCH2, 4], F32)
    kcs = p.sbuf("kcs", [128, NCH2, 4], F32)
    vcs = p.sbuf("vcs", [128, NCH2, 4], F32)
    em.dma("sync", qcs.ap, qcw.ap)
    em.dma("sync", kcs.ap, kcw.ap)
    em.dma("sync", vcs.ap, vcw.ap)
    project(q_fm, wq, qcs, 0)
    project(k_fm, wk, kcs, 0)
    project(v_fm, wv, vcs, 32)
    wbs = p.sbuf("wbs", [128, nch], BF16)
    em.dma("gpsimd", wbs.ap, wb.ap)
    for s0 in range(0, TT, 512):
        em.dma("sync", xnsub.ap, TV(hr[:, :, s0:s0 + 512], hx.name))
        pst = ps["R1"]
        for c in range(nch):
            em.mm(TV(pst.h[0:1, :], pst.name),
                  TV(wbs.h[:, c:c + 1], wbs.name),
                  TV(xnsub.h[:, c, :], xnsub.name),
                  start=(c == 0), stop=(c == nch - 1))
        em.act(bsta.ap, TV(pst.h[0:1, :], pst.name), AF.Sigmoid)
        em.dma("gpsimd", TV(beta_d.h[:, s0:s0 + 512], beta_d.name), bsta.ap)

    def l2n(dst, off):
        for s0 in range(0, TT, 512):
            for c in range(NCH2):
                em.act(TV(sc["xsq"].h[:, :], sc["xsq"].name),
                       TV(dst.h[:, c, off + s0:off + s0 + 512], dst.name),
                       AF.Square)
                em.mm(TV(ps["R2"].h[0:1, :], ps["R2"].name), sc["ones_col"].ap,
                      TV(sc["xsq"].h[:, :], sc["xsq"].name),
                      start=(c == 0), stop=(c == NCH2 - 1))
            rw = lambda t: TV(t.h[:, 0:512], t.name)
            em.ts("vector", rw(sc["r_var"]),
                  TV(ps["R2"].h[0:1, :], ps["R2"].name), 1e-6, None, ALU.add)
            em.act(rw(sc["r_sd"]), rw(sc["r_var"]), AF.Sqrt)
            em.recip(rw(sc["r_rs"]), rw(sc["r_sd"]))
            em.mm(TV(ps["P2"].h[:, :], ps["P2"].name), sc["ones_row"].ap,
                  rw(sc["r_rs"]))
            for c in range(NCH2):
                em.tt("vector", TV(dst.h[:, c, off + s0:off + s0 + 512],
                                   dst.name),
                      TV(dst.h[:, c, off + s0:off + s0 + 512], dst.name),
                      TV(ps["P2"].h[:, :], ps["P2"].name), ALU.mult)
    l2n(q_fm, 0)
    l2n(k_fm, 0)

    def to_tm(src_tile, src_cols, ck, slot):
        for c in range(NCH2):
            em.transpose(TV(ps["PT"].h[:, c * 128:(c + 1) * 128], ps["PT"].name),
                         TV(src_tile.h[:, c, src_cols + ck * 128:
                                       src_cols + (ck + 1) * 128],
                            src_tile.name), sb_ident(sc))
        em.copy("vector", TV(u_sb.h[:, :], u_sb.name),
                TV(ps["PT"].h[:, 0:DK], ps["PT"].name))
        em.dma("gpsimd", TV(br_tm.h[:, ck, slot, :], br_tm.name),
               TV(u_sb.h[:, :], u_sb.name))

    def fir(ntap, fw, slot):
        for c in range(NCH2):
            em.ts("vector", TV(tmpf.h[:, c, :], tmpf.name),
                  TV(v_fm.h[:, c, 32:32 + TT], v_fm.name),
                  TV(fw.h[:, c, ntap - 1:ntap], fw.name), None, ALU.mult)
            for j in range(1, ntap):
                eng = "vector"
                em.stt(eng, TV(tmpf.h[:, c, :], tmpf.name),
                       TV(v_fm.h[:, c, 32 - j:32 + TT - j], v_fm.name),
                       TV(fw.h[:, c, ntap - 1 - j:ntap - j], fw.name),
                       TV(tmpf.h[:, c, :], tmpf.name), ALU.mult, ALU.add)
        em.dma("gpsimd",
               TV(br_fm.h[slot].rearrange("(c p) t -> p c t", p=128),
                  br_fm.name), tmpf.ap)
        for ck in range(NCK):
            to_tm(tmpf, 0, ck, slot)
    firss = p.sbuf("firss", [128, NCH2, 3], F32)
    firls = p.sbuf("firls", [128, NCH2, 31], F32)
    em.dma("sync", firss.ap, firs.ap)
    em.dma("sync", firls.ap, firl.ap)
    fir(3, firss, 0)
    fir(31, firls, 1)
    for ck in range(NCK):
        to_tm(v_fm, 32, ck, 3)

    em.memset("vector", S_sb.ap, 0.0)
    em.memset("vector", S_bf.ap, 0.0)
    kb_c = p.sbuf("kb_c", [128, NCH2, 128], BF16)
    vb_c = p.sbuf("vb_c", [128, NCH2, 128], BF16)
    up_sb = p.sbuf("up_sb", [128, DK], BF16)
    beta_sb = p.sbuf("beta_sb", [1, TT], F32)
    em.dma("sync", beta_sb.ap, beta_d.ap)

    def _prep(ck):
        i = ck % 2
        t0 = ck * 128
        Y_sb, X_sb, XT_sb = YB[i], XB[i], XTB[i]
        em.mm(TV(ps["P2"].h[:, 0:128], ps["P2"].name), sc["ones_row"].ap,
              TV(beta_sb.h[:, t0:t0 + 128], beta_sb.name))
        em.copy("scalar", bb_sb.ap, TV(ps["P2"].h[:, 0:128], ps["P2"].name))
        for c in range(NCH2):
            em.tt("vector", TV(kb_c.h[:, c, :], kb_c.name),
                  TV(k_fm.h[:, c, t0:t0 + 128], k_fm.name),
                  bb_sb.ap, ALU.mult)
            em.tt("vector", TV(vb_c.h[:, c, :], vb_c.name),
                  TV(v_fm.h[:, c, 32 + t0:32 + t0 + 128], v_fm.name),
                  bb_sb.ap, ALU.mult)
        for c in range(NCH2):
            em.transpose(TV(ps["PT"].h[:, c * 128:(c + 1) * 128], ps["PT"].name),
                         TV(vb_c.h[:, c, :], vb_c.name), sb_ident(sc))
            em.transpose(TV(ps["PT"].h[:, 256 + c * 128:256 + (c + 1) * 128],
                            ps["PT"].name),
                         TV(kb_c.h[:, c, :], kb_c.name), sb_ident(sc))
        em.copy("vector", Y_sb.ap, TV(ps["PT"].h[:, 0:512], ps["PT"].name))
        for c in range(NCH2):
            em.mm(TV(ps["P3"].h[:, 0:128], ps["P3"].name),
                  TV(kb_c.h[:, c, :], kb_c.name),
                  TV(k_fm.h[:, c, t0:t0 + 128], k_fm.name),
                  start=(c == 0), stop=(c == NCH2 - 1))
        em.tt("vector", X_sb.ap, TV(ps["P3"].h[:, 0:128], ps["P3"].name),
              trilms.ap, ALU.mult)
        for c in range(NCH2):
            em.mm(TV(ps["P3"].h[:, 0:128], ps["P3"].name),
                  TV(k_fm.h[:, c, t0:t0 + 128], k_fm.name),
                  TV(kb_c.h[:, c, :], kb_c.name),
                  start=(c == 0), stop=(c == NCH2 - 1))
        em.tt("vector", XT_sb.ap, TV(ps["P3"].h[:, 0:128], ps["P3"].name),
              triums.ap, ALU.mult)
        for it in range(7):
            em.mm(TV(ps["P0"].h[:, :], ps["P0"].name), XT_sb.ap, Y_sb.ap)
            em.tt("vector", Y_sb.ap, Y_sb.ap,
                  TV(ps["P0"].h[:, :], ps["P0"].name), ALU.add)
            if it < 6:
                em.mm(TV(ps["P3"].h[:, 0:128], ps["P3"].name), XT_sb.ap,
                      X_sb.ap)
                em.mm(TV(ps["P1"].h[:, 0:128], ps["P1"].name), X_sb.ap,
                      XT_sb.ap)
                em.copy("vector", X_sb.ap,
                        TV(ps["P3"].h[:, 0:128], ps["P3"].name))
                em.copy("vector", XT_sb.ap,
                        TV(ps["P1"].h[:, 0:128], ps["P1"].name))
        for c in range(NCH2):
            em.transpose(TV(ps["PT2"].h[:, c * 128:(c + 1) * 128],
                            ps["PT2"].name),
                         TV(Y_sb.h[:, DK + c * 128:DK + (c + 1) * 128],
                            Y_sb.name), sb_ident(sc))
            em.transpose(TV(ps["PT2"].h[:, 256 + c * 128:256 + (c + 1) * 128],
                            ps["PT2"].name),
                         TV(k_fm.h[:, c, t0:t0 + 128], k_fm.name),
                         sb_ident(sc))
        em.copy("vector", TV(WF[i].h[:, :, :], WF[i].name),
                TV(ps["PT2"].h[:, 0:256].rearrange("p (c t) -> p c t",
                                                   c=NCH2),
                   ps["PT2"].name))
        em.copy("vector", KT[i].ap, TV(ps["PT2"].h[:, 256:512],
                                       ps["PT2"].name))

    def _tail(ck):
        i = ck % 2
        t0 = ck * 128
        Y_sb, w_fmc, k_tmc = YB[i], WF[i], KT[i]
        for c in range(NCH2):
            em.mm(TV(ps["P2"].h[:, 0:DK], ps["P2"].name),
                  TV(w_fmc.h[:, c, :], w_fmc.name),
                  TV(S_bf.h[:, c, :], S_bf.name),
                  start=(c == 0), stop=(c == NCH2 - 1))
        em.tt("vector", up_sb.ap, TV(Y_sb.h[:, 0:DK], Y_sb.name),
              TV(ps["P2"].h[:, 0:DK], ps["P2"].name), ALU.subtract)
        for c in range(NCH2):
            em.mm(TV(ps["P3"].h[:, 0:128], ps["P3"].name),
                  TV(k_fm.h[:, c, t0:t0 + 128], k_fm.name),
                  TV(q_fm.h[:, c, t0:t0 + 128], q_fm.name),
                  start=(c == 0), stop=(c == NCH2 - 1))
        em.tt("vector", at_sb.ap, TV(ps["P3"].h[:, 0:128], ps["P3"].name),
              triuIs.ap, ALU.mult)
        for c in range(NCH2):
            em.mm(TV(ps["P2"].h[:, 0:DK], ps["P2"].name),
                  TV(q_fm.h[:, c, t0:t0 + 128], q_fm.name),
                  TV(S_bf.h[:, c, :], S_bf.name),
                  start=(c == 0), stop=False)
        em.mm(TV(ps["P2"].h[:, 0:DK], ps["P2"].name), at_sb.ap, up_sb.ap,
              start=False, stop=True)
        em.copy("vector", u_sb.ap, TV(ps["P2"].h[:, 0:DK], ps["P2"].name))
        em.dma("gpsimd", TV(br_tm.h[:, ck, 2, :], br_tm.name), u_sb.ap)
        for c in range(NCH2):
            em.transpose(TV(ps["PT"].h[:, 512 + c * 128:512 + (c + 1) * 128],
                            ps["PT"].name),
                         TV(u_sb.h[:, c * 128:(c + 1) * 128], u_sb.name),
                         sb_ident(sc))
        em.copy("vector", TV(df_fm.h[:, :, :], df_fm.name),
                TV(ps["PT"].h[:, 512:768].rearrange("p (c t) -> p c t",
                                                    c=NCH2),
                   ps["PT"].name))
        em.dma("gpsimd",
               TV(br_fm.h[2].rearrange("(c p) t -> p c t", p=128)
                  [:, :, t0:t0 + 128], br_fm.name),
               TV(df_fm.h[:, :, :], df_fm.name))
        for c in range(NCH2):
            em.mm(TV(ps["P2"].h[:, 0:DK], ps["P2"].name),
                  TV(k_tmc.h[:, c * 128:(c + 1) * 128], k_tmc.name),
                  up_sb.ap)
            em.tt("vector", TV(S_sb.h[:, c, :], S_sb.name),
                  TV(S_sb.h[:, c, :], S_sb.name),
                  TV(ps["P2"].h[:, 0:DK], ps["P2"].name), ALU.add)
            em.copy("scalar", TV(S_bf.h[:, c, :], S_bf.name),
                    TV(S_sb.h[:, c, :], S_sb.name))

    _prep(0)
    for ck in range(NCK):
        if ck + 1 < NCK:
            _prep(ck + 1)
        _tail(ck)
    p.free(wstage, qcs, kcs, vcs, wbs, firss, firls, kb_c, vb_c, up_sb,
           beta_sb)

    g1stage = p.sbuf("g1stage", [128, 512], F32)
    for grp in range(12):
        for cc in range(2):
            em.dma("sync", g1stage.ap,
                   TV(gw1.h[1024 + grp * 256 + cc * 128:
                            1024 + grp * 256 + (cc + 1) * 128, :], gw1.name))
            em.mm(TV(ps["R1"].h[0:1, :], ps["R1"].name), sc["ones_col"].ap,
                  g1stage.ap, start=(cc == 0), stop=(cc == 1))
        em.copy("vector", TV(sc["r_mu"].h[:, :], sc["r_mu"].name),
                TV(ps["R1"].h[0:1, :], ps["R1"].name))
        row = (grp % 3) * 4 + grp // 3
        em.dma("gpsimd", TV(g1r_d.h[row:row + 1, :], g1r_d.name),
               TV(sc["r_mu"].h[:, :], sc["r_mu"].name))
    g1r_sb = p.sbuf("g1r_sb", [12, 512], BF16)
    em.dma("gpsimd", g1r_sb.ap, g1r_d.ap)

    tclip = p.sbuf("tclip", [1, 1], F32)
    em.dma("sync", tclip.ap, temp.ap)
    em.ts("vector", tclip.ap, tclip.ap, 0.2, 10.0, ALU.max, ALU.min)
    em.recip(tclip.ap, tclip.ap)
    invt = p.sbuf("invt", [128, 1], F32)
    em.mm(TV(ps["P2"].h[:, 0:1], ps["P2"].name), sc["ones_row"].ap, tclip.ap)
    em.copy("vector", invt.ap, TV(ps["P2"].h[:, 0:1], ps["P2"].name))
    epsc = p.sbuf("epsc", [1, 4], F32)
    em.dma("sync", epsc.ap, epsf.ap)
    em.ts("vector", epsc.ap, epsc.ap, 1e-7, 0.1, ALU.max, ALU.min)
    flrb = p.sbuf("flrb", [128, 4], F32)
    em.mm(TV(ps["P2"].h[:, 0:4], ps["P2"].name), sc["ones_row"].ap, epsc.ap)
    em.copy("vector", flrb.ap, TV(ps["P2"].h[:, 0:4], ps["P2"].name))
    gb2b = p.sbuf("gb2b", [128, 4], F32)
    gb2s = p.sbuf("gb2s", [1, 4], F32)
    em.dma("sync", gb2s.ap, gb2.ap)
    em.mm(TV(ps["P2"].h[:, 0:4], ps["P2"].name), sc["ones_row"].ap, gb2s.ap)
    em.copy("vector", gb2b.ap, TV(ps["P2"].h[:, 0:4], ps["P2"].name))
    onws = p.sbuf("onws", [1, DK], F32)
    em.dma("sync", onws.ap, onwr.ap)
    onwb = p.sbuf("onwb", [128, DK], F32)
    em.mm(TV(ps["P2"].h[:, 0:DK], ps["P2"].name), sc["ones_row"].ap, onws.ap)
    em.copy("vector", onwb.ap, TV(ps["P2"].h[:, 0:DK], ps["P2"].name))
    gb1s = p.sbuf("gb1s", [128, 4], F32)
    em.dma("sync", gb1s.ap, gb1c.ap)
    gw2s = p.sbuf("gw2s", [128, 4, 4], BF16)
    em.dma("gpsimd", gw2s.ap,
           TV(gw2.h.rearrange("(c p) e -> p c e", p=128), gw2.name))

    GRP = 512
    NG = GRP // 128
    brt_s = p.sbuf("brt_s", [128, NG, 4, DK], BF16)
    stat_s = p.sbuf("stat_s", [128, NG, 12], F32)
    gsq = p.sbuf("gsq", [128, NG * DK], BF16)
    statf = p.sbuf("statf", [12, GRP], BF16)
    stat_bf = p.sbuf("stat_bf", [128, NG, 12], BF16)
    hmid = p.sbuf("hmid", [128, 4, GRP], BF16)
    lg_f = p.sbuf("lg_f", [4, GRP], BF16)
    lgt_s = p.sbuf("lgt_s", [128, NG, 4], F32)
    mxg = p.sbuf("mxg", [128, NG], F32)
    smg = p.sbuf("smg", [128, NG], F32)
    wgt_s = p.sbuf("wgt_s", [128, NG, 4], F32)
    o_tmS = p.sbuf("o_tmS", [128, NG, DK], F32)
    rr = p.sbuf("rr", [128, NG], F32)
    g1w = p.sbuf("g1w", [128, 2, nch, 128], BF16)
    brf_s = p.sbuf("brf_s", [128, 3, NCH2, GRP], BF16)
    orr = o_out.h.rearrange("(c p) t -> p c t", p=128)
    for gi in range(TT // GRP):
        t0 = gi * GRP
        em.dma("sync", TV(brt_s.h[:].rearrange("p g b d -> p (g b) d"),
                          brt_s.name),
               TV(br_tm.h[:, gi * NG:(gi + 1) * NG, :, :]
                  .rearrange("t ck b d -> t (ck b) d"), br_tm.name))
        bv = TV(brt_s.h[:].rearrange("p g b d -> p (g b) d"), brt_s.name)
        em.reduce("vector", TV(stat_s.h[:, :, 0:4], stat_s.name), bv, ALU.add)
        for b_ in range(4):
            bslice = TV(brt_s.h[:, :, b_, :], brt_s.name)
            em.tt("vector", TV(gsq.h[:].rearrange("p (g d) -> p g d", d=DK),
                               gsq.name), bslice, bslice, ALU.mult)
            em.reduce("vector", TV(stat_s.h[:, :, 4 + b_:5 + b_], stat_s.name),
                      TV(gsq.h[:].rearrange("p (g d) -> p g d", d=DK),
                         gsq.name), ALU.add)
        em.reduce("vector", TV(stat_s.h[:, :, 8:12], stat_s.name), bv, ALU.max)
        sv = lambda sl: TV(stat_s.h[:, :, sl], stat_s.name)
        em.ts("vector", sv(slice(0, 4)), sv(slice(0, 4)), 1.0 / DK, None,
              ALU.mult)
        em.ts("vector", sv(slice(4, 8)), sv(slice(4, 8)), 1.0 / DK, None,
              ALU.mult)
        em.ts("vector", sv(slice(4, 8)), sv(slice(4, 8)), 1e-8, None, ALU.max)
        em.act(sv(slice(4, 8)), sv(slice(4, 8)), AF.Sqrt)
        em.copy("scalar", stat_bf.ap, stat_s.ap)
        for g in range(NG):
            em.transpose(TV(ps["PT"].h[0:12, g * 128:(g + 1) * 128],
                            ps["PT"].name),
                         TV(stat_bf.h[:, g, :], stat_bf.name), sb_ident(sc))
        em.copy("vector", statf.ap, TV(ps["PT"].h[0:12, 0:GRP], ps["PT"].name))
        em.dma("sync", xnsub.ap, TV(hr[:, :, t0:t0 + GRP], hx.name))
        em.dma("sync", TV(brf_s.h[:].rearrange("p s c t -> p (s c) t"),
                          brf_s.name),
               TV(br_fm.h.rearrange("s (c p) t -> p (s c) t", p=128)
                  [:, :, t0:t0 + GRP], br_fm.name))
        for m4 in range(4):
            slot = m4 % 2
            em.dma("gpsimd",
                   TV(g1w.h[:, slot].rearrange("p c m -> p (c m)"),
                      f"g1w.{slot}"),
                   TV(gw1h.h[m4], gw1h.name))
            pst = ps["P0"]
            for c in range(nch):
                em.mm(TV(pst.h[:, 0:GRP], pst.name),
                      TV(g1w.h[:, slot, c, :], f"g1w.{slot}"),
                      TV(xnsub.h[:, c, :], xnsub.name),
                      start=(c == 0), stop=False)
            em.mm(TV(pst.h[:, 0:GRP], pst.name),
                  TV(g1r_sb.h[:, m4 * 128:(m4 + 1) * 128], g1r_sb.name),
                  statf.ap, start=False, stop=False)
            em.dma("gpsimd",
                   TV(g1w.h[:, slot].rearrange("p c m -> p (c m)"),
                      f"g1w.{slot}"),
                   TV(gw1b.h[m4], gw1b.name))
            for b in range(4):
                for c in range(NCH2):
                    if b < 3:
                        rhs = TV(brf_s.h[:, b, c, :], brf_s.name)
                    else:
                        rhs = TV(v_fm.h[:, c, 32 + t0:32 + t0 + GRP], v_fm.name)
                    em.mm(TV(pst.h[:, 0:GRP], pst.name),
                          TV(g1w.h[:, slot, b * NCH2 + c, :], f"g1w.{slot}"),
                          rhs, start=False,
                          stop=(b == 3 and c == NCH2 - 1))
            em.act(TV(hmid.h[:, m4, :], hmid.name), pst.ap, AF.Gelu,
                   bias=TV(gb1s.h[:, m4:m4 + 1], gb1s.name))
        pst = ps["P3"]
        for m4 in range(4):
            em.mm(TV(pst.h[0:4, 0:GRP], pst.name),
                  TV(gw2s.h[:, m4, :], gw2s.name),
                  TV(hmid.h[:, m4, :], hmid.name),
                  start=(m4 == 0), stop=(m4 == 3))
        em.copy("vector", lg_f.ap, TV(pst.h[0:4, 0:GRP], pst.name))
        for g in range(NG):
            em.transpose(TV(ps["PT"].h[0:128, g * 4:(g + 1) * 4], ps["PT"].name),
                         TV(lg_f.h[:, g * 128:(g + 1) * 128], lg_f.name),
                         TV(sc["ident_bf"].h[0:4, 0:4], sc["ident_bf"].name))
        em.copy("vector", TV(lgt_s.h[:, :, :], lgt_s.name),
                TV(ps["PT"].h[0:128, 0:NG * 4], ps["PT"].name))
        em.tt("vector", lgt_s.ap, lgt_s.ap,
              TV(_expand_mid(gb2b.h[:], NG), gb2b.name), ALU.add)
        em.ts("vector", lgt_s.ap, lgt_s.ap, invt.ap, None, ALU.mult)
        em.reduce("vector", mxg.ap, lgt_s.ap, ALU.max)
        em.tt("vector", lgt_s.ap, lgt_s.ap, bcast(mxg.ap, 4), ALU.subtract)
        em.act(TV(wgt_s.h[:, :, :], wgt_s.name), lgt_s.ap, AF.Exp)
        em.reduce("vector", smg.ap, wgt_s.ap, ALU.add)
        em.recip(smg.ap, smg.ap)
        em.tt("vector", wgt_s.ap, wgt_s.ap, bcast(smg.ap, 4), ALU.mult)
        em.tt("vector", wgt_s.ap, wgt_s.ap,
              TV(_expand_mid(flrb.h[:], NG), flrb.name), ALU.max)
        em.reduce("vector", smg.ap, wgt_s.ap, ALU.add)
        em.recip(smg.ap, smg.ap)
        em.tt("vector", wgt_s.ap, wgt_s.ap, bcast(smg.ap, 4), ALU.mult)
        for g in range(NG):
            for b in range(4):
                wcol = TV(wgt_s.h[:, g, b:b + 1], wgt_s.name)
                if b == 0:
                    em.ts("vector", TV(o_tmS.h[:, g, :], o_tmS.name),
                          TV(brt_s.h[:, g, 0, :], brt_s.name), wcol, None,
                          ALU.mult)
                else:
                    em.stt("vector", TV(o_tmS.h[:, g, :], o_tmS.name),
                           TV(brt_s.h[:, g, b, :], brt_s.name), wcol,
                           TV(o_tmS.h[:, g, :], o_tmS.name), ALU.mult, ALU.add)
        em.tt("vector", TV(gsq.h[:, :], gsq.name),
              TV(o_tmS.h[:].rearrange("p g d -> p (g d)"), o_tmS.name),
              TV(o_tmS.h[:].rearrange("p g d -> p (g d)"), o_tmS.name),
              ALU.mult)
        em.reduce("vector", rr.ap,
                  TV(gsq.h[:].rearrange("p (g d) -> p g d", d=DK),
                     gsq.name), ALU.add)
        em.ts("vector", rr.ap, rr.ap, 1.0 / DK, 1e-5, ALU.mult, ALU.add)
        em.act(rr.ap, rr.ap, AF.Sqrt)
        em.recip(rr.ap, rr.ap)
        for g in range(NG):
            em.ts("vector", TV(o_tmS.h[:, g, :], o_tmS.name),
                  TV(o_tmS.h[:, g, :], o_tmS.name),
                  TV(rr.h[:, g:g + 1], rr.name), None, ALU.mult)
        em.tt("vector", TV(o_tmS.h[:].rearrange("p g d -> p (g d)"),
                           o_tmS.name),
              TV(o_tmS.h[:].rearrange("p g d -> p (g d)"), o_tmS.name),
              TV(_expand_mid(onwb.h[:], NG), onwb.name), ALU.mult)
        for g in range(NG):
            em.copy("vector", TV(u_sb.h[:, :], u_sb.name),
                    TV(o_tmS.h[:, g, :], o_tmS.name))
            for c in range(NCH2):
                em.transpose(TV(ps["PT"].h[:, c * 128:(c + 1) * 128],
                                ps["PT"].name),
                             TV(u_sb.h[:, c * 128:(c + 1) * 128], u_sb.name),
                             sb_ident(sc))
            em.copy("vector", TV(df_fm.h[:, :, :], df_fm.name),
                    TV(ps["PT"].h[:, 0:256].rearrange("p (c t) -> p c t",
                                                      c=NCH2), ps["PT"].name))
            em.dma("gpsimd",
                   TV(orr[:, :, t0 + g * 128:t0 + (g + 1) * 128], o_out.name),
                   TV(df_fm.h[:, :, :], df_fm.name))
    em.fence("sync")
    if pp is not None:
        p.free_since(mark)
    return p


def _expand_mid(ap, n):
    from concourse.ap import AP as _AP
    return _AP(ap.tensor, ap.offset,
               [list(ap.ap[0]), [0, n], list(ap.ap[1])])


def build_wo_residual(TOK=TPC, nch=NCH, pp=None, io=None):
    if pp is None:
        p = Prog("wores")
        em = p.em
        x = p.din("x_fm", [nch * 128, TOK])
        o = p.din("o_fm", [nch * 128, TOK], BF16)
        wo = p.din("wo", [nch * 128, nch * 128])
        xo = p.dout("xo_fm", [nch * 128, TOK])
        ps = alloc_psum_pool(p)
    else:
        p = pp
        em = p.em
        x, o, wo, xo = io["x"], io["o"], io["wo"], io["xo"]
        ps = io["ps"]
    mark = p.mark()
    o_sb = p.sbuf("o_sbW", [128, nch, TOK], BF16)
    em.dma("gpsimd", o_sb.ap,
           TV(o.h.rearrange("(c p) t -> p c t", p=128), o.name))
    wbuf = p.sbuf("wbufW", [128, 2, nch, 128], BF16)
    xres = p.sbuf("xresW", [128, 512], F32)
    xout = p.sbuf("xoutW", [128, 512], F32)
    xr = x.h.rearrange("(c p) t -> p c t", p=128)
    orr = xo.h.rearrange("(c p) t -> p c t", p=128)
    for m in range(nch):
        em.dma("gpsimd", TV(wbuf.h[:, m % 2], f"wbufW.{m % 2}"),
               TV(wo.h.rearrange("(c p) m -> p c m", p=128)
                  [:, :, m * 128:(m + 1) * 128], wo.name))
        for s0 in range(0, TOK, 512):
            pst = ps["P0" if (s0 // 512) % 2 == 0 else "P1"]
            for c in range(nch):
                em.mm(pst.ap, TV(wbuf.h[:, m % 2, c, :], f"wbufW.{m % 2}"),
                      TV(o_sb.h[:, c, s0:s0 + 512], o_sb.name),
                      start=(c == 0), stop=(c == nch - 1))
            em.dma("sync", xres.ap, TV(xr[:, m, s0:s0 + 512], x.name))
            em.tt("vector", xout.ap, pst.ap, xres.ap, ALU.add)
            em.dma("gpsimd", TV(orr[:, m, s0:s0 + 512], xo.name), xout.ap)
    em.fence("sync")
    if pp is not None:
        p.free_since(mark)
    return p


def _fmcols(v):
    n = v.shape[0] // 128
    return np.ascontiguousarray(v.reshape(n, 128, -1).transpose(1, 0, 2))


def host_l1_attn_core_inputs(lnx_b, b, h, Wq, Wk, Wv, Wb, qc, kc, vc,
                             fs, fl, gW1, gb1, gW2, gb2, temp, epsf, onw, DK=256):
    sl = slice(h * DK, (h + 1) * DK)
    tril_m = (-np.tril(np.ones((128, 128), np.float32), -1)).astype(_bf16())
    triu_m = (-np.triu(np.ones((128, 128), np.float32), 1)).astype(_bf16())
    triu_i = np.triu(np.ones((128, 128), np.float32), 0).astype(_bf16())
    ins = {
        "hx": lnx_b,
        "wq": np.ascontiguousarray(Wq[:, sl]),
        "wk": np.ascontiguousarray(Wk[:, sl]),
        "wv": np.ascontiguousarray(Wv[:, sl]),
        "wb": np.ascontiguousarray(Wb[:, h].reshape(8, 128).T),
        "qcw": _fmcols(qc[sl]), "kcw": _fmcols(kc[sl]), "vcw": _fmcols(vc[sl]),
        "firs": _fmcols(fs[h].reshape(DK, 3)),
        "firl": _fmcols(fl[h].reshape(DK, 31)),
        "gw1": gW1,
        "gw1h": tile_w(gW1[0:1024]),
        "gw1b": tile_w(gW1[4096:5120]),
        "gb1c": np.ascontiguousarray(gb1.reshape(4, 128).T),
        "gw2": gW2, "gb2": gb2.reshape(1, 4),
        "temp": np.array([[temp[h]]], np.float32),
        "epsf": epsf[h].reshape(1, 4),
        "onwr": onw.reshape(1, DK),
        "cst_trilm": tril_m, "cst_trium": triu_m, "cst_triuI": triu_i,
    }
    ins.update(host_common_inputs())
    return ins


def run_phase_l1_attn(x_np, Wq, Wk, Wv, Wb, qc, kc, vc, fs, fl,
                      gW1, gb1, gW2, gb2, temp, epsf, onw, Wo):
    pl, ncl = _get_prog("lnbf", build_ln_bf16)
    pa, nca = _get_prog("l1attn", build_l1_attn)
    pw, ncw = _get_prog("wores", build_wo_residual)
    com = host_common_inputs()
    in_maps = []
    for b, t0 in _core_ranges():
        m = {"x_fm": np.ascontiguousarray(x_np[b, t0:t0 + TPC].T)}
        m.update(com)
        in_maps.append(m)
    res = run_bass_kernel_spmd(ncl, in_maps, core_ids=list(range(N_CORES)))
    lnx = [np.empty((D, T), res.results[0]["lnx"].dtype) for _ in range(B)]
    for c, (b, t0) in enumerate(_core_ranges()):
        lnx[b][:, t0:t0 + TPC] = res.results[c]["lnx"]
    in_maps = []
    for core in range(N_CORES):
        b, h = core // 4, core % 4
        in_maps.append(host_l1_attn_core_inputs(
            lnx[b], b, h, Wq, Wk, Wv, Wb, qc, kc, vc, fs, fl,
            gW1, gb1, gW2, gb2, temp, epsf, onw))
    res = run_bass_kernel_spmd(nca, in_maps, core_ids=list(range(N_CORES)))
    o_all = [np.concatenate([res.results[b * 4 + h]["o_h"] for h in range(4)],
                            axis=0) for b in range(B)]
    in_maps = []
    for b, t0 in _core_ranges():
        in_maps.append({"x_fm": np.ascontiguousarray(x_np[b, t0:t0 + TPC].T),
                        "o_fm": np.ascontiguousarray(o_all[b][:, t0:t0 + TPC]),
                        "wo": Wo})
    res = run_bass_kernel_spmd(ncw, in_maps, core_ids=list(range(N_CORES)))
    out = np.empty_like(x_np)
    for c, (b, t0) in enumerate(_core_ranges()):
        out[b, t0:t0 + TPC] = res.results[c]["xo_fm"].T
    return out



def kernel(L0_attn_x, L0_attn_w_qkv, L0_attn_w_out,
           L0_ffn_rw1, L0_ffn_rb1, L0_ffn_rw2, L0_ffn_rb2,
           L0_ffn_ew1, L0_ffn_eb1, L0_ffn_ew2, L0_ffn_eb2,
           L1_attn_Wq, L1_attn_Wk, L1_attn_Wv, L1_attn_Wb,
           L1_attn_qc_w, L1_attn_kc_w, L1_attn_vc_w,
           L1_attn_fir_s, L1_attn_fir_l,
           L1_attn_gW1, L1_attn_gb1, L1_attn_gW2, L1_attn_gb2,
           L1_attn_temp, L1_attn_eps_floor, L1_attn_onorm_w, L1_attn_Wo,
           L1_ffn_Wqkv, L1_ffn_q_scale, L1_ffn_k_scale, L1_ffn_Wo,
           L1_ffn_ln_g, L1_ffn_ln_b, L1_ffn_W1, L1_ffn_b1,
           L1_ffn_W2, L1_ffn_b2):
    _I = dict(L0_attn_x=L0_attn_x, L0_attn_w_qkv=L0_attn_w_qkv,
              L0_attn_w_out=L0_attn_w_out, L0_ffn_rw1=L0_ffn_rw1,
              L0_ffn_rb1=L0_ffn_rb1, L0_ffn_rw2=L0_ffn_rw2,
              L0_ffn_rb2=L0_ffn_rb2, L0_ffn_ew1=L0_ffn_ew1,
              L0_ffn_eb1=L0_ffn_eb1, L0_ffn_ew2=L0_ffn_ew2,
              L0_ffn_eb2=L0_ffn_eb2, L1_attn_Wq=L1_attn_Wq,
              L1_attn_Wk=L1_attn_Wk, L1_attn_Wv=L1_attn_Wv,
              L1_attn_Wb=L1_attn_Wb, L1_attn_qc_w=L1_attn_qc_w,
              L1_attn_kc_w=L1_attn_kc_w, L1_attn_vc_w=L1_attn_vc_w,
              L1_attn_fir_s=L1_attn_fir_s, L1_attn_fir_l=L1_attn_fir_l,
              L1_attn_gW1=L1_attn_gW1, L1_attn_gb1=L1_attn_gb1,
              L1_attn_gW2=L1_attn_gW2, L1_attn_gb2=L1_attn_gb2,
              L1_attn_temp=L1_attn_temp, L1_attn_eps_floor=L1_attn_eps_floor,
              L1_attn_onorm_w=L1_attn_onorm_w, L1_attn_Wo=L1_attn_Wo,
              L1_ffn_Wqkv=L1_ffn_Wqkv, L1_ffn_q_scale=L1_ffn_q_scale,
              L1_ffn_k_scale=L1_ffn_k_scale, L1_ffn_Wo=L1_ffn_Wo,
              L1_ffn_ln_g=L1_ffn_ln_g, L1_ffn_ln_b=L1_ffn_ln_b,
              L1_ffn_W1=L1_ffn_W1, L1_ffn_b1=L1_ffn_b1, L1_ffn_W2=L1_ffn_W2,
              L1_ffn_b2=L1_ffn_b2)
    _I = {k: np.asarray(v) for k, v in _I.items()}
    import os
    if os.environ.get("KERNEL_UNFUSED", "0") != "1":
        return kernel_fused(_I)
    x = np.ascontiguousarray(np.asarray(L0_attn_x, dtype=np.float32))

    def l0ffn(x):
        return run_phase_l0_ffn(x, L0_ffn_rw1, L0_ffn_rb1, L0_ffn_rw2,
                                L0_ffn_rb2, L0_ffn_ew1, L0_ffn_eb1,
                                L0_ffn_ew2, L0_ffn_eb2)

    def l1ffn(x):
        return run_phase_l1_ffn(x, L1_ffn_Wqkv, L1_ffn_q_scale,
                                L1_ffn_k_scale, L1_ffn_Wo, L1_ffn_ln_g,
                                L1_ffn_ln_b, L1_ffn_W1, L1_ffn_b1,
                                L1_ffn_W2, L1_ffn_b2)

    x = l0ffn(x)
    x = run_phase_l0_attn(x, L0_attn_w_qkv, L0_attn_w_out)
    x = l0ffn(x)
    x = l1ffn(x)
    x = run_phase_l1_attn(x, L1_attn_Wq, L1_attn_Wk, L1_attn_Wv, L1_attn_Wb,
                          L1_attn_qc_w, L1_attn_kc_w, L1_attn_vc_w,
                          L1_attn_fir_s, L1_attn_fir_l,
                          L1_attn_gW1, L1_attn_gb1, L1_attn_gW2, L1_attn_gb2,
                          L1_attn_temp, L1_attn_eps_floor, L1_attn_onorm_w,
                          L1_attn_Wo)
    x = l1ffn(x)
    return x



RG8 = [list(range(8))]
RG4 = [[0, 1, 2, 3], [4, 5, 6, 7]]


def emit_axpy(p, em, x_d, y_d, alpha, TOK=TPC, nch=NCH, ydtype=BF16):
    mark = p.mark()
    xa = p.sbuf("ax_xa", [128, 512], F32)
    ya = p.sbuf("ax_ya", [128, 512], ydtype)
    oa = p.sbuf("ax_oa", [128, 512], F32)
    xr = x_d.h.rearrange("(c p) t -> p c t", p=128)
    yr = y_d.h.rearrange("(c p) t -> p c t", p=128)
    for c in range(nch):
        for s0 in range(0, TOK, 512):
            em.dma("sync", xa.ap, TV(xr[:, c, s0:s0 + 512], x_d.name))
            em.dma("sync", ya.ap, TV(yr[:, c, s0:s0 + 512], y_d.name))
            em.stt("vector", oa.ap, ya.ap, alpha, xa.ap, ALU.mult, ALU.add)
            em.dma("gpsimd", TV(xr[:, c, s0:s0 + 512], x_d.name), oa.ap)
    p.free_since(mark)


def emit_halo_exchange(p, em, src_d, W, dtype, idx_prev, halo_on, tag,
                       src_cast=None):
    mark = p.mark()
    tail_d = p.dram("tail_h", [1024, W], dtype)
    ag_d = p.dram("agt_h", [8 * 1024, W], dtype)
    halo_d = p.dram("halo_h", [1024, W], dtype)
    if src_cast is None:
        src_cast = dtype
    em.dma("gpsimd", tail_d.ap,
           TV(src_d.h[:, TPC - W:TPC], src_d.name))
    em.collective("AllGather", ALU.bypass, RG8, [tail_d.ap], [ag_d.ap])
    stage = p.sbuf("hstage", [128, W], dtype)
    for k in range(8):
        em.indirect(stage.ap, None, ag_d.ap,
                    TV(idx_prev.h[:, k:k + 1], idx_prev.name))
        em.ts("vector", stage.ap, stage.ap,
              TV(halo_on.h[:, 0:1], halo_on.name), None, ALU.mult)
        em.dma("gpsimd", TV(halo_d.h[k * 128:(k + 1) * 128, :], halo_d.name),
               stage.ap)
    p.free_since(mark)
    return halo_d


def build_fused(nch=NCH):
    dm = nch * 128
    p = Prog("fused")
    em = p.em
    X = p.din("x_in", [dm, TPC])
    xo_out = p.dout("x_out", [dm, TPC])
    rw1 = p.din("rw1", [32, 128, dm])
    rb1c = p.din("rb1c", [128, 32])
    rw2 = p.din("rw2", [4096, 8])
    rb2 = p.din("rb2", [1, 8])
    ew1 = p.din("ew1", [32, 128, dm])
    eb1c = p.din("eb1c", [128, 32])
    ew2x = p.din("ew2x", [8, 128, 4096])
    eb2r = p.din("eb2r", [1, dm])
    esel = p.din("esel", [1, 8])
    ones512 = p.din("cst_ones512", [1, 512])
    tril_s = p.din("cst_trilstrict", [128, 128])
    l0_wqkv = p.din("l0_wqkv", [16, 128, dm])
    l0_vw = p.din("l0_vw", [128, 8, dm])
    l0_wout = p.din("l0_wout", [8, 128, dm])
    l0_ropeA = p.din("l0_ropeA", [2, 128, 512 + TPC])
    l0_ropeB = p.din("l0_ropeB", [2, 128, 512 + TPC])
    l0_maskT = p.din("l0_maskT", [2, 8, 128, 640], BF16)
    w6 = {}
    w6["wqkv"] = p.din("l1_wqkv6", [6, 16, 128, dm])
    w6["vw_in"] = p.din("l1_vw6", [6, 128, 8, dm])
    w6["wo"] = p.din("l1_wo6", [6, 8, 128, dm])
    for nm in ("qscol", "qspcol", "kscol", "kspcol"):
        w6[nm] = p.din("l1_" + nm + "6", [6, 128, 1])
    w6["ln_g"] = p.din("l1_lng6", [6, 128, nch])
    w6["ln_b"] = p.din("l1_lnb6", [6, 128, nch])
    w6["w1"] = p.din("l1_w16", [6, 44, 128, dm])
    w6["b1c"] = p.din("l1_b1c6", [6, 128, 44])
    w6["w2"] = p.din("l1_w26", [6, 8, 128, 2816])
    w6["b2c"] = p.din("l1_b2c6", [6, 128, nch])
    l1_ropes = {nm: p.din("l1_rope" + nm, [128, 256 + 128 + TPC])
                for nm in ("Aq", "Bq", "Ak", "Bk")}
    l1_maskT = p.din("l1_maskT", [TPC // 128 + 1, 128, 384], BF16)
    cst_blk2 = p.din("cst_blk2", [128, 2])
    cst_e2 = p.din("cst_e2", [2, 128])
    A = {}
    A["wq"] = p.din("a_wq", [dm, 256])
    A["wk"] = p.din("a_wk", [dm, 256])
    A["wv"] = p.din("a_wv", [dm, 256])
    A["wb"] = p.din("a_wb", [128, nch])
    for nm in ("qcw", "kcw", "vcw"):
        A[nm] = p.din("a_" + nm, [128, 2, 4])
    A["firs"] = p.din("a_firs", [128, 2, 3])
    A["firl"] = p.din("a_firl", [128, 2, 31])
    A["gw1"] = p.din("a_gw1", [5120, 512])
    A["gw1h"] = p.din("a_gw1h", [4, 128, 1024])
    A["gw1b"] = p.din("a_gw1b", [4, 128, 1024])
    A["gb1c"] = p.din("a_gb1c", [128, 4])
    A["gw2"] = p.din("a_gw2", [512, 4])
    A["gb2"] = p.din("a_gb2", [1, 4])
    A["temp"] = p.din("a_temp", [1, 1])
    A["epsf"] = p.din("a_epsf", [1, 4])
    A["onwr"] = p.din("a_onwr", [1, 256])
    A["trilm"] = p.din("cst_trilm", [128, 128], BF16)
    A["trium"] = p.din("cst_trium", [128, 128], BF16)
    A["triuI"] = p.din("cst_triuI", [128, 128], BF16)
    a_wo_slice = p.din("a_wo_slice", [256, dm])
    l1a_wo_full_unused = None
    idx_prev = p.din("idx_prev", [128, 8], I32)
    halo_on = p.din("halo_on", [128, 1])
    tokid_in = p.din("cst_tokid", [128, 128], I32)
    iota8_in = p.din("cst_iota8", [1, 8])
    identf_in = p.din("cst_identf", [128, 128])
    dumpc_in = p.din("cst_dumpc", [128, 256])
    cons_idx = p.din("cons_idx", [16, 4], I32)

    ps = alloc_psum_pool(p)
    sc = alloc_common(p, nch)
    idxp_sb = p.sbuf("idxp_sb", [128, 8], I32)
    em.dma("sync", idxp_sb.ap, idx_prev.ap)
    hon_sb = p.sbuf("hon_sb", [128, 1], F32)
    em.dma("sync", hon_sb.ap, halo_on.ap)
    idf_sb = p.sbuf("idf_sb", [128, 128], F32)
    em.dma("sync", idf_sb.ap, identf_in.ap)
    ci_sb = p.sbuf("ci_sb", [16, 4], I32)
    em.dma("sync", ci_sb.ap, cons_idx.ap)

    x_d = p.dram("x_d", [dm, TPC])
    em.dma("sync", x_d.ap, X.ap)
    logits_d = p.dram("logits_d", [TPC, 8])
    logits_ag = p.dram("logits_ag", [8 * TPC, 8])
    lnx_tm_d = p.dram("lnx_tm_d", [TPC, dm], BF16)
    lnx_tm_ag = p.dram("lnx_tm_ag", [8 * TPC, dm], BF16)
    y_tm_d = p.dram("y_tm_d", [4096, dm], BF16)
    y_tm_ag = p.dram("y_tm_ag", [8 * 4096, dm], BF16)
    ymine_tm_d = p.dram("ymine_tm_d", [TPC, dm], BF16)
    dense_l0f = os.environ.get("L0F_DENSE", "0") == "1"
    if dense_l0f:
        lnx_d = p.dram("lnx_d", [dm, TPC], BF16)
        lnx_ag = p.dram("lnx_ag", [8, dm, TPC], BF16)
        y_blocks = p.dram("y_blocks", [8, dm, TPC], BF16)
        ymine_d = p.dram("ymine_d", [dm, TPC], BF16)

    def l0_ffn_phase():
        em.fence("gpsimd", streams=[em._qname(y_tm_ag.name),
                                    em._qname(lnx_tm_ag.name),
                                    "q|cn_pa"])
        if dense_l0f:
            build_l0f_router(pp=p, io=dict(
                x=x_d, rw1=rw1, rb1c=rb1c, rw2=rw2, rb2=rb2,
                lnx=lnx_d, logits=logits_d, ps=ps, sc=sc))
            em.collective("AllGather", ALU.bypass, RG8, [lnx_d.ap],
                          [TV(lnx_ag.h.rearrange("s d t -> (s d) t"),
                              lnx_ag.name)])
            em.collective("AllGather", ALU.bypass, RG8, [logits_d.ap],
                          [logits_ag.ap])
            build_l0f_expert(pp=p, io=dict(
                logits_all=logits_ag, ew1=ew1, eb1c=eb1c, ew2x=ew2x,
                eb2r=eb2r, esel=esel, ones512=ones512, lnx_blocks=lnx_ag,
                y_blocks=y_blocks, trils=tril_s, ps=ps, sc=sc))
            em.collective("ReduceScatter", ALU.add, RG8,
                          [TV(y_blocks.h.rearrange("s d t -> (s d) t"),
                              y_blocks.name)],
                          [ymine_d.ap])
            emit_axpy(p, em, x_d, ymine_d, 0.5)
            return
        build_l0f_router(pp=p, io=dict(
            x=x_d, rw1=rw1, rb1c=rb1c, rw2=rw2, rb2=rb2,
            lnx=None, lnx_tm=lnx_tm_d, logits=logits_d, ps=ps, sc=sc))
        em.fence("gpsimd", streams=[em._qname(lnx_tm_d.name)])
        em.collective("AllGather", ALU.bypass, RG8, [lnx_tm_d.ap],
                      [lnx_tm_ag.ap])
        em.collective("AllGather", ALU.bypass, RG8, [logits_d.ap],
                      [logits_ag.ap])
        cons_d = emit_l0f_sparse(p, em, ps, sc, io=dict(
            logits_all=logits_ag, ew1=ew1, eb1c=eb1c, ew2x=ew2x, eb2r=eb2r,
            esel=esel, ones512=ones512, trils=tril_s, tokid=tokid_in,
            iota8=iota8_in, lnx_tm_ag=lnx_tm_ag, y_tm=y_tm_d,
            dumpc=dumpc_in))
        em.fence("gpsimd", streams=[em._qname(y_tm_d.name)])
        em.collective("AllGather", ALU.bypass, RG8, [y_tm_d.ap],
                      [y_tm_ag.ap])
        emit_l0f_consume(p, em, ps, sc, io=dict(
            cons_d=cons_d, y_tm_ag=y_tm_ag, x=x_d, idf_sb=idf_sb,
            cons_idx=ci_sb, ymine_tm=ymine_tm_d))

    l0_ffn_phase()
    halo = emit_halo_exchange(p, em, x_d, 512, F32, idxp_sb, hon_sb, "l0a")
    xfull_d = p.dram("xfull", [dm, 512 + TPC])
    em.dma("sync", TV(xfull_d.h[:, 0:512], xfull_d.name), halo.ap)
    em.dma("sync", TV(xfull_d.h[:, 512:512 + TPC], xfull_d.name), x_d.ap)
    build_l0_attn(pp=p, io=dict(
        x=xfull_d, wqkv=l0_wqkv, vw_in=l0_vw, wout=l0_wout, ropeA=l0_ropeA,
        ropeB=l0_ropeB, maskT=l0_maskT, xo=x_d, ps=ps, sc=sc))
    l0_ffn_phase()

    h_d = p.dram("h_d", [dm, TPC])
    hfull_d = p.dram("hfull", [dm, 512 + TPC])

    def l1_ffn_phase():
        build_l1_prep(pp=p, io=dict(x=x_d, h0=h_d, ps=ps, sc=sc))
        for i in range(6):
            hb = emit_halo_exchange(p, em, h_d, 512, BF16, idxp_sb, hon_sb,
                                    f"l1f{i}")
            em.dma("gpsimd", TV(hfull_d.h[:, 0:512], hfull_d.name), hb.ap)
            em.dma("sync", TV(hfull_d.h[:, 512:512 + TPC], hfull_d.name),
                   h_d.ap)
            io = dict(h_in=hfull_d, h_out=h_d, ps=ps, sc=sc,
                      cst_blk2=cst_blk2, cst_e2=cst_e2,
                      maskT=l1_maskT)
            for nm in ("wqkv", "vw_in", "wo", "qscol", "qspcol", "kscol",
                       "kspcol", "ln_g", "ln_b", "w1", "b1c", "w2", "b2c"):
                io[nm] = Tile(w6[nm].h[i], w6[nm].name)
            for nm in ("Aq", "Bq", "Ak", "Bk"):
                io["rope" + nm] = l1_ropes[nm]
            build_l1_layer(pp=p, io=io)
        emit_axpy(p, em, x_d, h_d, 0.5, ydtype=F32)

    l1_ffn_phase()
    lnx2_d = p.dram("lnx2_d", [dm, TPC], BF16)
    build_ln_bf16(pp=p, io=dict(x=x_d, lnx=lnx2_d, ps=ps, sc=sc))
    hxag_d = p.dram("hxag", [4, dm, TPC], BF16)
    em.collective("AllGather", ALU.bypass, RG4, [lnx2_d.ap],
                  [TV(hxag_d.h.rearrange("s d t -> (s d) t"), hxag_d.name)])
    hx_d = p.dram("hx_d", [dm, T], BF16)
    for s in range(4):
        em.dma("sync", TV(hx_d.h[:, s * TPC:(s + 1) * TPC], hx_d.name),
               TV(hxag_d.h[s], hxag_d.name))
    o_d = p.dram("o_d", [256, T], BF16)
    build_l1_attn(pp=p, io=dict(
        hx=hx_d, wq=A["wq"], wk=A["wk"], wv=A["wv"], wb=A["wb"],
        qcw=A["qcw"], kcw=A["kcw"], vcw=A["vcw"], firs=A["firs"],
        firl=A["firl"], gw1=A["gw1"], gw1h=A["gw1h"], gw1b=A["gw1b"],
        gb1c=A["gb1c"], gw2=A["gw2"],
        gb2=A["gb2"], temp=A["temp"], epsf=A["epsf"], onwr=A["onwr"],
        trilm=A["trilm"], trium=A["trium"], triuI=A["triuI"], o_out=o_d,
        ps=ps, sc=sc))
    part_d = p.dram("part_d", [T, dm], BF16)
    mark = p.mark()
    o_sb = p.sbuf("wp_o", [128, 2, T], BF16)
    wo_sb = p.sbuf("wp_w", [128, 2, dm], BF16)
    pout = p.sbuf("wp_out", [128, 512], BF16)
    em.dma("gpsimd", o_sb.ap,
           TV(o_d.h.rearrange("(c p) t -> p c t", p=128), o_d.name))
    em.dma("gpsimd", wo_sb.ap,
           TV(a_wo_slice.h.rearrange("(c p) m -> p c m", p=128),
              a_wo_slice.name))
    for tt_ in range(T // 128):
        for n2 in range(2):
            pst = ps["P0" if n2 == 0 else "P1"]
            for c in range(2):
                em.mm(pst.ap,
                      TV(o_sb.h[:, c, tt_ * 128:(tt_ + 1) * 128], o_sb.name),
                      TV(wo_sb.h[:, c, n2 * 512:(n2 + 1) * 512], wo_sb.name),
                      start=(c == 0), stop=(c == 1))
            em.copy("vector", pout.ap, pst.ap)
            em.dma("gpsimd",
                   TV(part_d.h[tt_ * 128:(tt_ + 1) * 128,
                               n2 * 512:(n2 + 1) * 512], part_d.name),
                   pout.ap)
    p.free_since(mark)
    pmine_d = p.dram("pmine_d", [TPC, dm], BF16)
    em.collective("ReduceScatter", ALU.add, RG4, [part_d.ap], [pmine_d.ap])
    mark = p.mark()
    xa = p.sbuf("wr_xa", [128, 512], F32)
    pa = p.sbuf("wr_pa", [128, 512], BF16)
    for m in range(nch):
        for s0 in range(0, TPC, 512):
            em.dma("sync", xa.ap,
                   TV(x_d.h[m * 128:(m + 1) * 128, s0:s0 + 512], x_d.name))
            em.op("sync",
                  (lambda o, i: lambda e: e.dma_start_transpose(o, i))(
                      pa.h[:, :],
                      pmine_d.h[s0:s0 + 512, m * 128:(m + 1) * 128]),
                  reads=[TV(None, pmine_d.name)], writes=[TV(None, pa.name)],
                  dma_queue=em._qname(pa.name))
            em.tt("vector", xa.ap, xa.ap, pa.ap, ALU.add)
            em.dma("gpsimd",
                   TV(x_d.h[m * 128:(m + 1) * 128, s0:s0 + 512], x_d.name),
                   xa.ap)
    p.free_since(mark)
    l1_ffn_phase()
    em.dma("gpsimd", xo_out.ap, x_d.ap)
    em.fence("sync")
    em.fence("gpsimd")
    return p


_FUSED_SHARED = {}


def host_fused_core_inputs(core, I):
    b, t0 = core // 4, (core % 4) * TPC
    S = _FUSED_SHARED
    x_np = np.asarray(I["L0_attn_x"], dtype=np.float32)
    dm = D
    e = core
    h = core % 4
    ins = {}
    ins["x_in"] = np.ascontiguousarray(x_np[b, t0:t0 + TPC].T)
    ins["rw1"] = tile_w(I["L0_ffn_rw1"])
    ins["rb1c"] = np.ascontiguousarray(I["L0_ffn_rb1"].reshape(-1, 128).T)
    ins["rw2"] = I["L0_ffn_rw2"]
    ins["rb2"] = I["L0_ffn_rb2"].reshape(1, 8)
    ins["ew1"] = tile_w(I["L0_ffn_ew1"][e])
    ins["eb1c"] = np.ascontiguousarray(I["L0_ffn_eb1"][e].reshape(-1, 128).T)
    H_ = 4096
    ins["ew2x"] = np.ascontiguousarray(
        I["L0_ffn_ew2"][e].reshape(H_ // 128, 128, 8, 128)
        .transpose(2, 1, 0, 3).reshape(8, 128, H_))
    ins["eb2r"] = I["L0_ffn_eb2"][e].reshape(1, -1)
    esel = np.zeros((1, 8), np.float32)
    esel[0, e] = 1.0
    ins["esel"] = esel
    ins.update(host_l0f_common())
    if "sparse_cst" not in S:
        S["sparse_cst"] = {
            "cst_tokid": np.ascontiguousarray(
                np.arange(128, dtype=np.int32)[:, None] * 128
                + np.arange(128, dtype=np.int32)[None, :]),
            "cst_iota8": np.arange(8, dtype=np.float32).reshape(1, 8),
            "cst_identf": np.eye(128, dtype=np.float32),
            "cst_dumpc": (4096.0
                          + np.arange(256, dtype=np.float32)[None, :] * 128
                          + np.arange(128, dtype=np.float32)[:, None]),
        }
    ins.update(S["sparse_cst"])
    ins["cons_idx"] = np.ascontiguousarray(
        core * 16 + np.arange(16, dtype=np.int32)[:, None]
        + 128 * np.arange(4, dtype=np.int32)[None, :])
    lk = ("l0a", t0)
    if lk not in S:
        l0a = host_l0_attn_core_inputs(
            np.zeros((512 + TPC, D), np.float32), I["L0_attn_w_qkv"],
            I["L0_attn_w_out"], 512, T_half=1024)
        pos = np.maximum(np.arange(t0 - 512, t0 + TPC), 0)
        A1, B1 = host_l0_rope_tables(pos)
        l0a["ropeA"] = np.stack([A1, A1 * 0.125], 0)
        l0a["ropeB"] = np.stack([B1, B1 * 0.125], 0)
        m0 = host_window_maskT(t0, 8, 640, 512)
        m1 = host_window_maskT(t0 + 1024, 8, 640, 512)
        l0a["maskT"] = np.stack([m0, m1], 0).astype(_bf16())
        S[lk] = l0a
        S.setdefault("l0_w", (S[lk]["wqkv"], S[lk]["wout"]))
    ins["l0_wqkv"], ins["l0_wout"] = S["l0_w"]
    ins["l0_vw"] = S[lk]["vw_in"]
    ins["l0_ropeA"] = S[lk]["ropeA"]
    ins["l0_ropeB"] = S[lk]["ropeB"]
    ins["l0_maskT"] = S[lk]["maskT"]
    ins.update(host_common_inputs())
    DIN = 2730
    if "l1_w16" not in S:
        l1s = [host_l1_layer_core_inputs(
            None, I["L1_ffn_Wqkv"][i], I["L1_ffn_Wo"][i],
            I["L1_ffn_q_scale"][i], I["L1_ffn_k_scale"][i],
            I["L1_ffn_ln_g"][i], I["L1_ffn_ln_b"][i],
            I["L1_ffn_W1"][i], I["L1_ffn_b1"][i], I["L1_ffn_W2"][i],
            I["L1_ffn_b2"][i], t0, skip_h=True) for i in range(6)]
        S["l1_wqkv6"] = np.stack([s["wqkv"] for s in l1s])
        S["l1_vw6"] = np.stack([s["vw_in"] for s in l1s])
        S["l1_wo6"] = np.stack([s["wo"] for s in l1s])
        for nm in ("qscol", "qspcol", "kscol", "kspcol"):
            S["l1_" + nm + "6"] = np.stack([s[nm] for s in l1s])
        S["l1_lng6"] = np.stack([s["ln_g"] for s in l1s])
        S["l1_lnb6"] = np.stack([s["ln_b"] for s in l1s])
        S["l1_w16"] = np.stack([s["w1"] for s in l1s])
        S["l1_b1c6"] = np.stack([s["b1c"] for s in l1s])
        S["l1_w26"] = np.stack([s["w2"] for s in l1s])
        S["l1_b2c6"] = np.stack([s["b2c"] for s in l1s])
        S["cst_blk2"] = l1s[0]["cst_blk2"]
        S["cst_e2"] = l1s[0]["cst_e2"]
    for k in ("l1_wqkv6", "l1_vw6", "l1_wo6", "l1_qscol6", "l1_qspcol6", "l1_kscol6",
              "l1_kspcol6", "l1_lng6", "l1_lnb6", "l1_w16", "l1_b1c6",
              "l1_w26", "l1_b2c6", "cst_blk2", "cst_e2"):
        ins[k] = S[k]
    rk = ("l1rm", t0)
    if rk not in S:
        pos = np.arange(t0 - 256 - 128, t0 + TPC)
        Aq, Bq, Ak, Bk = host_l1_rope_tables(pos)
        QT_ALL = TPC // 128 + 1
        mT = host_window_maskT(t0 - 128, QT_ALL, 384, 257, lookback=256)
        S[rk] = dict(Aq=Aq, Bq=Bq, Ak=Ak, Bk=Bk, maskT=mT.astype(_bf16()))
    for nm in ("Aq", "Bq", "Ak", "Bk"):
        ins["l1_rope" + nm] = S[rk][nm]
    ins["l1_maskT"] = S[rk]["maskT"]
    a = host_l1_attn_core_inputs(
        np.zeros((D, 1), _bf16()), b, h, I["L1_attn_Wq"], I["L1_attn_Wk"],
        I["L1_attn_Wv"], I["L1_attn_Wb"], I["L1_attn_qc_w"],
        I["L1_attn_kc_w"], I["L1_attn_vc_w"], I["L1_attn_fir_s"],
        I["L1_attn_fir_l"], I["L1_attn_gW1"], I["L1_attn_gb1"],
        I["L1_attn_gW2"], I["L1_attn_gb2"], I["L1_attn_temp"],
        I["L1_attn_eps_floor"], I["L1_attn_onorm_w"])
    for k in ("wq", "wk", "wv", "wb", "qcw", "kcw", "vcw", "firs", "firl",
              "gw1", "gw1h", "gw1b", "gb1c", "gw2", "gb2", "temp", "epsf",
              "onwr"):
        ins["a_" + k] = a[k]
    ins["cst_trilm"] = a["cst_trilm"]
    ins["cst_trium"] = a["cst_trium"]
    ins["cst_triuI"] = a["cst_triuI"]
    ins["a_wo_slice"] = np.ascontiguousarray(
        I["L1_attn_Wo"][h * 256:(h + 1) * 256, :])
    prev = (core - 1) % 8
    pcol = np.arange(128, dtype=np.int32)[:, None]
    ins["idx_prev"] = (prev * 1024 + np.arange(8, dtype=np.int32)[None, :] * 128
                       + pcol).astype(np.int32)
    ins["halo_on"] = np.full((128, 1), 0.0 if core in (0, 4) else 1.0,
                             np.float32)
    return ins


def kernel_fused(I):
    p, nc = _get_prog("fused", build_fused)
    _FUSED_SHARED.clear()
    in_maps = [host_fused_core_inputs(c, I) for c in range(N_CORES)]
    _FUSED_SHARED.clear()
    res = run_bass_kernel_spmd(nc, in_maps, core_ids=list(range(N_CORES)))
    x_np = np.asarray(I["L0_attn_x"])
    out = np.empty((B, T, D), np.float32)
    for c, (b, t0) in enumerate(_core_ranges()):
        out[b, t0:t0 + TPC] = res.results[c]["x_out"].T
    return out





def emit_router_tm(p, em, ps, sc, xn, lnx_tm_d, TOK=TPC, nch=NCH):
    mark = p.mark()
    stg = p.sbuf("rtm_stg", [128, nch * 128], BF16)
    stg2 = p.sbuf("rtm_stg2", [128, nch * 128], BF16)
    for tt_ in range(TOK // 128):
        PS_T = ps["PT" if tt_ % 2 == 0 else "PT2"]
        sg = stg if tt_ % 2 == 0 else stg2
        for c in range(nch):
            em.transpose(TV(PS_T.h[:, c * 128:(c + 1) * 128], PS_T.name),
                         TV(xn.h[:, c, tt_ * 128:(tt_ + 1) * 128], xn.name),
                         sb_ident(sc))
        em.copy("vector", sg.ap, TV(PS_T.h[:, 0:nch * 128], PS_T.name))
        em.dma("gpsimd",
               TV(lnx_tm_d.h[tt_ * 128:(tt_ + 1) * 128, :],
                  f"{lnx_tm_d.name}.w{tt_}"),
               sg.ap, q=em._qname(lnx_tm_d.name))
    p.free_since(mark)


def emit_l0f_sparse(p, em, ps, sc, io, NT=B * T, nch=NCH, H=4096, CAP=4096):
    dm = nch * 128
    J = NT // 128
    rt = emit_routing(p, em, ps, sc, io["logits_all"], NT, CAP=CAP,
                      tril_tile=io["trils"])
    esel_s = p.sbuf("sp_esel", [1, 8], F32)
    em.dma("sync", esel_s.ap, io["esel"].ap)
    em.mm(TV(ps["P2"].h[:, 0:8], ps["P2"].name), sc["ones_row"].ap, esel_s.ap)
    eselb = p.sbuf("sp_eselb", [128, 8], F32)
    em.copy("vector", eselb.ap, TV(ps["P2"].h[:, 0:8], ps["P2"].name))
    e8 = p.sbuf("sp_e8", [1, 8], F32)
    em.dma("sync", e8.ap, io["iota8"].ap)
    em.mm(TV(ps["P3"].h[:, 0:8], ps["P3"].name), sc["ones_row"].ap, e8.ap)
    e8b = p.sbuf("sp_e8b", [128, 8], F32)
    em.copy("vector", e8b.ap, TV(ps["P3"].h[:, 0:8], ps["P3"].name))
    from concourse.ap import AP as _AP

    def _b8(t):
        a = t.h[:]
        return TV(_AP(a.tensor, a.offset,
                      [list(a.ap[0]), [0, J], list(a.ap[1])]), t.name)
    tmp3 = p.sbuf("sp_tmp3", [128, J, 8], F32)
    m1m = p.sbuf("sp_m1m", [128, J], F32)
    m2m = p.sbuf("sp_m2m", [128, J], F32)
    ev = p.sbuf("sp_ev", [128, J], F32)
    yrow1 = p.sbuf("sp_yr1", [128, J], F32)
    yrow2 = p.sbuf("sp_yr2", [128, J], F32)
    offs1 = p.sbuf("sp_of1", [128, J], F32)
    offs2 = p.sbuf("sp_of2", [128, J], F32)
    offs_i = p.sbuf("sp_ofi", [128, 2 * J], I32)
    for (mm_, k_, dst) in ((rt["m1"], rt["k1"], m1m), (rt["m2"], rt["k2"], m2m)):
        em.tt("vector", tmp3.ap, mm_.ap, _b8(eselb), ALU.mult)
        em.reduce("vector", dst.ap, tmp3.ap, ALU.add)
        em.tt("vector", dst.ap, dst.ap, k_.ap, ALU.mult)
    for (mm_, pos_, k_, dst) in ((rt["m1"], rt["pos1"], rt["k1"], yrow1),
                                 (rt["m2"], rt["pos2"], rt["k2"], yrow2)):
        em.tt("vector", tmp3.ap, mm_.ap, _b8(e8b), ALU.mult)
        em.reduce("vector", ev.ap, tmp3.ap, ALU.add)
        em.ts("vector", ev.ap, ev.ap, float(CAP), None, ALU.mult)
        em.tt("vector", ev.ap, ev.ap, pos_.ap, ALU.add)
        em.tt("vector", dst.ap, ev.ap, k_.ap, ALU.mult)
    dumpc_sb = p.sbuf("sp_dumpc", [128, 2 * J], F32)
    em.dma("sync", dumpc_sb.ap, io["dumpc"].ap)
    for (mask, pos_, dst, dc0) in ((m1m, rt["pos1"], offs1, 0),
                                   (m2m, rt["pos2"], offs2, J)):
        em.tt("vector", dst.ap, pos_.ap, mask.ap, ALU.mult)
        em.ts("vector", ev.ap, mask.ap, -1.0, 1.0, ALU.mult, ALU.add)
        em.tt("vector", ev.ap, ev.ap,
              TV(dumpc_sb.h[:, dc0:dc0 + J], dumpc_sb.name), ALU.mult)
        em.tt("vector", dst.ap, dst.ap, ev.ap, ALU.add)
    em.copy("vector", TV(offs_i.h[:, 0:J], offs_i.name), offs1.ap)
    em.copy("vector", TV(offs_i.h[:, J:2 * J], offs_i.name), offs2.ap)
    if io.get("dbg") is not None:
        em.dma("sync", io["dbg"].ap, offs_i.ap)
    cons_d = p.dram("cons_d", [4, 128, J])
    em.dma("gpsimd", TV(cons_d.h[0], cons_d.name), rt["w1b"].ap)
    em.dma("gpsimd", TV(cons_d.h[1], cons_d.name), rt["w2b"].ap)
    em.dma("gpsimd", TV(cons_d.h[2], cons_d.name), yrow1.ap)
    em.dma("gpsimd", TV(cons_d.h[3], cons_d.name), yrow2.ap)
    glist_d = p.dram("glist_d", [CAP + 2 * J * 128, 1], I32)
    zi = p.sbuf("sp_zi", [128, CAP // 128], I32)
    em.memset("vector", zi.ap, 0)
    em.dma("gpsimd",
           TV(glist_d.h[0:CAP, :].rearrange("(pp k) o -> pp (k o)", pp=128),
              glist_d.name), zi.ap)
    tokid = p.sbuf("sp_tokid", [128, J], I32)
    em.dma("sync", tokid.ap, io["tokid"].ap)
    gq = em._qname(glist_d.name)
    em.fence("gpsimd", streams=[gq])
    for j in range(J):
        for off0 in (0, J):
            em.indirect(TV(glist_d.h[:, :], f"{glist_d.name}.s{off0 + j}"),
                        TV(offs_i.h[:, off0 + j:off0 + j + 1], offs_i.name),
                        TV(tokid.h[:, j:j + 1], tokid.name), None, q=gq)
    p.free(*(rt["tiles"] + [esel_s, eselb, e8, e8b, tmp3, m1m, m2m, ev,
                            yrow1, yrow2, offs1, offs2, offs_i, dumpc_sb,
                            zi, tokid]))
    p._dbg_glist = glist_d
    mark2 = p.mark()
    xg_fm = p.sbuf("sp_xgfm", [128, nch, CAP], BF16)
    gl_sb = p.sbuf("sp_gl", [128, CAP // 128], I32)
    em.fence("sync", streams=[em._qname(glist_d.name)])
    em.dma("sync", gl_sb.ap,
           TV(glist_d.h[0:CAP, :].rearrange("(k pp) o -> pp (k o)", pp=128),
              glist_d.name), allow_slow_non_contiguous=True)
    growA = p.sbuf("sp_growA", [128, dm], BF16)
    growB = p.sbuf("sp_growB", [128, dm], BF16)
    for k in range(CAP // 128):
        g = growA if k % 2 == 0 else growB
        PS_T = ps["PT" if k % 2 == 0 else "PT2"]
        em.indirect(g.ap, None, io["lnx_tm_ag"].ap,
                    TV(gl_sb.h[:, k:k + 1], gl_sb.name))
        for c in range(nch):
            em.transpose(TV(PS_T.h[:, c * 128:(c + 1) * 128], PS_T.name),
                         TV(g.h[:, c * 128:(c + 1) * 128], g.name),
                         sb_ident(sc))
        em.copy("vector",
                TV(xg_fm.h[:, :, k * 128:(k + 1) * 128], xg_fm.name),
                TV(PS_T.h[:, 0:nch * 128].rearrange("p (c t) -> p c t",
                                                    c=nch),
                   PS_T.name))
    o512 = p.sbuf("sp_o512", [1, 512], BF16)
    em.dma("gpsimd", o512.ap, io["ones512"].ap)
    eb1col = p.sbuf("sp_eb1col", [128, H // 128], F32)
    em.dma("sync", eb1col.ap, io["eb1c"].ap)
    wbuf = p.sbuf("sp_wbuf", [128, 4, nch, 128], BF16)
    w2buf = p.sbuf("sp_w2buf", [128, 4, nch, 128], BF16)
    h1 = p.sbuf("sp_h1", [128, H // 128, 1024], BF16)
    b2row = p.sbuf("sp_b2row", [1, dm], BF16)
    em.dma("gpsimd", b2row.ap, io["eb2r"].ap)
    ybfA = p.sbuf("sp_ybfA", [128, 512], BF16)
    ybfB = p.sbuf("sp_ybfB", [128, 512], BF16)
    ystA = p.sbuf("sp_ystA", [128, 4, 128], BF16)
    ystB = p.sbuf("sp_ystB", [128, 4, 128], BF16)
    y_tm = io["y_tm"]
    NM = H // 128
    ew1, ew2x = io["ew1"], io["ew2x"]
    for tb in range(CAP // 1024):
        tb0 = tb * 1024
        for m in range(NM):
            wb = TV(wbuf.h[:, m % 4].rearrange("p c m -> p (c m)"),
                    f"sp_wbuf.{m % 4}")
            em.dma("gpsimd", wb, TV(ew1.h[m], ew1.name))
            for s0 in range(0, 1024, 512):
                pst = ps["P0" if (s0 // 512) % 2 == 0 else "P1"]
                for c in range(nch):
                    em.mm(pst.ap,
                          TV(wbuf.h[:, m % 4, c, :], f"sp_wbuf.{m % 4}"),
                          TV(xg_fm.h[:, c, tb0 + s0:tb0 + s0 + 512],
                             xg_fm.name),
                          start=(c == 0), stop=(c == nch - 1))
                em.act(TV(h1.h[:, m, s0:s0 + 512], h1.name), pst.ap, AF.Relu,
                       bias=TV(eb1col.h[:, m:m + 1], eb1col.name))
        for m2 in range(nch):
            for cc0 in range(0, NM, nch):
                slot = (cc0 // nch) % 4
                w2b_ = TV(w2buf.h[:, slot].rearrange("p c m -> p (c m)"),
                          f"sp_w2buf.{slot}")
                em.dma("gpsimd", w2b_,
                       TV(ew2x.h[m2][:, cc0 * 128:(cc0 + nch) * 128],
                          ew2x.name))
                for s0 in range(0, 1024, 512):
                    pst = ps["P2" if s0 == 0 else "P3"]
                    for c in range(nch):
                        em.mm(pst.ap,
                              TV(w2buf.h[:, slot, c, :], f"sp_w2buf.{slot}"),
                              TV(h1.h[:, cc0 + c, s0:s0 + 512], h1.name),
                              start=(cc0 + c == 0), stop=False)
            for s0 in range(0, 1024, 512):
                pst = ps["P2" if s0 == 0 else "P3"]
                em.mm(pst.ap,
                      TV(b2row.h[:, m2 * 128:(m2 + 1) * 128], b2row.name),
                      o512.ap, start=False, stop=True)
                ybf = ybfA if s0 == 0 else ybfB
                yst = ystA if s0 == 0 else ystB
                PS_T = ps["PT" if s0 == 0 else "PT2"]
                em.copy("vector", ybf.ap, pst.ap)
                for j in range(4):
                    em.transpose(TV(PS_T.h[:, j * 128:(j + 1) * 128],
                                    PS_T.name),
                                 TV(ybf.h[:, j * 128:(j + 1) * 128], ybf.name),
                                 sb_ident(sc))
                em.copy("scalar",
                        TV(yst.h[:, :, :], yst.name),
                        TV(PS_T.h[:, 0:512].rearrange("p (j t) -> p j t", j=4),
                           PS_T.name))
                r0 = tb0 + s0
                em.dma("gpsimd",
                       TV(y_tm.h[r0:r0 + 512, m2 * 128:(m2 + 1) * 128]
                          .rearrange("(j pp) t -> pp j t", pp=128),
                          f"{y_tm.name}.w{r0}_{m2}"),
                       yst.ap, q=em._qname(y_tm.name))
    em.fence("sync")
    p.free_since(mark2)
    return cons_d


def emit_l0f_consume(p, em, ps, sc, io, nch=NCH, CAP=4096):
    dm = nch * 128
    cons_d, y_ag, x_d = io["cons_d"], io["y_tm_ag"], io["x"]
    idf = io["idf_sb"]
    mark = p.mark()
    crows = p.sbuf("cn_crows", [16, 128], F32)
    cc = [p.sbuf(f"cn_cc{k}", [128, 16], F32) for k in range(2)]
    yri = [p.sbuf(f"cn_yri{k}", [128, 16], I32) for k in range(2)]
    ccf = p.sbuf("cn_ccf", [128, 16], F32)
    cons_flat = TV(cons_d.h.rearrange("s p j -> (s p) j"), cons_d.name)
    ci = io["cons_idx"]
    for k in range(4):
        em.indirect(crows.ap, None, cons_flat,
                    TV(ci.h[:, k:k + 1], ci.name))
        em.transpose(TV(ps["P2"].h[:, 0:16], ps["P2"].name), crows.ap,
                     TV(idf.h[0:16, 0:16], idf.name))
        dst = cc[k] if k < 2 else ccf
        em.copy("vector", dst.ap, TV(ps["P2"].h[:, 0:16], ps["P2"].name))
        if k >= 2:
            em.copy("vector", yri[k - 2].ap, ccf.ap)
    y1g = [p.sbuf(f"cn_y1g{i}", [128, dm], BF16) for i in range(2)]
    y2g = [p.sbuf(f"cn_y2g{i}", [128, dm], BF16) for i in range(2)]
    ycomb = [p.sbuf(f"cn_ycomb{i}", [128, dm], F32) for i in range(2)]
    for i in range(2):
        em.memset("vector", y1g[i].ap, 0.0)
        em.memset("vector", y2g[i].ap, 0.0)
    ymine_tm = io["ymine_tm"]
    for tt_ in range(TPC // 128):
        a = tt_ % 2
        em.indirect(y1g[a].ap, None, y_ag.ap,
                    TV(yri[0].h[:, tt_:tt_ + 1], yri[0].name))
        em.indirect(y2g[a].ap, None, y_ag.ap,
                    TV(yri[1].h[:, tt_:tt_ + 1], yri[1].name))
        em.ts("vector", ycomb[a].ap, y1g[a].ap,
              TV(cc[0].h[:, tt_:tt_ + 1], cc[0].name), None, ALU.mult)
        em.stt("vector", ycomb[a].ap, y2g[a].ap,
               TV(cc[1].h[:, tt_:tt_ + 1], cc[1].name), ycomb[a].ap,
               ALU.mult, ALU.add)
        em.dma("gpsimd",
               TV(ymine_tm.h[tt_ * 128:(tt_ + 1) * 128, :],
                  f"{ymine_tm.name}.w{tt_}"),
               ycomb[a].ap, q=em._qname(ymine_tm.name))
    em.fence("sync", streams=[em._qname(ymine_tm.name)])
    xa = p.sbuf("cn_xa", [128, 512], F32)
    pa = p.sbuf("cn_pa", [128, 512], BF16)
    for m in range(nch):
        for s0 in range(0, TPC, 512):
            em.dma("sync", xa.ap,
                   TV(x_d.h[m * 128:(m + 1) * 128, s0:s0 + 512], x_d.name))
            em.op("sync",
                  (lambda o, i: lambda e: e.dma_start_transpose(o, i))(
                      pa.h[:, :],
                      ymine_tm.h[s0:s0 + 512, m * 128:(m + 1) * 128]),
                  reads=[TV(None, ymine_tm.name)], writes=[TV(None, pa.name)],
                  dma_queue=em._qname(pa.name))
            em.stt("vector", xa.ap, pa.ap, 0.5, xa.ap, ALU.mult, ALU.add)
            em.dma("scalar",
                   TV(x_d.h[m * 128:(m + 1) * 128, s0:s0 + 512], x_d.name),
                   xa.ap)
    em.fence("sync")
    p.free_since(mark)
